# version 02
# baseline (speedup 1.0000x reference)
import numpy as np
import ml_dtypes
from contextlib import ExitStack

import concourse.bass as bass
import concourse.mybir as mybir
from concourse.bass_utils import run_bass_kernel_spmd

F32 = mybir.dt.float32
BF16 = mybir.dt.bfloat16
I16 = mybir.dt.int16
I32 = mybir.dt.int32
I8 = mybir.dt.int8
AF = mybir.ActivationFunctionType
OP = mybir.AluOpType
AX = mybir.AxisListType.X

D = 1024
DT = D // 128
HEADS = 16
HD = 64
CHUNK = 128
HID = 4096
FE, FS, FH = 8, 2, 2048
FHT = FH // 128
EPS_RMS = 1e-6
EPS_LN = 1e-5
SCALE_ATT = HD ** -0.5
AB = 1024
NPT = AB // 128


class Emit:
    def __init__(self, nc):
        self.nc = nc
        self.sems = {}
        self.count = {}

    def sem(self, name):
        if name not in self.sems:
            self.sems[name] = self.nc.alloc_semaphore(name)
            self.count[name] = 0
        return self.sems[name]

    def inc(self, instr, name, n=1):
        instr.then_inc(self.sem(name), n)
        self.count[name] += n
        return (name, self.count[name])

    def wait(self, eng, tok):
        if tok is None:
            return
        name, val = tok
        if val > 0:
            eng.wait_ge(self.sem(name), val)

    def now(self, name):
        return (name, self.count.get(name, 0))


class Chained:

    def __init__(self, em, eng, name, buddies=()):
        object.__setattr__(self, "em", em)
        object.__setattr__(self, "eng", eng)
        object.__setattr__(self, "name", name)
        object.__setattr__(self, "buddies", buddies)

    def __getattr__(self, attr):
        f = getattr(self.eng, attr)

        def call(*a, **k):
            em, eng, name = self.em, self.eng, self.name
            if em.count.get(name, 0) > 0:
                eng.wait_ge(em.sem(name), em.count[name])
            for b in self.buddies:
                if em.count.get(b, 0) > 0:
                    eng.wait_ge(em.sem(b), em.count[b])
            return em.inc(f(*a, **k), name, 1)

        return call

    def wait(self, tok):
        self.em.wait(self.eng, tok)

    def tok(self):
        return self.em.now(self.name)


B_STAGE = 99


def build_program(NTOK, LAYERS, OWN, phases="ABCF"):
    NBLK = NTOK // AB
    NOC = OWN // CHUNK
    nc = bass.Bass("TRN2", target_bir_lowering=False, debug=False)
    nc.num_devices = 8

    dr = lambda n, s, dt, k="ExternalInput": nc.dram_tensor(n, s, dt, kind=k).ap()
    x0 = dr("x0", [NTOK, D], F32)
    wqkvT = dr("wqkvT", [LAYERS, D, 3 * D], F32)
    wgateT = dr("wgateT", [LAYERS, D, D], F32)
    wprojT = dr("wprojT", [LAYERS, D, D], F32)
    wfc1T = dr("wfc1T", [LAYERS, D, 2 * HID], F32)
    wfc2T = dr("wfc2T", [LAYERS, HID, D], F32)
    bqkv = dr("bqkv", [LAYERS, 3 * D], F32)
    bgate = dr("bgate", [LAYERS, D], F32)
    bproj = dr("bproj", [LAYERS, D], F32)
    bfc1 = dr("bfc1", [LAYERS, 2 * HID], F32)
    bfc2 = dr("bfc2", [LAYERS, D], F32)
    norm1w = dr("norm1w", [LAYERS, D], F32)
    norm2w = dr("norm2w", [LAYERS, D], F32)
    anormwT = dr("anormwT", [LAYERS, 64, 16], F32)
    crep = dr("crep", [128, NTOK], F32)
    srep = dr("srep", [128, NTOK], F32)
    identb = dr("identb", [128, 128], BF16)
    identf = dr("identf", [128, 128], F32)
    pmatb = dr("pmatb", [128, 128], BF16)
    fgw = dr("fgw", [D, FE], F32)
    fgb = dr("fgb", [1, FE], F32)
    frw1 = dr("frw1", [FE, D, FH], F32)
    frb1 = dr("frb1", [FE, FH], F32)
    frw2 = dr("frw2", [FE, D, FH], F32)
    frb2 = dr("frb2", [FE, FH], F32)
    frw3 = dr("frw3", [FE, FH, D], F32)
    frb3 = dr("frb3", [FE, D], F32)
    fsw1 = dr("fsw1", [FS, D, FH], F32)
    fsb1 = dr("fsb1", [FS, FH], F32)
    fsw2 = dr("fsw2", [FS, D, FH], F32)
    fsb2 = dr("fsb2", [FS, FH], F32)
    fsw3 = dr("fsw3", [FS, FH, D], F32)
    fsb3 = dr("fsb3", [FS, D], F32)
    ownidx = dr("ownidx", [128, NOC], I32)
    out = dr("out", [OWN, D], F32, "ExternalOutput")

    x_st = dr("x_st", [NTOK, D], F32, "Internal")
    xn_st = dr("xn_st", [NTOK, D], BF16, "Internal")
    acc_st = dr("acc_st", [OWN, D], F32, "Internal")
    swq_st = dr("swq_st", [D, 3 * D], BF16, "Internal")
    swg_st = dr("swg_st", [D, D], BF16, "Internal")
    swp_st = dr("swp_st", [D, D], BF16, "Internal")
    sw1_st = dr("sw1_st", [D, 2 * HID], BF16, "Internal")
    sw2_st = dr("sw2_st", [HID, D], BF16, "Internal")
    row_st = dr("row_st", [1, 2048], F32, "Internal")
    qtd = dr("qtd", [DT, 128, 128], BF16, "Internal")
    ktd = dr("ktd", [DT, 128, 128], BF16, "Internal")
    vtd = dr("vtd", [DT, 128, 128], BF16, "Internal")

    em = Emit(nc)
    ctx = ExitStack()
    sb = lambda name, shape, dt: ctx.enter_context(nc.sbuf_tensor(name, shape, dt))
    _uid = [0]

    def pst(nm, shape, dt):
        _uid[0] += 1
        return nc.psum_tensor(f"{nm}_{_uid[0]}", shape, dt)

    def fence():
        for nm in ("pe", "v", "s", "g"):
            if em.count.get(nm, 0) > 0:
                nc.sync.wait_ge(em.sem(nm), em.count[nm])

    def gfence():
        for nm in ("pe", "v", "s"):
            if em.count.get(nm, 0) > 0:
                nc.gpsimd.wait_ge(em.sem(nm), em.count[nm])

    IDENT = sb("IDENT", [128, 128], BF16)
    IDENTF = sb("IDENTF", [128, 128], F32)
    PMAT = sb("PMAT", [128, 128], BF16)
    ONECF = sb("ONECF", [128, 1], F32)
    ONE1F = sb("ONE1F", [1, 1], F32)
    ONESB = sb("ONESB", [128, 1], BF16)
    NWB = sb("NWB", [128, 2 * D], F32)
    NWAT = sb("NWAT", [64, 16], F32)
    BIASF = sb("BIASF", [128, 2 * D], F32)
    BCOL = sb("BCOL", [128, 96], F32)
    BCF = sb("BCF", [128, 48], F32)
    WSC = sb("WSC", [128, 8], F32)
    KV = sb("KV", [128, 8 * 128], F32)
    KVB = sb("KVB", [128, 8 * 128], BF16)
    DECAY = sb("DECAY", [128, 8 * NPT], F32)
    ST = sb("ST", [128, 176], F32)
    ROWS = sb("ROWS", [1, 2048], F32)
    XSB = sb("XSB", [128, AB], F32)
    SM = sb("SM", [128, 2 * D], F32)
    SM2 = sb("SM2", [128, 2 * D], F32)
    XI16 = sb("XI16", [128, D], I16)
    XQN = sb("XQN", [128, D], BF16)
    KNAT = sb("KNAT", [128, D], BF16)
    VNC = sb("VNC", [128, D], BF16)
    XOLD = sb("XOLD", [128, D], F32)
    XNEW = sb("XNEW", [128, D], F32)
    IXC = sb("IXC", [128, NOC], I32)
    WEB = sb("WEB", [128, 128], F32)
    BIGB = sb("BIGB", [128, 32768], BF16)
    WAR = sb("WAR", [128, 16384], BF16)
    WF32 = sb("WF32", [128, D], F32)
    VL4B = sb("VL4B", [128, 8192], BF16)
    UPC = sb("UPC", [128, 1024], BF16)
    XQED = sb("XQED", [128, 2048], BF16)

    XQT = BIGB[:, 0:8192]
    QR = BIGB[:, 8192:16384]
    KR = BIGB[:, 16384:24576]
    VTR = BIGB[:, 24576:32768]
    XQ2T = WAR[:, 8192:16384]
    BIGF = BIGB.bitcast(F32)
    WARF = WAR.bitcast(F32)
    CREP = WARF[:, 4096:5120]
    SREP = WARF[:, 5120:6144]

    SP = nc.sync

    class _PE:
        def __init__(self):
            self.last = None

        def matmul(self, *a, **k):
            self.last = nc.tensor.matmul(*a, **k)
            return self.last

        def transpose(self, *a, **k):
            self.last = nc.tensor.transpose(*a, **k)
            return self.last

        def wait_ge(self, sem, val):
            return nc.tensor.wait_ge(sem, val)

        def mark(self):
            return em.inc(self.last, "pe")

    PE = _PE()
    V = Chained(em, nc.vector, "v", buddies=("s",))
    S = Chained(em, nc.scalar, "s", buddies=("v",))
    GC = Chained(em, nc.gpsimd, "g")

    class Cols:
        def __init__(self):
            self.n = 0

        def take(self, k):
            a = self.n
            self.n += k
            assert self.n <= 176, self.n
            return a

    CL = Cols()
    c_sum = CL.take(NPT)
    c_msq = CL.take(NPT)
    c_rstd = CL.take(NPT)
    c_m1 = CL.take(NPT)
    c_xs = CL.take(NPT)
    c_qsc = CL.take(NPT)
    c_x127 = CL.take(NPT)
    c_t0 = CL.take(NPT)
    c_t1 = CL.take(NPT)
    c_xsa = CL.take(NPT)
    c_amax = CL.take(4)
    c_abias = CL.take(4)
    c_den = CL.take(4)
    c_rden = CL.take(4)
    c_wacc = CL.take(1)
    c_wtmp = CL.take(1)
    c_ln = CL.take(5 * NPT)

    def stc(c0, k=1):
        return ST[:, c0:c0 + k]

    def bcast_row(row_ap, n, dst_ap, ps_bc):
        for j in range(0, n, 512):
            w = min(512, n - j)
            PE.matmul(ps_bc[:, j % 512:j % 512 + w] if False else ps_bc[:, 0:w],
                      ONECF[0:1, 0:1].broadcast_to([1, 128]),
                      row_ap[0:1, j:j + w], start=True, stop=True)
            tpe = PE.mark()
            S.wait(tpe)
            S.copy(dst_ap[:, j:j + w], ps_bc[:, 0:w])
            ts = S.tok()
            em.wait(PE, ts)
        return S.tok()

    def col_from_rows(row_ap, start, ncols, dst_cols, ps_c):
        for j in range(ncols):
            PE.matmul(ps_c[:, j:j + 1],
                      row_ap[0:1, start + j * 128:start + (j + 1) * 128],
                      ONE1F[0:1, 0:1], start=True, stop=True)
        tpe = PE.mark()
        V.wait(tpe)
        V.tensor_copy(dst_cols, ps_c[:, 0:ncols])
        tv = V.tok()
        em.wait(PE, tv)
        return tv

    def cols_to_row(col_block, ncols, dst_row):
        em.wait(SP, em.now("v"))
        for nm in ("dob", "drw"):
            if em.count.get(nm, 0):
                SP.wait_ge(em.sem(nm), em.count[nm])
        t1 = em.inc(SP.dma_start(
            out=row_st.rearrange("a b -> (a b)")[0:128 * ncols].rearrange(
                "(c p) -> p c", p=128),
            in_=col_block, allow_slow_non_contiguous=True), "dob", 16)
        em.wait(SP, t1)
        t2 = em.inc(SP.dma_start(out=dst_row, in_=row_st[0:1, 0:128 * ncols]), "dob", 16)
        return t2

    def row_to_cols(src_row, ncols, dst_cols):
        em.wait(SP, em.now("v"))
        for nm in ("dob", "drw"):
            if em.count.get(nm, 0):
                SP.wait_ge(em.sem(nm), em.count[nm])
        t1 = em.inc(SP.dma_start(out=row_st[0:1, 0:128 * ncols], in_=src_row), "dob", 16)
        em.wait(SP, t1)
        t2 = em.inc(SP.dma_start(
            out=dst_cols,
            in_=row_st.rearrange("a b -> (a b)")[0:128 * ncols].rearrange(
                "(c p) -> p c", p=128), allow_slow_non_contiguous=True), "dob", 16)
        return t2

    def rmsnorm_quant_chunk(xin_f32, nwb_ap, ci, out_bf):
        xw = SM2[:, 0:D]
        V.tensor_tensor(xw, xin_f32, nwb_ap, OP.mult)
        tv = V.tok()
        S.wait(tv)
        S.square(SM2[:, D:2 * D], xin_f32)
        tsq = S.tok()
        V.wait(tsq)
        V.reduce_sum(stc(c_sum + ci), SM2[:, D:2 * D], axis=AX)
        V.reduce_max(stc(c_m1 + ci), xw, axis=AX, apply_absolute_value=True)
        V.tensor_scalar(stc(c_t0 + ci), stc(c_sum + ci), 1.0 / D, EPS_RMS, OP.mult, OP.add)
        V.reciprocal(stc(c_t1 + ci), stc(c_t0 + ci))
        tv = V.tok()
        S.wait(tv)
        S.sqrt(stc(c_rstd + ci), stc(c_t1 + ci))
        ts = S.tok()
        V.wait(ts)
        V.tensor_tensor(stc(c_xs + ci), stc(c_rstd + ci), stc(c_m1 + ci), OP.mult)
        V.tensor_scalar(stc(c_xs + ci), stc(c_xs + ci), 1e-5, None, OP.max)
        V.reciprocal(stc(c_t0 + ci), stc(c_xs + ci))
        V.tensor_tensor(stc(c_t1 + ci), stc(c_t0 + ci), stc(c_rstd + ci), OP.mult)
        V.tensor_scalar(stc(c_qsc + ci), stc(c_t1 + ci), 127.0, None, OP.mult)
        V.tensor_scalar(stc(c_x127 + ci), stc(c_xs + ci), 1.0 / 127.0, None, OP.mult)
        tv = V.tok()
        S.wait(tv)
        S.activation(XI16[:, 0:D], xw, AF.Copy, scale=stc(c_qsc + ci))
        ts = S.tok()
        V.wait(ts)
        V.wait(em.now("pe"))
        V.tensor_copy(out_bf, XI16[:, 0:D])
        return V.tok()

    def transpose_chunk(src_bf, ps_tr, col_map):
        tprev = em.now("s")
        em.wait(PE, tprev)
        em.wait(PE, em.now("v"))
        for f in range(DT):
            PE.transpose(ps_tr[:, f * 128:(f + 1) * 128],
                         src_bf[:, f * 128:(f + 1) * 128], IDENT[:])
        tpe = PE.mark()
        S.wait(tpe)
        for f in range(DT):
            S.copy(col_map(f), ps_tr[:, f * 128:(f + 1) * 128])
        return S.tok()

    tc = None
    for dst, src in ((IDENT[:], identb), (IDENTF[:], identf), (PMAT[:], pmatb), (IXC[:], ownidx)):
        tc = em.inc(SP.dma_start(out=dst, in_=src), "dcst", 16)
    V.wait(tc)
    V.memset(ONECF[:], 1.0)
    V.memset(ONE1F[:], 1.0)
    V.memset(ONESB[:], 1.0)

    for blk in range(NBLK):
        fence()
        ld = []
        for c in range(NPT):
            t0 = blk * AB + c * 128
            ld.append(em.inc(SP.dma_start(out=BIGF[:, c * D:(c + 1) * D],
                                          in_=x0[t0:t0 + 128, :]), "dpre", 16))
        for c in range(NPT):
            xc = BIGF[:, c * D:(c + 1) * D]
            V.wait(ld[-1])
            V.reduce_sum(stc(c_ln + c), xc, axis=AX)
            tv = V.tok()
            S.wait(tv)
            S.square(SM2[:, 0:D], xc)
            ts = S.tok()
            V.wait(ts)
            V.reduce_sum(stc(c_ln + NPT + c), SM2[:, 0:D], axis=AX)
        cM = stc(c_ln + 2 * NPT, NPT)
        cV_ = stc(c_ln + 3 * NPT, NPT)
        cR = stc(c_ln + 4 * NPT, NPT)
        V.tensor_scalar(cM, stc(c_ln, NPT), 1.0 / D, None, OP.mult)
        V.tensor_tensor(cV_, cM, cM, OP.mult)
        V.tensor_scalar(stc(c_ln + NPT, NPT), stc(c_ln + NPT, NPT), 1.0 / D, None, OP.mult)
        V.tensor_tensor(cV_, stc(c_ln + NPT, NPT), cV_, OP.subtract)
        V.tensor_scalar(cV_, cV_, EPS_LN, None, OP.add)
        V.reciprocal(cV_, cV_)
        tv = V.tok()
        S.wait(tv)
        S.sqrt(cR, cV_)
        ts = S.tok()
        V.wait(ts)
        V.tensor_tensor(cM, cM, cR, OP.mult)
        V.tensor_scalar(cM, cM, -1.0, None, OP.mult)
        for c in range(NPT):
            t0 = blk * AB + c * 128
            xc = BIGF[:, c * D:(c + 1) * D]
            V.tensor_scalar(SM2[:, 0:D], xc, cR[:, c:c + 1], cM[:, c:c + 1],
                            OP.mult, OP.add)
            ta = V.tok()
            em.wait(SP, ta)
            ts1 = em.inc(SP.dma_start(out=x_st[t0:t0 + 128, :], in_=SM2[:, 0:D]), "dxs", 16)
            V.tensor_copy(XQN[:, 0:D], SM2[:, 0:D])
            tb = V.tok()
            em.wait(SP, tb)
            ts2 = em.inc(SP.dma_start(out=xn_st[t0:t0 + 128, :], in_=XQN[:, 0:D]), "dxn", 16)
            V.wait(ts1)
            V.wait(ts2)

    def prep_weights(l):
        specs = [(wqkvT[l], swq_st, D, 3 * D, 0), (wgateT[l], swg_st, D, D, 1),
                 (wprojT[l], swp_st, D, D, 2), (wfc1T[l], sw1_st, D, 2 * HID, 3),
                 (wfc2T[l], sw2_st, HID, D, 4)]
        with pst("pw_ps", [128, 8], F32) as pw_ps:
            prev_store = None
            for wap, st_ap, krows, ncols, k in specs:
                V.memset(stc(c_wacc), 0.0)
                for kt in range(krows // 128):
                    for j0 in range(0, ncols, D):
                        w = min(D, ncols - j0)
                        tg = em.now("v")
                        ts0 = em.now("s")
                        em.wait(SP, tg)
                        em.wait(SP, ts0)
                        tl = em.inc(SP.dma_start(
                            out=WF32[:, 0:w],
                            in_=wap[kt * 128:(kt + 1) * 128, j0:j0 + w]), "dwf", 16)
                        S.wait(tl)
                        S.wait(prev_store)
                        S.sign(XQN[:, 0:w], WF32[:, 0:w])
                        tsg = S.tok()
                        V.wait(tl)
                        V.reduce_sum(stc(c_wtmp), WF32[:, 0:w], axis=AX,
                                      apply_absolute_value=True)
                        V.tensor_tensor(stc(c_wacc), stc(c_wacc), stc(c_wtmp), OP.add)
                        em.wait(SP, tsg)
                        prev_store = em.inc(SP.dma_start(
                            out=st_ap[kt * 128:(kt + 1) * 128, j0:j0 + w],
                            in_=XQN[:, 0:w]), "dsw", 16)
                tg = V.tok()
                em.wait(PE, tg)
                PE.matmul(pw_ps[0:1, 0:1], ONECF[:], stc(c_wacc), start=True, stop=True)
                tp = PE.mark()
                V.wait(tp)
                V.tensor_scalar(ROWS[0:1, 0:1], pw_ps[0:1, 0:1],
                                1.0 / (krows * ncols), 1e-5, OP.mult, OP.max)
                tv = V.tok()
                em.wait(PE, tv)
                PE.matmul(pw_ps[:, k:k + 1], ONE1F[0:1, 0:1].broadcast_to([1, 128]),
                          ROWS[0:1, 0:1], start=True, stop=True)
                tp = PE.mark()
                V.wait(tp)
                V.tensor_copy(WSC[:, k:k + 1], pw_ps[:, k:k + 1])
                tv = V.tok()
                em.wait(PE, tv)
        SP.wait_ge(em.sem("dsw"), em.count["dsw"])

    def layer_consts(l):
        with pst("lc_bc", [128, 512], F32) as lc_bc, \
             pst("lc_c", [128, 96], F32) as lc_c:
            r1 = lambda ap: ap.rearrange("(a b) -> a b", a=1)

            def ld_row(src, n):
                tp = em.now("pe")
                em.wait(SP, tp)
                for nm in ("dob", "drw"):
                    if em.count.get(nm, 0):
                        SP.wait_ge(em.sem(nm), em.count[nm])
                tl = em.inc(SP.dma_start(out=ROWS[0:1, 0:n], in_=src), "drw", 16)
                em.wait(PE, tl)

            for idx, row_ap in enumerate((norm1w[l], norm2w[l])):
                ld_row(r1(row_ap), D)
                bcast_row(ROWS, D, NWB[:, idx * D:(idx + 1) * D], lc_bc)
            for idx, row_ap in enumerate((r1(bproj[l]), r1(bfc2[l]))):
                ld_row(row_ap, D)
                bcast_row(ROWS, D, BIASF[:, idx * D:(idx + 1) * D], lc_bc)
            tg0 = em.now("v")
            em.wait(SP, tg0)
            tnw = em.inc(SP.dma_start(out=NWAT[:], in_=anormwT[l]), "dnw", 16)
            V.wait(tnw)
            ld_row(r1(bqkv[l])[0:1, 0:2 * D], 2 * D)
            col_from_rows(ROWS, 0, 16, BCOL[:, 0:16], lc_c)
            ld_row(r1(bqkv[l])[0:1, 2 * D:3 * D], D)
            col_from_rows(ROWS, 0, 8, BCOL[:, 16:24], lc_c)
            ld_row(r1(bgate[l]), D)
            col_from_rows(ROWS, 0, 8, BCOL[:, 24:32], lc_c)
            for piece in range(4):
                ld_row(r1(bfc1[l])[0:1, piece * 2048:(piece + 1) * 2048], 2048)
                col_from_rows(ROWS, 0, 16, BCOL[:, 32 + piece * 16:48 + piece * 16], lc_c)

    def stream_panel(dst_view, src_st, kt_rows, col0, ncols, kw=512, sem="dwA"):
        tld = None
        for kt in range(kt_rows):
            tld = em.inc(SP.dma_start(
                out=dst_view[:, kt * kw:kt * kw + ncols],
                in_=src_st[kt * 128:(kt + 1) * 128, col0:col0 + ncols]), sem, 16)
        return tld

    def phase_A(l, blk):
        base = blk * AB
        with pst("pa_mm", [128, 512], F32) as pa_mm, \
             pst("pa_mm2", [128, 512], F32) as pa_mm2, \
             pst("pa_tr", [128, 1024], BF16) as pa_tr, \
             pst("pa_bc", [128, 512], F32) as pa_bc:
            fence()
            SP.wait_ge(em.sem("dxs"), em.count["dxs"])
            trc = em.inc(SP.dma_start(out=CREP[:], in_=crep[:, base:base + AB]), "dcr", 16)
            trs = em.inc(SP.dma_start(out=SREP[:], in_=srep[:, base:base + AB]), "dcr", 16)
            for c in range(NPT):
                t0 = base + c * 128
                tv0 = em.now("v")
                em.wait(SP, tv0)
                tl = em.inc(SP.dma_start(out=XOLD[:, 0:D], in_=x_st[t0:t0 + 128, :]),
                            "dxo", 16)
                V.wait(tl)
                rmsnorm_quant_chunk(XOLD[:, 0:D], NWB[:, 0:D], c, XQN[:, 0:D])
                transpose_chunk(XQN[:, 0:D], pa_tr,
                                lambda f, c=c: XQT[:, f * AB + c * 128:f * AB + (c + 1) * 128])
            trow = cols_to_row(stc(c_x127, NPT), NPT, ROWS[0:1, 0:AB])
            em.wait(PE, trow)
            bcast_row(ROWS, AB, XSB, pa_bc)
            txsb = em.now("s")
            for pan in range(4):
                woff = (pan % 2) * 4096
                tpe_guard = em.now("pe")
                em.wait(SP, tpe_guard)
                tld = stream_panel(WAR[:, woff:woff + 4096], swq_st, DT, pan * 512, 512,
                                   sem="dwA" if pan % 2 == 0 else "dwB")
                em.wait(PE, tld)
                for fb in range(4):
                    fglob = pan * 4 + fb
                    dstb = QR if fglob < 8 else KR
                    floc = fglob % 8
                    for t2 in range(2):
                        dst = pa_mm if t2 == 0 else pa_mm2
                        tev = em.now("s")
                        em.wait(PE, tev)
                        for kt in range(DT):
                            PE.matmul(dst[:, 0:512],
                                      WAR[:, woff + kt * 512 + fb * 128:woff + kt * 512 + (fb + 1) * 128],
                                      XQT[:, kt * AB + t2 * 512:kt * AB + (t2 + 1) * 512],
                                      start=(kt == 0), stop=(kt == DT - 1))
                        tpe = PE.mark()
                        V.wait(tpe)
                        V.wait(txsb)
                        V.scalar_tensor_tensor(SM[:, 0:512], dst[:, 0:512], WSC[:, 0:1],
                                               XSB[:, t2 * 512:(t2 + 1) * 512],
                                               OP.mult, OP.mult)
                        tv = V.tok()
                        S.wait(tv)
                        S.activation(dstb[:, floc * AB + t2 * 512:floc * AB + (t2 + 1) * 512],
                                     SM[:, 0:512], AF.Identity, bias=BCOL[:, fglob:fglob + 1])
            for eh in range(2):
                woff = (eh % 2) * 4096
                tpe_guard = em.now("pe")
                em.wait(SP, tpe_guard)
                tld = stream_panel(WAR[:, woff:woff + 4096], swq_st, DT, 2 * D + eh * 512, 512,
                                   sem="dwA" if eh % 2 == 0 else "dwB")
                em.wait(PE, tld)
                for fb in range(4):
                    fglob = eh * 4 + fb
                    for t2 in range(2):
                        dst = pa_mm if t2 == 0 else pa_mm2
                        tev = em.now("s")
                        em.wait(PE, tev)
                        for kt in range(DT):
                            PE.matmul(dst[:, 0:512],
                                      WAR[:, woff + kt * 512 + fb * 128:woff + kt * 512 + (fb + 1) * 128],
                                      XQT[:, kt * AB + t2 * 512:kt * AB + (t2 + 1) * 512],
                                      start=(kt == 0), stop=(kt == DT - 1))
                        tpe = PE.mark()
                        V.wait(tpe)
                        V.wait(txsb)
                        V.scalar_tensor_tensor(SM[:, 0:512], dst[:, 0:512], WSC[:, 0:1],
                                               XSB[:, t2 * 512:(t2 + 1) * 512],
                                               OP.mult, OP.mult)
                        tv = V.tok()
                        S.wait(tv)
                        S.activation(VTR[:, fglob * AB + t2 * 512:fglob * AB + (t2 + 1) * 512],
                                     SM[:, 0:512], AF.Identity,
                                     bias=BCOL[:, 16 + fglob:17 + fglob])
            for gh in range(2):
                woff = (gh % 2) * 4096
                tpe_guard = em.now("pe")
                em.wait(SP, tpe_guard)
                tld = stream_panel(WAR[:, woff:woff + 4096], swg_st, DT, gh * 512, 512,
                                   sem="dwA" if gh % 2 == 0 else "dwB")
                em.wait(PE, tld)
                for fb in range(4):
                    fglob = gh * 4 + fb
                    for t2 in range(2):
                        dst = pa_mm if t2 == 0 else pa_mm2
                        tev = em.now("v")
                        em.wait(PE, tev)
                        for kt in range(DT):
                            PE.matmul(dst[:, 0:512],
                                      WAR[:, woff + kt * 512 + fb * 128:woff + kt * 512 + (fb + 1) * 128],
                                      XQT[:, kt * AB + t2 * 512:kt * AB + (t2 + 1) * 512],
                                      start=(kt == 0), stop=(kt == DT - 1))
                        tpe = PE.mark()
                        V.wait(tpe)
                        V.scalar_tensor_tensor(SM[:, 0:512], dst[:, 0:512], WSC[:, 1:2],
                                               XSB[:, t2 * 512:(t2 + 1) * 512],
                                               OP.mult, OP.mult)
                        tv = V.tok()
                        S.wait(tv)
                        S.activation(SM2[:, 0:512], SM[:, 0:512], AF.Sigmoid,
                                     bias=BCOL[:, 24 + fglob:25 + fglob])
                        ts = S.tok()
                        V.wait(ts)
                        for cc in range(4):
                            c = t2 * 4 + cc
                            V.reduce_sum(DECAY[:, fglob * NPT + c:fglob * NPT + c + 1],
                                         SM2[:, cc * 128:(cc + 1) * 128], axis=AX)
            V.tensor_scalar(DECAY[:, 0:8 * NPT], DECAY[:, 0:8 * NPT],
                            1.0 / CHUNK, None, OP.mult)
            V.wait(trs)
            for buf in (QR, KR):
                for f in range(DT):
                    tev = em.now("s")
                    em.wait(PE, tev)
                    for t2 in range(2):
                        dst = pa_mm if t2 == 0 else pa_mm2
                        PE.matmul(dst[:, 0:512], PMAT[:],
                                  buf[:, f * AB + t2 * 512:f * AB + (t2 + 1) * 512],
                                  start=True, stop=True)
                    tpe = PE.mark()
                    S.wait(tpe)
                    S.copy(XQN[:, 0:512], pa_mm[:, 0:512])
                    S.copy(XQN[:, 512:1024], pa_mm2[:, 0:512])
                    tsh = S.tok()
                    V.tensor_tensor(SM2[:, 0:AB], buf[:, f * AB:(f + 1) * AB], CREP[:], OP.mult)
                    V.wait(tsh)
                    V.tensor_tensor(SM2[:, AB:2 * AB], XQN[:, 0:AB], SREP[:], OP.mult)
                    V.tensor_tensor(buf[:, f * AB:(f + 1) * AB], SM2[:, 0:AB],
                                    SM2[:, AB:2 * AB], OP.add)
            tv = V.tok()
            em.wait(PE, tv)

    def phase_B(l, blk):
        base = blk * AB
        Q5 = BIGB[0:64, 0:2048].rearrange("p (h q) -> p h q", h=16)
        K5 = BIGB[0:64, 2048:4096].rearrange("p (h q) -> p h q", h=16)
        TQKV = BIGF[0:64, 2048:4096]
        TQKV3 = TQKV.rearrange("p (h q) -> p q h", h=16)
        AVT = SM[0:64, 0:2048]
        AVT3 = AVT.rearrange("p (q h) -> p q h", h=16)
        XWT = SM2[0:64, 0:2048]
        XWT3 = XWT.rearrange("p (q h) -> p q h", h=16)
        V43 = VL4B[0:16, 0:8192].rearrange("p (e q) -> p e q", q=128)
        P1 = WF32[0:64, 0:128]
        M1 = WF32[0:64, 128:256]
        QSCS = WF32[0:64, 256:384]
        RDB = XQN[0:1, 0:512]
        QROW = ROWS[0:1, 0:128]
        XQE3 = XQED.rearrange("p (q h) -> p h q", h=16)

        def hsl(r0, r1, h):
            return XQE3[r0:r1, h:h + 1, :].rearrange("p o q -> p (o q)")

        def lsl(v3, lq, n):
            return v3[:, :, lq:lq + 1].rearrange(f"p {n} o -> p ({n} o)")

        with pst("pb_kn", [128, 1024], BF16) as pb_kn, \
             pst("pb_qk", [128, 1024], F32) as pb_qk, \
             pst("pb_sp", [128, 512], F32) as pb_sp, \
             pst("pb_dn", [128, 512], F32) as pb_dn, \
             pst("pb_av", [128, 512], F32) as pb_av, \
             pst("pb_p1", [128, 512], F32) as pb_p1, \
             pst("pb_p2", [128, 512], F32) as pb_p2:
            fence()
            tld0 = stream_panel(WAR[:, 0:4096], swp_st, DT, 0, 512, sem="dwA")
            tld1 = stream_panel(WAR[:, 4096:8192], swp_st, DT, 512, 512, sem="dwB")
            gfence()
            for c in range(NPT):
                t0 = base + c * 128
                if em.count.get("dbn", 0):
                    nc.gpsimd.wait_ge(em.sem("dbn"), em.count["dbn"])
                for dstd, srcb in ((qtd, QR), (ktd, KR), (vtd, VTR)):
                    em.inc(nc.gpsimd.dma_start(
                        out=dstd.rearrange("t r q -> r t q"),
                        in_=srcb.rearrange("r (t q) -> r t q", t=DT)[:, :, c * 128:(c + 1) * 128]),
                        "dbg", 16)
                V.wait(em.now("pe"))
                V.tensor_copy(KVB[:], KV[:])
                tkvb = V.tok()
                tprev = em.now("s")
                em.wait(PE, tprev)
                em.wait(PE, em.now("v"))
                for f in range(DT):
                    PE.transpose(pb_kn[:, f * 128:(f + 1) * 128],
                                 KR[:, f * AB + c * 128:f * AB + (c + 1) * 128], IDENT[:])
                tpe = PE.mark()
                S.wait(tpe)
                S.copy(KNAT[:, 0:D], pb_kn[:, 0:D])
                tkn = S.tok()
                em.wait(PE, tkn)
                for f in range(DT):
                    PE.transpose(pb_kn[:, f * 128:(f + 1) * 128],
                                 VTR[:, f * AB + c * 128:f * AB + (c + 1) * 128], IDENT[:])
                tpe = PE.mark()
                S.wait(tpe)
                S.copy(VNC[:, 0:D], pb_kn[:, 0:D])
                tvn = S.tok()
                if B_STAGE < 2:
                    continue
                em.wait(PE, tkvb)
                TQ4 = TQKV.rearrange("p (hh par q) -> p par hh q", hh=8, par=2)
                for par in range(2):
                    ro = par * 64
                    dst = pb_p1 if par == 0 else pb_p2
                    for hfq in range(2):
                        tev = em.now("s")
                        em.wait(PE, tev)
                        for j in range(4):
                            ft = hfq * 4 + j
                            PE.matmul(dst[0:64, j * 128:(j + 1) * 128],
                                      KVB[ro:ro + 64, ft * 128 + ro:ft * 128 + ro + 64],
                                      QR[ro:ro + 64, ft * AB + c * 128:ft * AB + (c + 1) * 128],
                                      start=True, stop=True)
                        tpe = PE.mark()
                        S.wait(tpe)
                        S.copy(TQ4[:, par:par + 1, hfq * 4:hfq * 4 + 4, :]
                               .rearrange("p o h q -> p (o h) q"),
                               dst[0:64, 0:512].rearrange("p (h q) -> p h q", h=4))
                if B_STAGE < 3:
                    continue
                em.wait(PE, tvn)
                for rnd in range(2):
                    tvu = em.now("v")
                    em.wait(PE, tvu)
                    for pp in range(4):
                        p = rnd * 4 + pp
                        PE.matmul(pb_sp[:, pp * 128:(pp + 1) * 128],
                                  KNAT[:, p * 128:(p + 1) * 128],
                                  VNC[:, p * 128:(p + 1) * 128],
                                  start=True, stop=True)
                    tpe = PE.mark()
                    V.wait(tpe)
                    V.wait(tkvb)
                    for pp in range(4):
                        p = rnd * 4 + pp
                        V.scalar_tensor_tensor(KV[:, p * 128:(p + 1) * 128],
                                               KV[:, p * 128:(p + 1) * 128],
                                               DECAY[:, p * NPT + c:p * NPT + c + 1],
                                               pb_sp[:, pp * 128:(pp + 1) * 128],
                                               OP.mult, OP.add)
                if B_STAGE < 4:
                    continue
                tpg = em.now("pe")
                em.wait(SP, tpg)
                SP.wait_ge(em.sem("dbg"), em.count["dbg"])
                tQ5 = em.inc(SP.dma_start(
                    out=BIGB[0:64, 0:2048].rearrange("p (h q) -> p h q", h=16),
                    in_=qtd.rearrange("t (hp d) q -> d (t hp) q", hp=2)), "dbn", 16)
                tK5 = em.inc(SP.dma_start(
                    out=BIGB[0:64, 2048:4096].rearrange("p (h q) -> p h q", h=16),
                    in_=ktd.rearrange("t (hp d) q -> d (t hp) q", hp=2)), "dbn", 16)
                tV4 = em.inc(SP.dma_start(
                    out=VL4B[0:16, 0:8192],
                    in_=vtd.rearrange("t (gp e) q -> (t gp) (e q)", gp=2)), "dbn", 16)
                em.wait(PE, tV4)
                if B_STAGE < 5:
                    continue
                for s in range(4):
                    half = (s % 2) * 512
                    tev = em.now("s")
                    em.wait(PE, tev)
                    em.wait(PE, em.now("v"))
                    for lo in range(32):
                        lq = s * 32 + lo
                        PE.matmul(pb_sp[0:16, lo * 16:(lo + 1) * 16],
                                  lsl(K5, lq, "h"), lsl(Q5, lq, "h"),
                                  start=True, stop=True)
                    tpe = PE.mark()
                    S.wait(tpe)
                    S.activation(UPC[0:16, half:half + 512], pb_sp[0:16, 0:512],
                                 AF.Exp, scale=SCALE_ATT)
                    tex = S.tok()
                    em.wait(PE, tex)
                    PE.matmul(pb_dn[0:1, 0:512], ONESB[0:16, 0:1],
                              UPC[0:16, half:half + 512], start=True, stop=True)
                    tpe = PE.mark()
                    V.wait(tpe)
                    with nc.allow_low_precision(
                            reason="softmax weights are bf16 matmul operands; "
                                   "bf16 denominator recip matches that precision"):
                        V.reciprocal(RDB, pb_dn[0:1, 0:512])
                    trc = V.tok()
                    em.wait(PE, trc)
                    PE.matmul(pb_dn[0:16, 0:512], ONESB[0:1, 0:1].broadcast_to([1, 16]),
                              RDB, start=True, stop=True)
                    tpe = PE.mark()
                    V.wait(tpe)
                    V.tensor_tensor(UPC[0:16, half:half + 512], UPC[0:16, half:half + 512],
                                    pb_dn[0:16, 0:512], OP.mult)
                    tA = V.tok()
                    em.wait(PE, tA)
                    for lo in range(32):
                        lq = s * 32 + lo
                        PE.matmul(pb_av[0:64, lo * 16:(lo + 1) * 16],
                                  lsl(V43, lq, "e"),
                                  UPC[0:16, half + lo * 16:half + (lo + 1) * 16],
                                  start=True, stop=True)
                    tpe = PE.mark()
                    V.wait(tpe)
                    V.tensor_tensor(AVT3[:, s * 32:(s + 1) * 32, :],
                                    pb_av[0:64, 0:512].rearrange("p (q h) -> p q h", h=16),
                                    TQKV3[:, s * 32:(s + 1) * 32, :], OP.add)
                    tavs = V.tok()
                    em.wait(PE, tavs)
                if B_STAGE < 6:
                    continue
                S.square(XWT, AVT)
                tsq = S.tok()
                V.wait(tsq)
                V.reduce_sum(P1, XWT3, axis=AX)
                V.tensor_tensor(XWT3, AVT3,
                                NWAT.rearrange("p (o h) -> p o h", o=1).broadcast_to([64, 128, 16]),
                                OP.mult)
                V.reduce_max(M1, XWT3, axis=AX, apply_absolute_value=True)
                tv2 = V.tok()
                em.wait(PE, tv2)
                PE.transpose(pb_qk[:, 0:64], P1, IDENTF[0:64, 0:64])
                PE.transpose(pb_qk[:, 64:128], M1, IDENTF[0:64, 0:64])
                tpe = PE.mark()
                V.wait(tpe)
                V.reduce_sum(stc(c_sum + c), pb_qk[:, 0:64], axis=AX)
                V.reduce_max(stc(c_m1 + c), pb_qk[:, 64:128], axis=AX)
                V.tensor_scalar(stc(c_t0 + c), stc(c_sum + c), 1.0 / D, EPS_RMS,
                                OP.mult, OP.add)
                V.reciprocal(stc(c_t1 + c), stc(c_t0 + c))
                tv = V.tok()
                S.wait(tv)
                S.sqrt(stc(c_rstd + c), stc(c_t1 + c))
                ts = S.tok()
                V.wait(ts)
                V.tensor_tensor(stc(c_xs + c), stc(c_rstd + c), stc(c_m1 + c), OP.mult)
                V.tensor_scalar(stc(c_xs + c), stc(c_xs + c), 1e-5, None, OP.max)
                V.reciprocal(stc(c_t0 + c), stc(c_xs + c))
                V.tensor_tensor(stc(c_t1 + c), stc(c_t0 + c), stc(c_rstd + c), OP.mult)
                V.tensor_scalar(stc(c_qsc + c), stc(c_t1 + c), 127.0, None, OP.mult)
                V.tensor_scalar(stc(c_xsa + c), stc(c_xs + c), 1.0 / 127.0, None, OP.mult)
                tv = V.tok()
                em.wait(PE, tv)
                PE.transpose(pb_qk[0:1, 128:256], stc(c_qsc + c), IDENTF[:])
                tpe = PE.mark()
                S.wait(tpe)
                for nm in ("dob", "drw"):
                    if em.count.get(nm, 0):
                        nc.scalar.wait_ge(em.sem(nm), em.count[nm])
                S.copy(QROW, pb_qk[0:1, 128:256])
                tqr = S.tok()
                em.wait(PE, tqr)
                PE.matmul(pb_qk[0:64, 256:384], ONE1F[0:1, 0:1].broadcast_to([1, 64]),
                          QROW, start=True, stop=True)
                tpe = PE.mark()
                S.wait(tpe)
                S.copy(QSCS, pb_qk[0:64, 256:384])
                tqs = S.tok()
                V.wait(tqs)
                V.tensor_tensor(XWT3, XWT3,
                                QSCS.rearrange("p (q o) -> p q o", o=1).broadcast_to([64, 128, 16]),
                                OP.mult)
                tv = V.tok()
                V.wait(em.now("dxq"))
                V.wait(em.now("pe"))
                for hf in range(2):
                    S.wait(tv)
                    S.activation(XI16[0:64, 0:D], XWT[:, hf * D:(hf + 1) * D], AF.Copy)
                    ts = S.tok()
                    V.wait(ts)
                    V.tensor_copy(XQED[0:64, hf * D:(hf + 1) * D], XI16[0:64, 0:D])
                tqe = V.tok()
                if B_STAGE < 7:
                    continue
                em.wait(SP, tqe)
                tdup = em.inc(SP.dma_start(out=XQED[64:128, 0:2048],
                                           in_=XQED[0:64, 0:2048]), "dxq", 16)
                tv0 = em.now("v")
                em.wait(SP, tv0)
                tox = em.inc(SP.dma_start(out=XOLD[:, 0:D], in_=x_st[t0:t0 + 128, :]),
                             "dxo", 16)
                if c == 0:
                    em.wait(PE, tld0)
                    em.wait(PE, tld1)
                em.wait(PE, tqe)
                em.wait(PE, tdup)
                for fh in range(2):
                    tev = em.now("v")
                    em.wait(PE, tev)
                    for j in range(8):
                        PE.matmul(pb_p1[:, 0:512],
                                  hsl(0, 64, 2 * j),
                                  WAR[0:64, fh * 4096 + j * 512:fh * 4096 + (j + 1) * 512],
                                  start=(j == 0), stop=(j == 7))
                    for j in range(8):
                        PE.matmul(pb_p2[:, 0:512],
                                  hsl(64, 128, 2 * j + 1),
                                  WAR[64:128, fh * 4096 + j * 512:fh * 4096 + (j + 1) * 512],
                                  start=(j == 0), stop=(j == 7))
                    tpe = PE.mark()
                    S.wait(tpe)
                    S.copy(SM2[:, 0:512], pb_p2[:, 0:512])
                    ts2 = S.tok()
                    V.wait(ts2)
                    V.wait(tox)
                    V.tensor_tensor(SM2[:, 0:512], SM2[:, 0:512], pb_p1[:, 0:512], OP.add)
                    V.tensor_scalar(SM2[:, 512:1024], SM2[:, 0:512],
                                    stc(c_xsa + c), WSC[:, 2:3], OP.mult, OP.mult)
                    V.tensor_tensor(SM2[:, 512:1024], SM2[:, 512:1024],
                                    BIASF[:, fh * 512:(fh + 1) * 512], OP.add)
                    V.tensor_tensor(XNEW[:, fh * 512:(fh + 1) * 512], SM2[:, 512:1024],
                                    XOLD[:, fh * 512:(fh + 1) * 512], OP.add)
                tnew = V.tok()
                em.wait(SP, tnew)
                tst = em.inc(SP.dma_start(out=x_st[t0:t0 + 128, :], in_=XNEW[:, 0:D]),
                             "dxs", 16)
                rmsnorm_quant_chunk(XNEW[:, 0:D], NWB[:, D:2 * D], c, XQN[:, 0:D])
                V.wait(tst)
                transpose_chunk(XQN[:, 0:D], pb_kn,
                                lambda f, c=c: XQ2T[:, f * AB + c * 128:f * AB + (c + 1) * 128])

    def phase_C(l, blk):
        base = blk * AB
        with pst("pc_mm", [128, 512], F32) as pc_mm, \
             pst("pc_mm2", [128, 512], F32) as pc_mm2, \
             pst("pc_bc", [128, 512], F32) as pc_bc:
            fence()
            SP.wait_ge(em.sem("dxs"), em.count["dxs"])
            trow = cols_to_row(stc(c_x127, NPT), NPT, ROWS[0:1, 0:AB])
            em.wait(PE, trow)
            bcast_row(ROWS, AB, XSB, pc_bc)
            txsb = em.now("s")
            GLUv = BIGB
            for pan in range(16):
                woff = (pan % 2) * 4096
                tpe_guard = em.now("pe")
                em.wait(SP, tpe_guard)
                tld = stream_panel(WAR[:, woff:woff + 4096], sw1_st, DT, pan * 512, 512,
                                   sem="dwA" if pan % 2 == 0 else "dwB")
                em.wait(PE, tld)
                for pr in range(2):
                    ja = pan * 4 + pr * 2
                    jg = ja + 1
                    hidx = pan * 2 + pr
                    for t2 in range(2):
                        tev = em.now("s")
                        em.wait(PE, tev)
                        for kt in range(DT):
                            PE.matmul(pc_mm[:, 0:512],
                                      WAR[:, woff + kt * 512 + (pr * 2) * 128:woff + kt * 512 + (pr * 2 + 1) * 128],
                                      XQ2T[:, kt * AB + t2 * 512:kt * AB + (t2 + 1) * 512],
                                      start=(kt == 0), stop=(kt == DT - 1))
                        for kt in range(DT):
                            PE.matmul(pc_mm2[:, 0:512],
                                      WAR[:, woff + kt * 512 + (pr * 2 + 1) * 128:woff + kt * 512 + (pr * 2 + 2) * 128],
                                      XQ2T[:, kt * AB + t2 * 512:kt * AB + (t2 + 1) * 512],
                                      start=(kt == 0), stop=(kt == DT - 1))
                        tpe = PE.mark()
                        V.wait(tpe)
                        V.wait(txsb)
                        V.scalar_tensor_tensor(SM[:, 0:512], pc_mm[:, 0:512], WSC[:, 3:4],
                                               XSB[:, t2 * 512:(t2 + 1) * 512],
                                               OP.mult, OP.mult)
                        V.scalar_tensor_tensor(SM[:, 512:1024], pc_mm2[:, 0:512], WSC[:, 3:4],
                                               XSB[:, t2 * 512:(t2 + 1) * 512],
                                               OP.mult, OP.mult)
                        tv = V.tok()
                        S.wait(tv)
                        S.activation(SM2[:, 0:512], SM[:, 0:512], AF.Identity,
                                     bias=BCOL[:, 32 + ja:33 + ja])
                        S.activation(SM2[:, 512:1024], SM[:, 512:1024], AF.Silu,
                                     bias=BCOL[:, 32 + jg:33 + jg])
                        ts = S.tok()
                        V.wait(ts)
                        V.tensor_tensor(GLUv[:, hidx * AB + t2 * 512:hidx * AB + (t2 + 1) * 512],
                                        SM2[:, 0:512], SM2[:, 512:1024], OP.mult)
            with pst("pc_q32", [128, 128], F32) as pc_q32:
                V.reduce_max(SM[:, 0:AB],
                             GLUv[:, 0:32 * AB].rearrange("p (j t) -> p t j", j=32),
                             axis=AX, apply_absolute_value=True)
                tvt = V.tok()
                em.wait(PE, tvt)
                for ci in range(NPT):
                    PE.transpose(pc_q32[:, 0:128],
                                 SM[:, ci * 128:(ci + 1) * 128], IDENTF[:])
                    tpq = PE.mark()
                    V.wait(tpq)
                    V.reduce_max(stc(c_xs + ci), pc_q32[:, 0:128], axis=AX)
                    tvq = V.tok()
                    em.wait(PE, tvq)
                V.tensor_scalar(stc(c_xs, NPT), stc(c_xs, NPT), 1e-5, None, OP.max)
                V.reciprocal(stc(c_t0, NPT), stc(c_xs, NPT))
                V.tensor_scalar(stc(c_qsc, NPT), stc(c_t0, NPT), 127.0, None, OP.mult)
                V.tensor_scalar(stc(c_x127, NPT), stc(c_xs, NPT), 1.0 / 127.0, None, OP.mult)
            trow2 = cols_to_row(stc(c_qsc, NPT), NPT, ROWS[0:1, 0:AB])
            em.wait(PE, trow2)
            bcast_row(ROWS, AB, XSB, pc_bc)
            tq = em.now("s")
            for j in range(32):
                if j == 0:
                    V.wait(tq)
                V.tensor_tensor(SM[:, 0:D], GLUv[:, j * AB:(j + 1) * AB], XSB[:], OP.mult)
                tvq = V.tok()
                S.wait(tvq)
                S.activation(XI16[:, 0:D], SM[:, 0:D], AF.Copy)
                ts = S.tok()
                V.wait(ts)
                V.tensor_copy(GLUv[:, j * AB:(j + 1) * AB], XI16[:, 0:D])
            for fhv in range(2):
                tpe_guard = em.now("pe")
                em.wait(SP, tpe_guard)
                SP.wait_ge(em.sem("dxs"), em.count["dxs"])
                tlda = stream_panel(WAR[:, 0:8192], sw2_st, HID // 128, fhv * 512, 256,
                                    kw=256, sem="dwA")
                tldb = stream_panel(WAR[:, 8192:16384], sw2_st, HID // 128,
                                    fhv * 512 + 256, 256, kw=256, sem="dwB")
                em.wait(PE, tlda)
                em.wait(PE, tldb)
                for c in range(NPT):
                    t0 = base + c * 128
                    tv0 = em.now("v")
                    em.wait(SP, tv0)
                    tox = em.inc(SP.dma_start(out=XOLD[:, 0:D],
                                              in_=x_st[t0:t0 + 128, :]), "dxo", 16)
                    for q2 in range(2):
                        woff = q2 * 8192
                        fcol = fhv * 512 + q2 * 256
                        dst = pc_mm if q2 == 0 else pc_mm2
                        tev = em.now("v")
                        em.wait(PE, tev)
                        for kt in range(HID // 128):
                            PE.matmul(dst[:, 0:256],
                                      GLUv[:, kt * AB + c * 128:kt * AB + (c + 1) * 128],
                                      WAR[:, woff + kt * 256:woff + (kt + 1) * 256],
                                      start=(kt == 0), stop=(kt == HID // 128 - 1))
                        tpe = PE.mark()
                        V.wait(tpe)
                        V.wait(tox)
                        V.tensor_scalar(SM[:, 0:256], dst[:, 0:256],
                                        stc(c_x127 + c), WSC[:, 4:5], OP.mult, OP.mult)
                        V.tensor_tensor(SM[:, 0:256], SM[:, 0:256],
                                        BIASF[:, D + fcol:D + fcol + 256], OP.add)
                        V.tensor_tensor(XNEW[:, fcol:fcol + 256], SM[:, 0:256],
                                        XOLD[:, fcol:fcol + 256], OP.add)
                    tnew = V.tok()
                    em.wait(SP, tnew)
                    tst = em.inc(SP.dma_start(
                        out=x_st[t0:t0 + 128, fhv * 512:(fhv + 1) * 512],
                        in_=XNEW[:, fhv * 512:(fhv + 1) * 512]), "dxs", 16)
                    V.wait(tst)

    def ffn_consts(e, w1b, w2b, w3b):
        with pst("fb_c", [128, 32], F32) as fb_c, \
             pst("fb_bc", [128, 512], F32) as fb_bc:
            r1 = lambda ap: ap.rearrange("(a b) -> a b", a=1)

            def ld_row2(src, n):
                tp = em.now("pe")
                em.wait(SP, tp)
                for nm in ("dob", "drw"):
                    if em.count.get(nm, 0):
                        SP.wait_ge(em.sem(nm), em.count[nm])
                tl = em.inc(SP.dma_start(out=ROWS[0:1, 0:n], in_=src), "drw", 16)
                em.wait(PE, tl)

            ld_row2(r1(w1b), FH)
            col_from_rows(ROWS, 0, FHT, BCF[:, 0:FHT], fb_c)
            ld_row2(r1(w2b), FH)
            col_from_rows(ROWS, 0, FHT, BCF[:, 16:16 + FHT], fb_c)
            ld_row2(r1(w3b), D)
            bcast_row(ROWS, D, BIASF[:, 0:D], fb_bc)

    def phase_FFN():
        XNT = BIGB[:, 0:DT * OWN]
        GLUH = BIGB[:, DT * OWN:DT * OWN + FHT * AB]
        with pst("ff_tr", [128, 1024], BF16) as ff_tr, \
             pst("ff_mm", [128, 512], F32) as ff_mm, \
             pst("ff_mm2", [128, 512], F32) as ff_mm2, \
             pst("ff_bc", [128, 512], F32) as ff_bc:
            fence()
            gfence()
            SP.wait_ge(em.sem("dxn"), em.count["dxn"])
            V.memset(XNEW[:, 0:D], 0.0)
            tz = V.tok()
            em.wait(SP, tz)
            for gq in range(NOC):
                em.inc(SP.dma_start(out=acc_st[gq * 128:(gq + 1) * 128, :],
                                    in_=XNEW[:, 0:D]), "dacc", 16)
            SP.wait_ge(em.sem("dacc"), em.count["dacc"])
            for gq in range(NOC):
                tprev_s = em.now("s")
                em.wait(nc.gpsimd, tprev_s)
                tg = em.inc(nc.gpsimd.indirect_dma_start(
                    XQN[:, 0:D], None, xn_st,
                    bass.IndirectOffsetOnAxis(IXC[:, gq:gq + 1], 0)), "dgx", 16)
                em.wait(PE, tg)
                transpose_chunk(XQN[:, 0:D], ff_tr,
                                lambda f, gq=gq: XNT[:, f * OWN + gq * 128:f * OWN + (gq + 1) * 128])
            gfence()
            tgw = None
            for kt in range(DT):
                tgw = em.inc(nc.gpsimd.dma_start(out=WAR[:, kt * FE:(kt + 1) * FE],
                                                 in_=fgw[kt * 128:(kt + 1) * 128, :]),
                             "dgw", 16)
            tp0 = em.now("pe")
            em.wait(SP, tp0)
            for nm in ("dob", "drw"):
                if em.count.get(nm, 0):
                    SP.wait_ge(em.sem(nm), em.count[nm])
            tgb = em.inc(SP.dma_start(out=ROWS[0:1, 0:FE], in_=fgb), "drw", 16)
            em.wait(PE, tgw)
            em.wait(PE, tgb)
            for gq in range(NOC):
                tpv = em.now("v")
                em.wait(PE, tpv)
                for kt in range(DT):
                    PE.matmul(ff_mm[:, 0:FE],
                              XNT[:, kt * OWN + gq * 128:kt * OWN + (gq + 1) * 128],
                              WAR[:, kt * FE:(kt + 1) * FE],
                              start=(kt == 0), stop=False)
                PE.matmul(ff_mm[:, 0:FE], ONECF[0:1, 0:1].broadcast_to([1, 128]),
                          ROWS[0:1, 0:FE], start=False, stop=True)
                tpe = PE.mark()
                V.wait(tpe)
                V.reduce_max(stc(c_t0), ff_mm[:, 0:FE], axis=AX)
                V.tensor_scalar(stc(c_t1), stc(c_t0), -1.0, None, OP.mult)
                tv = V.tok()
                S.wait(tv)
                S.activation(SM[:, 0:FE], ff_mm[:, 0:FE], AF.Exp,
                             bias=stc(c_t1), accum_out=stc(c_den))
                ts = S.tok()
                V.wait(ts)
                V.reciprocal(stc(c_rden), stc(c_den))
                V.tensor_scalar(SM[:, 8:16], SM[:, 0:FE], stc(c_rden), None, OP.mult)
                V.reduce_max(stc(c_t0), SM[:, 8:16], axis=AX)
                V.tensor_scalar(SM[:, 16:24], SM[:, 8:16], stc(c_t0), None, OP.is_ge)
                V.tensor_scalar(SM[:, 24:32], SM[:, 16:24], -3.0, None, OP.mult)
                V.tensor_tensor(SM[:, 24:32], SM[:, 8:16], SM[:, 24:32], OP.add)
                V.reduce_max(stc(c_t1), SM[:, 24:32], axis=AX)
                V.tensor_scalar(SM[:, 16:24], SM[:, 8:16], stc(c_t1), None, OP.is_ge)
                V.tensor_tensor(WEB[:, gq * 8:(gq + 1) * 8], SM[:, 8:16],
                                SM[:, 16:24], OP.mult)
            for e in range(FE + FS):
                routed = e < FE
                w1 = frw1[e] if routed else fsw1[e - FE]
                w2 = frw2[e] if routed else fsw2[e - FE]
                w3 = frw3[e] if routed else fsw3[e - FE]
                b1 = frb1[e] if routed else fsb1[e - FE]
                b2 = frb2[e] if routed else fsb2[e - FE]
                b3 = frb3[e] if routed else fsb3[e - FE]
                SP.wait_ge(em.sem("dacc"), em.count["dacc"])
                ffn_consts(e, b1, b2, b3)
                for th in range(max(OWN // AB, 1)):
                    tw0 = th * AB
                    twn = min(AB, OWN)
                    for pan in range(FH // 512):
                        woff = (pan % 2) * 4096
                        tpe_guard = em.now("pe")
                        em.wait(nc.gpsimd, tpe_guard)
                        tld = None
                        for kt in range(DT):
                            tld = em.inc(nc.gpsimd.dma_start(
                                out=WAR[:, woff + kt * 512:woff + (kt + 1) * 512],
                                in_=w1[kt * 128:(kt + 1) * 128, pan * 512:(pan + 1) * 512]),
                                "dgw", 16)
                        for kt in range(DT):
                            tld = em.inc(nc.gpsimd.dma_start(
                                out=WAR[:, 8192 + woff + kt * 512:8192 + woff + (kt + 1) * 512],
                                in_=w2[kt * 128:(kt + 1) * 128, pan * 512:(pan + 1) * 512]),
                                "dgw", 16)
                        em.wait(PE, tld)
                        for fb in range(4):
                            hglob = pan * 4 + fb
                            for t2 in range(twn // 512):
                                tcol = tw0 + t2 * 512
                                tev = em.now("v")
                                em.wait(PE, tev)
                                for kt in range(DT):
                                    PE.matmul(ff_mm[:, 0:512],
                                              WAR[:, woff + kt * 512 + fb * 128:woff + kt * 512 + (fb + 1) * 128],
                                              XNT[:, kt * OWN + tcol:kt * OWN + tcol + 512],
                                              start=(kt == 0), stop=(kt == DT - 1))
                                for kt in range(DT):
                                    PE.matmul(ff_mm2[:, 0:512],
                                              WAR[:, 8192 + woff + kt * 512 + fb * 128:8192 + woff + kt * 512 + (fb + 1) * 128],
                                              XNT[:, kt * OWN + tcol:kt * OWN + tcol + 512],
                                              start=(kt == 0), stop=(kt == DT - 1))
                                tpe = PE.mark()
                                S.wait(tpe)
                                S.activation(XQN[:, 0:512], ff_mm[:, 0:512],
                                             AF.Silu, bias=BCF[:, hglob:hglob + 1])
                                ts = S.tok()
                                V.wait(ts)
                                V.scalar_tensor_tensor(
                                    GLUH[:, hglob * AB + t2 * 512:hglob * AB + (t2 + 1) * 512],
                                    ff_mm2[:, 0:512], BCF[:, 16 + hglob:17 + hglob],
                                    XQN[:, 0:512], OP.add, OP.mult)
                    tpe_guard = em.now("pe")
                    em.wait(nc.gpsimd, tpe_guard)
                    tld = None
                    for fh2 in range(2):
                        for kt in range(FHT):
                            tld = em.inc(nc.gpsimd.dma_start(
                                out=WAR[:, fh2 * 8192 + kt * 512:fh2 * 8192 + (kt + 1) * 512],
                                in_=w3[kt * 128:(kt + 1) * 128, fh2 * 512:(fh2 + 1) * 512]),
                                "dgw", 16)
                    em.wait(PE, tld)
                    for cc in range(twn // 128):
                        gq = (tw0 + cc * 128) // 128
                        tv0 = em.now("v")
                        em.wait(SP, tv0)
                        tac = em.inc(SP.dma_start(out=XOLD[:, 0:D],
                                                  in_=acc_st[gq * 128:(gq + 1) * 128, :]),
                                     "dxo", 16)
                        for fh2 in range(2):
                            dst = ff_mm if fh2 == 0 else ff_mm2
                            tev = em.now("v")
                            em.wait(PE, tev)
                            for kt in range(FHT):
                                PE.matmul(dst[:, 0:512],
                                          GLUH[:, kt * AB + cc * 128:kt * AB + (cc + 1) * 128],
                                          WAR[:, fh2 * 8192 + kt * 512:fh2 * 8192 + (kt + 1) * 512],
                                          start=(kt == 0), stop=(kt == FHT - 1))
                            tpe = PE.mark()
                            V.wait(tpe)
                            V.wait(tac)
                            V.tensor_tensor(SM[:, 0:512], dst[:, 0:512],
                                            BIASF[:, fh2 * 512:(fh2 + 1) * 512], OP.add)
                            if routed:
                                V.tensor_scalar(SM[:, 512:1024], SM[:, 0:512],
                                                WEB[:, gq * 8 + e:gq * 8 + e + 1],
                                                None, OP.mult)
                            else:
                                V.tensor_scalar(SM[:, 512:1024], SM[:, 0:512],
                                                0.5, None, OP.mult)
                            V.tensor_tensor(XNEW[:, fh2 * 512:(fh2 + 1) * 512],
                                            SM[:, 512:1024],
                                            XOLD[:, fh2 * 512:(fh2 + 1) * 512], OP.add)
                        tnew = V.tok()
                        em.wait(SP, tnew)
                        tst = em.inc(SP.dma_start(out=acc_st[gq * 128:(gq + 1) * 128, :],
                                                  in_=XNEW[:, 0:D]), "dacc", 16)
                        V.wait(tst)

    for l in range(LAYERS):
        if not (set("ABC") & set(phases)):
            break
        prep_weights(l)
        layer_consts(l)
        V.memset(KV[:], 0.0)
        for blk in range(NBLK):
            if "A" in phases:
                phase_A(l, blk)
            if "B" in phases:
                phase_B(l, blk)
            if "C" in phases:
                phase_C(l, blk)

    if "F" in phases:
        phase_FFN()

    fence()
    gfence()
    SP.wait_ge(em.sem("dacc"), em.count["dacc"])
    SP.wait_ge(em.sem("dxs"), em.count["dxs"])
    for gq in range(NOC):
        tvp = em.now("v")
        em.wait(nc.gpsimd, tvp)
        em.wait(SP, tvp)
        tg1 = em.inc(nc.gpsimd.indirect_dma_start(
            SM[:, 0:D], None, x_st,
            bass.IndirectOffsetOnAxis(IXC[:, gq:gq + 1], 0)), "dgx", 16)
        tg2 = em.inc(nc.gpsimd.indirect_dma_start(
            SM2[:, 0:D], None, x0,
            bass.IndirectOffsetOnAxis(IXC[:, gq:gq + 1], 0)), "dgx", 16)
        ta = em.inc(SP.dma_start(out=XOLD[:, 0:D],
                                 in_=acc_st[gq * 128:(gq + 1) * 128, :]), "dxo", 16)
        V.wait(tg2)
        V.wait(ta)
        V.tensor_tensor(XNEW[:, 0:D], SM[:, 0:D], SM2[:, 0:D], OP.add)
        V.tensor_tensor(XNEW[:, 0:D], XNEW[:, 0:D], XOLD[:, 0:D], OP.add)
        tv = V.tok()
        em.wait(SP, tv)
        tw = em.inc(SP.dma_start(out=out[gq * 128:(gq + 1) * 128, :], in_=XNEW[:, 0:D]),
                    "dout", 16)
        V.wait(tw)
    nc.gpsimd.wait_ge(em.sem("dout"), em.count["dout"])

    ctx.close()
    return nc



def _rope_tables(n):
    inv = (1.0 / np.power(np.float32(10000.0),
                          (np.arange(0, HD, 2, dtype=np.float32) / np.float32(HD)).astype(np.float32))
           ).astype(np.float32)
    f = (np.arange(n, dtype=np.float32)[:, None] * inv[None, :]).astype(np.float32)
    cos = np.cos(f).astype(np.float32)
    sin = np.sin(f).astype(np.float32)
    rows = (np.arange(128) % HD) // 2
    crep = cos.T[rows, :]
    srep = sin.T[rows, :]
    return np.ascontiguousarray(crep), np.ascontiguousarray(srep)


def _pmat():
    P = np.zeros((128, 128), np.float32)
    for i in range(64):
        P[2 * i, 2 * i + 1] = -1.0
        P[2 * i + 1, 2 * i] = 1.0
    return np.ascontiguousarray(P.T).astype(ml_dtypes.bfloat16)


def _interleave_fc1_cols(wT):
    a = wT[:, 0:HID].reshape(wT.shape[0], HID // 128, 128)
    g = wT[:, HID:2 * HID].reshape(wT.shape[0], HID // 128, 128)
    inter = np.stack([a, g], axis=2)
    return np.ascontiguousarray(inter.reshape(wT.shape[0], 2 * HID))


def _interleave_fc1_bias(b):
    a = b[0:HID].reshape(HID // 128, 128)
    g = b[HID:2 * HID].reshape(HID // 128, 128)
    return np.ascontiguousarray(np.stack([a, g], axis=1).reshape(2 * HID))


def stage_inputs(inputs, NTOK, LAYERS, OWN, n_cores=8):
    f32 = np.float32
    L = LAYERS
    qkv_w = np.asarray(inputs["L0_attn_qkv_w"], f32)[:L]
    gate_w = np.asarray(inputs["L0_attn_gate_w"], f32)[:L]
    proj_w = np.asarray(inputs["L0_attn_proj_w"], f32)[:L]
    fc1_w = np.asarray(inputs["L0_attn_fc1_w"], f32)[:L]
    fc2_w = np.asarray(inputs["L0_attn_fc2_w"], f32)[:L]
    com = {
        "wqkvT": np.ascontiguousarray(qkv_w.transpose(0, 2, 1)),
        "wgateT": np.ascontiguousarray(gate_w.transpose(0, 2, 1)),
        "wprojT": np.ascontiguousarray(proj_w.transpose(0, 2, 1)),
        "wfc1T": np.ascontiguousarray(
            np.stack([_interleave_fc1_cols(np.ascontiguousarray(w.T)) for w in fc1_w])),
        "wfc2T": np.ascontiguousarray(fc2_w.transpose(0, 2, 1)),
        "bqkv": np.asarray(inputs["L0_attn_qkv_b"], f32)[:L],
        "bgate": np.asarray(inputs["L0_attn_gate_b"], f32)[:L],
        "bproj": np.asarray(inputs["L0_attn_proj_b"], f32)[:L],
        "bfc1": np.stack([_interleave_fc1_bias(np.asarray(b, f32))
                          for b in np.asarray(inputs["L0_attn_fc1_b"], f32)[:L]]),
        "bfc2": np.asarray(inputs["L0_attn_fc2_b"], f32)[:L],
        "norm1w": np.asarray(inputs["L0_attn_norm1_w"], f32)[:L],
        "norm2w": np.asarray(inputs["L0_attn_norm2_w"], f32)[:L],
        "anormwT": np.ascontiguousarray(
            np.asarray(inputs["L0_attn_attn_norm_w"], f32)[:L]
            .reshape(L, HEADS, HD).transpose(0, 2, 1)),
        "identb": np.eye(128, dtype=ml_dtypes.bfloat16),
        "identf": np.eye(128, dtype=np.float32),
        "pmatb": _pmat(),
        "fgw": np.asarray(inputs["L0_ffn_gate_w"], f32),
        "fgb": np.asarray(inputs["L0_ffn_gate_b"], f32).reshape(1, FE),
        "frw1": np.asarray(inputs["L0_ffn_rw1"], f32),
        "frb1": np.asarray(inputs["L0_ffn_rb1"], f32),
        "frw2": np.asarray(inputs["L0_ffn_rw2"], f32),
        "frb2": np.asarray(inputs["L0_ffn_rb2"], f32),
        "frw3": np.asarray(inputs["L0_ffn_rw3"], f32),
        "frb3": np.asarray(inputs["L0_ffn_rb3"], f32),
        "fsw1": np.asarray(inputs["L0_ffn_sw1"], f32),
        "fsb1": np.asarray(inputs["L0_ffn_sb1"], f32),
        "fsw2": np.asarray(inputs["L0_ffn_sw2"], f32),
        "fsb2": np.asarray(inputs["L0_ffn_sb2"], f32),
        "fsw3": np.asarray(inputs["L0_ffn_sw3"], f32),
        "fsb3": np.asarray(inputs["L0_ffn_sb3"], f32),
    }
    crep_, srep_ = _rope_tables(NTOK)
    com["crep"] = crep_
    com["srep"] = srep_
    x = np.asarray(inputs["L0_attn_x"], f32)
    in_maps = []
    for c in range(n_cores):
        b = c // 4
        q = c % 4
        idx = (np.arange(128, dtype=np.int32)[:, None]
               + (q * OWN + np.arange(OWN // 128, dtype=np.int32)[None, :] * 128))
        m = dict(com)
        m["x0"] = np.ascontiguousarray(x[b, 0:NTOK, :])
        m["ownidx"] = np.ascontiguousarray(idx)
        in_maps.append(m)
    return in_maps


def kernel(**inputs):
    x = np.asarray(inputs["L0_attn_x"])
    B, NTOK, _ = x.shape
    LAYERS = np.asarray(inputs["L0_attn_norm1_w"]).shape[0]
    OWN = NTOK // 4
    in_maps = stage_inputs(inputs, NTOK, LAYERS, OWN, n_cores=8)
    nc = build_program(NTOK, LAYERS, OWN)
    res = run_bass_kernel_spmd(nc, in_maps, core_ids=list(range(8)))
    outp = np.empty((B, NTOK, D), np.float32)
    for c in range(8):
        b = c // 4
        q = c % 4
        outp[b, q * OWN:(q + 1) * OWN, :] = res.results[c]["out"]
    return outp


# version 03
# speedup vs baseline: 1.0712x; 1.0712x over previous
import numpy as np
import ml_dtypes
from contextlib import ExitStack

import concourse.bass as bass
import concourse.mybir as mybir
from concourse.bass_utils import run_bass_kernel_spmd

F32 = mybir.dt.float32
BF16 = mybir.dt.bfloat16
I16 = mybir.dt.int16
I32 = mybir.dt.int32
I8 = mybir.dt.int8
AF = mybir.ActivationFunctionType
OP = mybir.AluOpType
AX = mybir.AxisListType.X

D = 1024
DT = D // 128
HEADS = 16
HD = 64
CHUNK = 128
HID = 4096
FE, FS, FH = 8, 2, 2048
FHT = FH // 128
EPS_RMS = 1e-6
EPS_LN = 1e-5
SCALE_ATT = HD ** -0.5
AB = 1024
NPT = AB // 128


class Emit:
    def __init__(self, nc):
        self.nc = nc
        self.sems = {}
        self.count = {}
        self.last_wait = {}

    def sem(self, name):
        if name not in self.sems:
            self.sems[name] = self.nc.alloc_semaphore(name)
            self.count[name] = 0
        return self.sems[name]

    def inc(self, instr, name, n=1):
        instr.then_inc(self.sem(name), n)
        self.count[name] += n
        return (name, self.count[name])

    def wait(self, eng, tok):
        if tok is None:
            return
        name, val = tok
        if val <= 0:
            return
        key = (id(eng), name)
        if self.last_wait.get(key, 0) >= val:
            return
        eng.wait_ge(self.sem(name), val)
        self.last_wait[key] = val

    def now(self, name):
        return (name, self.count.get(name, 0))


class Chained:

    def __init__(self, em, eng, name, buddies=()):
        object.__setattr__(self, "em", em)
        object.__setattr__(self, "eng", eng)
        object.__setattr__(self, "name", name)
        object.__setattr__(self, "buddies", buddies)

    def __getattr__(self, attr):
        f = getattr(self.eng, attr)

        def call(*a, **k):
            em, eng, name = self.em, self.eng, self.name
            em.wait(eng, em.now(name))
            for b in self.buddies:
                em.wait(eng, em.now(b))
            return em.inc(f(*a, **k), name, 1)

        return call

    def wait(self, tok):
        self.em.wait(self.eng, tok)

    def tok(self):
        return self.em.now(self.name)


B_STAGE = 99


def build_program(NTOK, LAYERS, OWN, phases="ABCF"):
    NBLK = NTOK // AB
    NOC = OWN // CHUNK
    nc = bass.Bass("TRN2", target_bir_lowering=False, debug=False)
    nc.num_devices = 8

    dr = lambda n, s, dt, k="ExternalInput": nc.dram_tensor(n, s, dt, kind=k).ap()
    x0 = dr("x0", [NTOK, D], F32)
    wqkvT = dr("wqkvT", [LAYERS, D, 3 * D], F32)
    wgateT = dr("wgateT", [LAYERS, D, D], F32)
    wprojT = dr("wprojT", [LAYERS, D, D], F32)
    wfc1T = dr("wfc1T", [LAYERS, D, 2 * HID], F32)
    wfc2T = dr("wfc2T", [LAYERS, HID, D], F32)
    bqkv = dr("bqkv", [LAYERS, 3 * D], F32)
    bgate = dr("bgate", [LAYERS, D], F32)
    bproj = dr("bproj", [LAYERS, D], F32)
    bfc1 = dr("bfc1", [LAYERS, 2 * HID], F32)
    bfc2 = dr("bfc2", [LAYERS, D], F32)
    norm1w = dr("norm1w", [LAYERS, D], F32)
    norm2w = dr("norm2w", [LAYERS, D], F32)
    anormwT = dr("anormwT", [LAYERS, 64, 16], F32)
    crep = dr("crep", [128, NTOK], F32)
    srep = dr("srep", [128, NTOK], F32)
    identb = dr("identb", [128, 128], BF16)
    identf = dr("identf", [128, 128], F32)
    pmatb = dr("pmatb", [128, 128], BF16)
    fgw = dr("fgw", [D, FE], F32)
    fgb = dr("fgb", [1, FE], F32)
    frw1 = dr("frw1", [FE, D, FH], F32)
    frb1 = dr("frb1", [FE, FH], F32)
    frw2 = dr("frw2", [FE, D, FH], F32)
    frb2 = dr("frb2", [FE, FH], F32)
    frw3 = dr("frw3", [FE, FH, D], F32)
    frb3 = dr("frb3", [FE, D], F32)
    fsw1 = dr("fsw1", [FS, D, FH], F32)
    fsb1 = dr("fsb1", [FS, FH], F32)
    fsw2 = dr("fsw2", [FS, D, FH], F32)
    fsb2 = dr("fsb2", [FS, FH], F32)
    fsw3 = dr("fsw3", [FS, FH, D], F32)
    fsb3 = dr("fsb3", [FS, D], F32)
    ownidx = dr("ownidx", [128, NOC], I32)
    out = dr("out", [OWN, D], F32, "ExternalOutput")

    x_st = dr("x_st", [NTOK, D], F32, "Internal")
    xn_st = dr("xn_st", [NTOK, D], BF16, "Internal")
    acc_st = dr("acc_st", [OWN, D], F32, "Internal")
    swq_st = dr("swq_st", [D, 3 * D], BF16, "Internal")
    swg_st = dr("swg_st", [D, D], BF16, "Internal")
    swp_st = dr("swp_st", [D, D], BF16, "Internal")
    sw1_st = dr("sw1_st", [D, 2 * HID], BF16, "Internal")
    sw2_st = dr("sw2_st", [HID, D], BF16, "Internal")
    row_st = dr("row_st", [1, 2048], F32, "Internal")
    qtd = dr("qtd", [DT, 128, 128], BF16, "Internal")
    ktd = dr("ktd", [DT, 128, 128], BF16, "Internal")
    vtd = dr("vtd", [DT, 128, 128], BF16, "Internal")

    em = Emit(nc)
    ctx = ExitStack()
    sb = lambda name, shape, dt: ctx.enter_context(nc.sbuf_tensor(name, shape, dt))
    _uid = [0]

    def pst(nm, shape, dt):
        _uid[0] += 1
        return nc.psum_tensor(f"{nm}_{_uid[0]}", shape, dt)

    def fence():
        for nm in ("pe", "v", "s", "g"):
            em.wait(nc.sync, em.now(nm))

    def gfence():
        for nm in ("pe", "v", "s"):
            em.wait(nc.gpsimd, em.now(nm))

    IDENT = sb("IDENT", [128, 128], BF16)
    IDENTF = sb("IDENTF", [128, 128], F32)
    PMAT = sb("PMAT", [128, 128], BF16)
    ONECF = sb("ONECF", [128, 1], F32)
    ONE1F = sb("ONE1F", [1, 1], F32)
    ONESB = sb("ONESB", [128, 1], BF16)
    NWB = sb("NWB", [128, 2 * D], F32)
    NWAT = sb("NWAT", [64, 16], F32)
    BIASF = sb("BIASF", [128, 2 * D], F32)
    BCOL = sb("BCOL", [128, 96], F32)
    BCF = sb("BCF", [128, 48], F32)
    WSC = sb("WSC", [128, 8], F32)
    KV = sb("KV", [128, 8 * 128], F32)
    KVB = sb("KVB", [128, 8 * 128], BF16)
    DECAY = sb("DECAY", [128, 8 * NPT], F32)
    ST = sb("ST", [128, 176], F32)
    ROWS = sb("ROWS", [1, 2048], F32)
    XSB = sb("XSB", [128, AB], F32)
    SM = sb("SM", [128, 2 * D], F32)
    SM2 = sb("SM2", [128, 2 * D], F32)
    XI16 = sb("XI16", [128, D], I16)
    XQN = sb("XQN", [128, D], BF16)
    KNAT = sb("KNAT", [128, D], BF16)
    VNC = sb("VNC", [128, D], BF16)
    XOLD = sb("XOLD", [128, D], F32)
    XNEW = sb("XNEW", [128, D], F32)
    IXC = sb("IXC", [128, NOC], I32)
    WEB = sb("WEB", [128, 128], F32)
    BIGB = sb("BIGB", [128, 32768], BF16)
    WAR = sb("WAR", [128, 16384], BF16)
    WF32 = sb("WF32", [128, D], F32)
    VL4B = sb("VL4B", [128, 8192], BF16)
    UPC = sb("UPC", [128, 1024], BF16)
    XQED = sb("XQED", [128, 2048], BF16)

    XQT = BIGB[:, 0:8192]
    QR = BIGB[:, 8192:16384]
    KR = BIGB[:, 16384:24576]
    VTR = BIGB[:, 24576:32768]
    XQ2T = WAR[:, 8192:16384]
    BIGF = BIGB.bitcast(F32)
    WARF = WAR.bitcast(F32)
    CREP = WARF[:, 4096:5120]
    SREP = WARF[:, 5120:6144]

    SP = nc.sync

    class _PE:
        def __init__(self):
            self.last = None

        def matmul(self, *a, **k):
            self.last = nc.tensor.matmul(*a, **k)
            return self.last

        def transpose(self, *a, **k):
            self.last = nc.tensor.transpose(*a, **k)
            return self.last

        def wait_ge(self, sem, val):
            return nc.tensor.wait_ge(sem, val)

        def mark(self):
            return em.inc(self.last, "pe")

    PE = _PE()
    V = Chained(em, nc.vector, "v", buddies=("s",))
    S = Chained(em, nc.scalar, "s", buddies=("v",))
    GC = Chained(em, nc.gpsimd, "g")

    class Cols:
        def __init__(self):
            self.n = 0

        def take(self, k):
            a = self.n
            self.n += k
            assert self.n <= 176, self.n
            return a

    CL = Cols()
    c_sum = CL.take(NPT)
    c_msq = CL.take(NPT)
    c_rstd = CL.take(NPT)
    c_m1 = CL.take(NPT)
    c_xs = CL.take(NPT)
    c_qsc = CL.take(NPT)
    c_x127 = CL.take(NPT)
    c_t0 = CL.take(NPT)
    c_t1 = CL.take(NPT)
    c_xsa = CL.take(NPT)
    c_amax = CL.take(4)
    c_abias = CL.take(4)
    c_den = CL.take(4)
    c_rden = CL.take(4)
    c_wacc = CL.take(1)
    c_wtmp = CL.take(1)
    c_ln = CL.take(5 * NPT)

    def stc(c0, k=1):
        return ST[:, c0:c0 + k]

    def bcast_row(row_ap, n, dst_ap, ps_bc):
        for j in range(0, n, 512):
            w = min(512, n - j)
            PE.matmul(ps_bc[:, j % 512:j % 512 + w] if False else ps_bc[:, 0:w],
                      ONECF[0:1, 0:1].broadcast_to([1, 128]),
                      row_ap[0:1, j:j + w], start=True, stop=True)
            tpe = PE.mark()
            S.wait(tpe)
            S.copy(dst_ap[:, j:j + w], ps_bc[:, 0:w])
            ts = S.tok()
            em.wait(PE, ts)
        return S.tok()

    def col_from_rows(row_ap, start, ncols, dst_cols, ps_c):
        for j in range(ncols):
            PE.matmul(ps_c[:, j:j + 1],
                      row_ap[0:1, start + j * 128:start + (j + 1) * 128],
                      ONE1F[0:1, 0:1], start=True, stop=True)
        tpe = PE.mark()
        V.wait(tpe)
        V.tensor_copy(dst_cols, ps_c[:, 0:ncols])
        tv = V.tok()
        em.wait(PE, tv)
        return tv

    def cols_to_row(col_block, ncols, dst_row):
        em.wait(SP, em.now("v"))
        for nm in ("dob", "drw"):
            if em.count.get(nm, 0):
                em.wait(SP, em.now(nm))
        t1 = em.inc(SP.dma_start(
            out=row_st.rearrange("a b -> (a b)")[0:128 * ncols].rearrange(
                "(c p) -> p c", p=128),
            in_=col_block, allow_slow_non_contiguous=True), "dob", 16)
        em.wait(SP, t1)
        t2 = em.inc(SP.dma_start(out=dst_row, in_=row_st[0:1, 0:128 * ncols]), "dob", 16)
        return t2

    def row_to_cols(src_row, ncols, dst_cols):
        em.wait(SP, em.now("v"))
        for nm in ("dob", "drw"):
            if em.count.get(nm, 0):
                em.wait(SP, em.now(nm))
        t1 = em.inc(SP.dma_start(out=row_st[0:1, 0:128 * ncols], in_=src_row), "dob", 16)
        em.wait(SP, t1)
        t2 = em.inc(SP.dma_start(
            out=dst_cols,
            in_=row_st.rearrange("a b -> (a b)")[0:128 * ncols].rearrange(
                "(c p) -> p c", p=128), allow_slow_non_contiguous=True), "dob", 16)
        return t2

    def rmsnorm_quant_chunk(xin_f32, nwb_ap, ci, out_bf):
        xw = SM2[:, 0:D]
        V.tensor_tensor(xw, xin_f32, nwb_ap, OP.mult)
        tv = V.tok()
        S.wait(tv)
        S.square(SM2[:, D:2 * D], xin_f32)
        tsq = S.tok()
        V.wait(tsq)
        V.reduce_sum(stc(c_sum + ci), SM2[:, D:2 * D], axis=AX)
        V.reduce_max(stc(c_m1 + ci), xw, axis=AX, apply_absolute_value=True)
        V.tensor_scalar(stc(c_t0 + ci), stc(c_sum + ci), 1.0 / D, EPS_RMS, OP.mult, OP.add)
        V.reciprocal(stc(c_t1 + ci), stc(c_t0 + ci))
        tv = V.tok()
        S.wait(tv)
        S.sqrt(stc(c_rstd + ci), stc(c_t1 + ci))
        ts = S.tok()
        V.wait(ts)
        V.tensor_tensor(stc(c_xs + ci), stc(c_rstd + ci), stc(c_m1 + ci), OP.mult)
        V.tensor_scalar(stc(c_xs + ci), stc(c_xs + ci), 1e-5, None, OP.max)
        V.reciprocal(stc(c_t0 + ci), stc(c_xs + ci))
        V.tensor_tensor(stc(c_t1 + ci), stc(c_t0 + ci), stc(c_rstd + ci), OP.mult)
        V.tensor_scalar(stc(c_qsc + ci), stc(c_t1 + ci), 127.0, None, OP.mult)
        V.tensor_scalar(stc(c_x127 + ci), stc(c_xs + ci), 1.0 / 127.0, None, OP.mult)
        tv = V.tok()
        S.wait(tv)
        S.activation(XI16[:, 0:D], xw, AF.Copy, scale=stc(c_qsc + ci))
        ts = S.tok()
        V.wait(ts)
        V.wait(em.now("pe"))
        V.tensor_copy(out_bf, XI16[:, 0:D])
        return V.tok()

    def transpose_chunk(src_bf, ps_tr, col_map):
        tprev = em.now("s")
        em.wait(PE, tprev)
        em.wait(PE, em.now("v"))
        for f in range(DT):
            PE.transpose(ps_tr[:, f * 128:(f + 1) * 128],
                         src_bf[:, f * 128:(f + 1) * 128], IDENT[:])
        tpe = PE.mark()
        S.wait(tpe)
        for f in range(DT):
            S.copy(col_map(f), ps_tr[:, f * 128:(f + 1) * 128])
        return S.tok()

    tc = None
    for dst, src in ((IDENT[:], identb), (IDENTF[:], identf), (PMAT[:], pmatb), (IXC[:], ownidx)):
        tc = em.inc(SP.dma_start(out=dst, in_=src), "dcst", 16)
    V.wait(tc)
    V.memset(ONECF[:], 1.0)
    V.memset(ONE1F[:], 1.0)
    V.memset(ONESB[:], 1.0)

    for blk in range(NBLK):
        fence()
        ld = []
        for c in range(NPT):
            t0 = blk * AB + c * 128
            ld.append(em.inc(SP.dma_start(out=BIGF[:, c * D:(c + 1) * D],
                                          in_=x0[t0:t0 + 128, :]), "dpre", 16))
        for c in range(NPT):
            xc = BIGF[:, c * D:(c + 1) * D]
            V.wait(ld[-1])
            V.reduce_sum(stc(c_ln + c), xc, axis=AX)
            tv = V.tok()
            S.wait(tv)
            S.square(SM2[:, 0:D], xc)
            ts = S.tok()
            V.wait(ts)
            V.reduce_sum(stc(c_ln + NPT + c), SM2[:, 0:D], axis=AX)
        cM = stc(c_ln + 2 * NPT, NPT)
        cV_ = stc(c_ln + 3 * NPT, NPT)
        cR = stc(c_ln + 4 * NPT, NPT)
        V.tensor_scalar(cM, stc(c_ln, NPT), 1.0 / D, None, OP.mult)
        V.tensor_tensor(cV_, cM, cM, OP.mult)
        V.tensor_scalar(stc(c_ln + NPT, NPT), stc(c_ln + NPT, NPT), 1.0 / D, None, OP.mult)
        V.tensor_tensor(cV_, stc(c_ln + NPT, NPT), cV_, OP.subtract)
        V.tensor_scalar(cV_, cV_, EPS_LN, None, OP.add)
        V.reciprocal(cV_, cV_)
        tv = V.tok()
        S.wait(tv)
        S.sqrt(cR, cV_)
        ts = S.tok()
        V.wait(ts)
        V.tensor_tensor(cM, cM, cR, OP.mult)
        V.tensor_scalar(cM, cM, -1.0, None, OP.mult)
        for c in range(NPT):
            t0 = blk * AB + c * 128
            xc = BIGF[:, c * D:(c + 1) * D]
            V.tensor_scalar(SM2[:, 0:D], xc, cR[:, c:c + 1], cM[:, c:c + 1],
                            OP.mult, OP.add)
            ta = V.tok()
            em.wait(SP, ta)
            ts1 = em.inc(SP.dma_start(out=x_st[t0:t0 + 128, :], in_=SM2[:, 0:D]), "dxs", 16)
            V.tensor_copy(XQN[:, 0:D], SM2[:, 0:D])
            tb = V.tok()
            em.wait(SP, tb)
            ts2 = em.inc(SP.dma_start(out=xn_st[t0:t0 + 128, :], in_=XQN[:, 0:D]), "dxn", 16)
            V.wait(ts1)
            V.wait(ts2)

    def prep_weights(l):
        specs = [(wqkvT[l], swq_st, D, 3 * D, 0), (wgateT[l], swg_st, D, D, 1),
                 (wprojT[l], swp_st, D, D, 2), (wfc1T[l], sw1_st, D, 2 * HID, 3),
                 (wfc2T[l], sw2_st, HID, D, 4)]
        with pst("pw_ps", [128, 8], F32) as pw_ps:
            prev_store = None
            for wap, st_ap, krows, ncols, k in specs:
                V.memset(stc(c_wacc), 0.0)
                for kt in range(krows // 128):
                    for j0 in range(0, ncols, D):
                        w = min(D, ncols - j0)
                        tg = em.now("v")
                        ts0 = em.now("s")
                        em.wait(SP, tg)
                        em.wait(SP, ts0)
                        tl = em.inc(SP.dma_start(
                            out=WF32[:, 0:w],
                            in_=wap[kt * 128:(kt + 1) * 128, j0:j0 + w]), "dwf", 16)
                        S.wait(tl)
                        S.wait(prev_store)
                        S.sign(XQN[:, 0:w], WF32[:, 0:w])
                        tsg = S.tok()
                        V.wait(tl)
                        V.reduce_sum(stc(c_wtmp), WF32[:, 0:w], axis=AX,
                                      apply_absolute_value=True)
                        V.tensor_tensor(stc(c_wacc), stc(c_wacc), stc(c_wtmp), OP.add)
                        em.wait(SP, tsg)
                        prev_store = em.inc(SP.dma_start(
                            out=st_ap[kt * 128:(kt + 1) * 128, j0:j0 + w],
                            in_=XQN[:, 0:w]), "dsw", 16)
                tg = V.tok()
                em.wait(PE, tg)
                PE.matmul(pw_ps[0:1, 0:1], ONECF[:], stc(c_wacc), start=True, stop=True)
                tp = PE.mark()
                V.wait(tp)
                V.tensor_scalar(ROWS[0:1, 0:1], pw_ps[0:1, 0:1],
                                1.0 / (krows * ncols), 1e-5, OP.mult, OP.max)
                tv = V.tok()
                em.wait(PE, tv)
                PE.matmul(pw_ps[:, k:k + 1], ONE1F[0:1, 0:1].broadcast_to([1, 128]),
                          ROWS[0:1, 0:1], start=True, stop=True)
                tp = PE.mark()
                V.wait(tp)
                V.tensor_copy(WSC[:, k:k + 1], pw_ps[:, k:k + 1])
                tv = V.tok()
                em.wait(PE, tv)
        em.wait(SP, em.now("dsw"))

    def layer_consts(l):
        with pst("lc_bc", [128, 512], F32) as lc_bc, \
             pst("lc_c", [128, 96], F32) as lc_c:
            r1 = lambda ap: ap.rearrange("(a b) -> a b", a=1)

            def ld_row(src, n):
                tp = em.now("pe")
                em.wait(SP, tp)
                for nm in ("dob", "drw"):
                    if em.count.get(nm, 0):
                        em.wait(SP, em.now(nm))
                tl = em.inc(SP.dma_start(out=ROWS[0:1, 0:n], in_=src), "drw", 16)
                em.wait(PE, tl)

            for idx, row_ap in enumerate((norm1w[l], norm2w[l])):
                ld_row(r1(row_ap), D)
                bcast_row(ROWS, D, NWB[:, idx * D:(idx + 1) * D], lc_bc)
            for idx, row_ap in enumerate((r1(bproj[l]), r1(bfc2[l]))):
                ld_row(row_ap, D)
                bcast_row(ROWS, D, BIASF[:, idx * D:(idx + 1) * D], lc_bc)
            tg0 = em.now("v")
            em.wait(SP, tg0)
            tnw = em.inc(SP.dma_start(out=NWAT[:], in_=anormwT[l]), "dnw", 16)
            V.wait(tnw)
            ld_row(r1(bqkv[l])[0:1, 0:2 * D], 2 * D)
            col_from_rows(ROWS, 0, 16, BCOL[:, 0:16], lc_c)
            ld_row(r1(bqkv[l])[0:1, 2 * D:3 * D], D)
            col_from_rows(ROWS, 0, 8, BCOL[:, 16:24], lc_c)
            ld_row(r1(bgate[l]), D)
            col_from_rows(ROWS, 0, 8, BCOL[:, 24:32], lc_c)
            for piece in range(4):
                ld_row(r1(bfc1[l])[0:1, piece * 2048:(piece + 1) * 2048], 2048)
                col_from_rows(ROWS, 0, 16, BCOL[:, 32 + piece * 16:48 + piece * 16], lc_c)

    def stream_panel(dst_view, src_st, kt_rows, col0, ncols, kw=512, sem="dwA"):
        tld = None
        for kt in range(kt_rows):
            tld = em.inc(SP.dma_start(
                out=dst_view[:, kt * kw:kt * kw + ncols],
                in_=src_st[kt * 128:(kt + 1) * 128, col0:col0 + ncols]), sem, 16)
        return tld

    def phase_A(l, blk):
        base = blk * AB
        with pst("pa_mm", [128, 512], F32) as pa_mm, \
             pst("pa_mm2", [128, 512], F32) as pa_mm2, \
             pst("pa_tr", [128, 1024], BF16) as pa_tr, \
             pst("pa_bc", [128, 512], F32) as pa_bc:
            fence()
            em.wait(SP, em.now("dxs"))
            trc = em.inc(SP.dma_start(out=CREP[:], in_=crep[:, base:base + AB]), "dcr", 16)
            trs = em.inc(SP.dma_start(out=SREP[:], in_=srep[:, base:base + AB]), "dcr", 16)
            for c in range(NPT):
                t0 = base + c * 128
                tv0 = em.now("v")
                em.wait(SP, tv0)
                tl = em.inc(SP.dma_start(out=XOLD[:, 0:D], in_=x_st[t0:t0 + 128, :]),
                            "dxo", 16)
                V.wait(tl)
                rmsnorm_quant_chunk(XOLD[:, 0:D], NWB[:, 0:D], c, XQN[:, 0:D])
                transpose_chunk(XQN[:, 0:D], pa_tr,
                                lambda f, c=c: XQT[:, f * AB + c * 128:f * AB + (c + 1) * 128])
            trow = cols_to_row(stc(c_x127, NPT), NPT, ROWS[0:1, 0:AB])
            em.wait(PE, trow)
            bcast_row(ROWS, AB, XSB, pa_bc)
            txsb = em.now("s")
            for pan in range(4):
                woff = (pan % 2) * 4096
                tpe_guard = em.now("pe")
                em.wait(SP, tpe_guard)
                tld = stream_panel(WAR[:, woff:woff + 4096], swq_st, DT, pan * 512, 512,
                                   sem="dwA" if pan % 2 == 0 else "dwB")
                em.wait(PE, tld)
                for fb in range(4):
                    fglob = pan * 4 + fb
                    dstb = QR if fglob < 8 else KR
                    floc = fglob % 8
                    for t2 in range(2):
                        dst = pa_mm if t2 == 0 else pa_mm2
                        tev = em.now("s")
                        em.wait(PE, tev)
                        for kt in range(DT):
                            PE.matmul(dst[:, 0:512],
                                      WAR[:, woff + kt * 512 + fb * 128:woff + kt * 512 + (fb + 1) * 128],
                                      XQT[:, kt * AB + t2 * 512:kt * AB + (t2 + 1) * 512],
                                      start=(kt == 0), stop=(kt == DT - 1))
                        tpe = PE.mark()
                        V.wait(tpe)
                        V.wait(txsb)
                        V.scalar_tensor_tensor(SM[:, 0:512], dst[:, 0:512], WSC[:, 0:1],
                                               XSB[:, t2 * 512:(t2 + 1) * 512],
                                               OP.mult, OP.mult)
                        tv = V.tok()
                        S.wait(tv)
                        S.activation(dstb[:, floc * AB + t2 * 512:floc * AB + (t2 + 1) * 512],
                                     SM[:, 0:512], AF.Identity, bias=BCOL[:, fglob:fglob + 1])
            for eh in range(2):
                woff = (eh % 2) * 4096
                tpe_guard = em.now("pe")
                em.wait(SP, tpe_guard)
                tld = stream_panel(WAR[:, woff:woff + 4096], swq_st, DT, 2 * D + eh * 512, 512,
                                   sem="dwA" if eh % 2 == 0 else "dwB")
                em.wait(PE, tld)
                for fb in range(4):
                    fglob = eh * 4 + fb
                    for t2 in range(2):
                        dst = pa_mm if t2 == 0 else pa_mm2
                        tev = em.now("s")
                        em.wait(PE, tev)
                        for kt in range(DT):
                            PE.matmul(dst[:, 0:512],
                                      WAR[:, woff + kt * 512 + fb * 128:woff + kt * 512 + (fb + 1) * 128],
                                      XQT[:, kt * AB + t2 * 512:kt * AB + (t2 + 1) * 512],
                                      start=(kt == 0), stop=(kt == DT - 1))
                        tpe = PE.mark()
                        V.wait(tpe)
                        V.wait(txsb)
                        V.scalar_tensor_tensor(SM[:, 0:512], dst[:, 0:512], WSC[:, 0:1],
                                               XSB[:, t2 * 512:(t2 + 1) * 512],
                                               OP.mult, OP.mult)
                        tv = V.tok()
                        S.wait(tv)
                        S.activation(VTR[:, fglob * AB + t2 * 512:fglob * AB + (t2 + 1) * 512],
                                     SM[:, 0:512], AF.Identity,
                                     bias=BCOL[:, 16 + fglob:17 + fglob])
            for gh in range(2):
                woff = (gh % 2) * 4096
                tpe_guard = em.now("pe")
                em.wait(SP, tpe_guard)
                tld = stream_panel(WAR[:, woff:woff + 4096], swg_st, DT, gh * 512, 512,
                                   sem="dwA" if gh % 2 == 0 else "dwB")
                em.wait(PE, tld)
                for fb in range(4):
                    fglob = gh * 4 + fb
                    for t2 in range(2):
                        dst = pa_mm if t2 == 0 else pa_mm2
                        tev = em.now("v")
                        em.wait(PE, tev)
                        for kt in range(DT):
                            PE.matmul(dst[:, 0:512],
                                      WAR[:, woff + kt * 512 + fb * 128:woff + kt * 512 + (fb + 1) * 128],
                                      XQT[:, kt * AB + t2 * 512:kt * AB + (t2 + 1) * 512],
                                      start=(kt == 0), stop=(kt == DT - 1))
                        tpe = PE.mark()
                        V.wait(tpe)
                        V.scalar_tensor_tensor(SM[:, 0:512], dst[:, 0:512], WSC[:, 1:2],
                                               XSB[:, t2 * 512:(t2 + 1) * 512],
                                               OP.mult, OP.mult)
                        tv = V.tok()
                        S.wait(tv)
                        S.activation(SM2[:, 0:512], SM[:, 0:512], AF.Sigmoid,
                                     bias=BCOL[:, 24 + fglob:25 + fglob])
                        ts = S.tok()
                        V.wait(ts)
                        for cc in range(4):
                            c = t2 * 4 + cc
                            V.reduce_sum(DECAY[:, fglob * NPT + c:fglob * NPT + c + 1],
                                         SM2[:, cc * 128:(cc + 1) * 128], axis=AX)
            V.tensor_scalar(DECAY[:, 0:8 * NPT], DECAY[:, 0:8 * NPT],
                            1.0 / CHUNK, None, OP.mult)
            V.wait(trs)
            for buf in (QR, KR):
                for f in range(DT):
                    tev = em.now("s")
                    em.wait(PE, tev)
                    for t2 in range(2):
                        dst = pa_mm if t2 == 0 else pa_mm2
                        PE.matmul(dst[:, 0:512], PMAT[:],
                                  buf[:, f * AB + t2 * 512:f * AB + (t2 + 1) * 512],
                                  start=True, stop=True)
                    tpe = PE.mark()
                    S.wait(tpe)
                    S.copy(XQN[:, 0:512], pa_mm[:, 0:512])
                    S.copy(XQN[:, 512:1024], pa_mm2[:, 0:512])
                    tsh = S.tok()
                    V.tensor_tensor(SM2[:, 0:AB], buf[:, f * AB:(f + 1) * AB], CREP[:], OP.mult)
                    V.wait(tsh)
                    V.tensor_tensor(SM2[:, AB:2 * AB], XQN[:, 0:AB], SREP[:], OP.mult)
                    V.tensor_tensor(buf[:, f * AB:(f + 1) * AB], SM2[:, 0:AB],
                                    SM2[:, AB:2 * AB], OP.add)
            tv = V.tok()
            em.wait(PE, tv)

    def phase_B(l, blk, pbstate):
        base = blk * AB
        Q5 = BIGB[0:64, 0:2048].rearrange("p (h q) -> p h q", h=16)
        K5 = BIGB[0:64, 2048:4096].rearrange("p (h q) -> p h q", h=16)
        TQKV = BIGF[0:64, 2048:4096]
        TQKV3 = TQKV.rearrange("p (h q) -> p q h", h=16)
        AVT = SM[0:64, 0:2048]
        AVT3 = AVT.rearrange("p (q h) -> p q h", h=16)
        XWT = SM2[0:64, 0:2048]
        XWT3 = XWT.rearrange("p (q h) -> p q h", h=16)
        V43 = VL4B[0:16, 0:8192].rearrange("p (e q) -> p e q", q=128)
        P1 = WF32[0:64, 0:128]
        M1 = WF32[0:64, 128:256]
        QSCS = WF32[0:64, 256:384]
        RDB = XQN[0:1, 0:1024]
        QROW = ROWS[0:1, 0:128]
        KV3 = KV[:].rearrange("r (p q) -> r p q", q=128)
        DC3 = DECAY[:].rearrange("r (p n) -> r p n", n=NPT)
        XQE3 = XQED.rearrange("p (q h) -> p h q", h=16)

        def hsl(r0, r1, h):
            return XQE3[r0:r1, h:h + 1, :].rearrange("p o q -> p (o q)")

        def lsl(v3, lq, n):
            return v3[:, :, lq:lq + 1].rearrange(f"p {n} o -> p ({n} o)")

        with pst("pb_kn", [128, 1024], BF16) as pb_kn, \
             pst("pb_sp", [128, 1024], F32) as pb_sp, \
             pst("pb_dn", [128, 1024], F32) as pb_dn, \
             pst("pb_av", [128, 512], F32) as pb_av, \
             pst("pb_p1", [128, 512], F32) as pb_p1, \
             pst("pb_p2", [128, 512], F32) as pb_p2:
            fence()
            tld0 = stream_panel(WAR[:, 0:4096], swp_st, DT, 0, 512, sem="dwA")
            tld1 = stream_panel(WAR[:, 4096:8192], swp_st, DT, 512, 512, sem="dwB")
            gfence()
            pbstate["q5war"] = em.now("pe")
            for c in range(NPT):
                t0 = base + c * 128
                if em.count.get("dbn", 0):
                    em.wait(nc.gpsimd, em.now("dbn"))
                for dstd, srcb in ((qtd, QR), (ktd, KR), (vtd, VTR)):
                    em.inc(nc.gpsimd.dma_start(
                        out=dstd.rearrange("t r q -> r t q"),
                        in_=srcb.rearrange("r (t q) -> r t q", t=DT)[:, :, c * 128:(c + 1) * 128]),
                        "dbg", 16)
                if B_STAGE >= 4:
                    em.wait(SP, pbstate.get("q5war"))
                    em.wait(SP, em.now("dbg"))
                    tQ5 = em.inc(SP.dma_start(
                        out=BIGB[0:64, 0:2048].rearrange("p (h q) -> p h q", h=16),
                        in_=qtd.rearrange("t (hp d) q -> d (t hp) q", hp=2)), "dbn", 16)
                    tK5 = em.inc(SP.dma_start(
                        out=BIGB[0:64, 2048:4096].rearrange("p (h q) -> p h q", h=16),
                        in_=ktd.rearrange("t (hp d) q -> d (t hp) q", hp=2)), "dbn", 16)
                    tV4 = em.inc(SP.dma_start(
                        out=VL4B[0:16, 0:8192],
                        in_=vtd.rearrange("t (gp e) q -> (t gp) (e q)", gp=2)), "dbn", 16)
                V.wait(em.now("pe"))
                V.tensor_copy(KVB[:], KV[:])
                tkvb = V.tok()
                tprev = em.now("s")
                em.wait(PE, tprev)
                em.wait(PE, em.now("v"))
                for f in range(DT):
                    PE.transpose(pb_kn[:, f * 128:(f + 1) * 128],
                                 KR[:, f * AB + c * 128:f * AB + (c + 1) * 128], IDENT[:])
                tpe = PE.mark()
                S.wait(tpe)
                S.copy(KNAT[:, 0:D], pb_kn[:, 0:D])
                tkn = S.tok()
                em.wait(PE, tkn)
                for f in range(DT):
                    PE.transpose(pb_kn[:, f * 128:(f + 1) * 128],
                                 VTR[:, f * AB + c * 128:f * AB + (c + 1) * 128], IDENT[:])
                tpe = PE.mark()
                S.wait(tpe)
                S.copy(VNC[:, 0:D], pb_kn[:, 0:D])
                tvn = S.tok()
                if B_STAGE < 2:
                    continue
                em.wait(PE, tkvb)
                TQ4 = TQKV.rearrange("p (hh par q) -> p par hh q", hh=8, par=2)
                for par in range(2):
                    ro = par * 64
                    dst = pb_p1 if par == 0 else pb_p2
                    for hfq in range(2):
                        tev = em.now("s")
                        em.wait(PE, tev)
                        for j in range(4):
                            ft = hfq * 4 + j
                            PE.matmul(dst[0:64, j * 128:(j + 1) * 128],
                                      KVB[ro:ro + 64, ft * 128 + ro:ft * 128 + ro + 64],
                                      QR[ro:ro + 64, ft * AB + c * 128:ft * AB + (c + 1) * 128],
                                      start=True, stop=True)
                        tpe = PE.mark()
                        S.wait(tpe)
                        S.copy(TQ4[:, par:par + 1, hfq * 4:hfq * 4 + 4, :]
                               .rearrange("p o h q -> p (o h) q"),
                               dst[0:64, 0:512].rearrange("p (h q) -> p h q", h=4))
                if B_STAGE < 3:
                    continue
                em.wait(PE, tvn)
                em.wait(PE, em.now("v"))
                for p in range(8):
                    PE.matmul(pb_sp[:, p * 128:(p + 1) * 128],
                              KNAT[:, p * 128:(p + 1) * 128],
                              VNC[:, p * 128:(p + 1) * 128],
                              start=True, stop=True)
                tpe = PE.mark()
                V.wait(tpe)
                V.wait(tkvb)
                V.tensor_tensor(KV3, KV3,
                                DC3[:, :, c:c + 1].broadcast_to([128, 8, 128]),
                                OP.mult)
                V.tensor_tensor(KV[:], KV[:], pb_sp[0:128, 0:1024], OP.add)
                tkvu = V.tok()
                if B_STAGE < 5:
                    continue
                em.wait(PE, tV4)
                for s in range(2):
                    u0 = 0
                    tev = em.now("s")
                    em.wait(PE, tev)
                    em.wait(PE, em.now("v"))
                    for lo in range(64):
                        lq = s * 64 + lo
                        PE.matmul(pb_sp[0:16, lo * 16:(lo + 1) * 16],
                                  lsl(K5, lq, "h"), lsl(Q5, lq, "h"),
                                  start=True, stop=True)
                    tpe = PE.mark()
                    S.wait(tpe)
                    S.activation(UPC[u0:u0 + 16, 0:1024], pb_sp[0:16, 0:1024],
                                 AF.Exp, scale=SCALE_ATT)
                    tex = S.tok()
                    em.wait(PE, tex)
                    PE.matmul(pb_dn[0:1, 0:512], ONESB[0:16, 0:1],
                              UPC[u0:u0 + 16, 0:512], start=True, stop=True)
                    PE.matmul(pb_dn[0:1, 512:1024], ONESB[0:16, 0:1],
                              UPC[u0:u0 + 16, 512:1024], start=True, stop=True)
                    tpe = PE.mark()
                    V.wait(tpe)
                    with nc.allow_low_precision(
                            reason="softmax weights are bf16 matmul operands; "
                                   "bf16 denominator recip matches that precision"):
                        V.reciprocal(RDB, pb_dn[0:1, 0:1024])
                    trc = V.tok()
                    em.wait(PE, trc)
                    PE.matmul(pb_dn[0:16, 0:512], ONESB[0:1, 0:1].broadcast_to([1, 16]),
                              RDB[0:1, 0:512], start=True, stop=True)
                    PE.matmul(pb_dn[0:16, 512:1024], ONESB[0:1, 0:1].broadcast_to([1, 16]),
                              RDB[0:1, 512:1024], start=True, stop=True)
                    tpe = PE.mark()
                    V.wait(tpe)
                    V.tensor_tensor(UPC[u0:u0 + 16, 0:1024], UPC[u0:u0 + 16, 0:1024],
                                    pb_dn[0:16, 0:1024], OP.mult)
                    tA = V.tok()
                    em.wait(PE, tA)
                    for lo in range(32):
                        lq = s * 64 + lo
                        PE.matmul(pb_p1[0:64, lo * 16:(lo + 1) * 16],
                                  lsl(V43, lq, "e"),
                                  UPC[u0:u0 + 16, lo * 16:(lo + 1) * 16],
                                  start=True, stop=True)
                    tpeA = PE.mark()
                    for lo in range(32, 64):
                        lq = s * 64 + lo
                        PE.matmul(pb_av[0:64, (lo - 32) * 16:(lo - 31) * 16],
                                  lsl(V43, lq, "e"),
                                  UPC[u0:u0 + 16, lo * 16:(lo + 1) * 16],
                                  start=True, stop=True)
                    tpeB = PE.mark()
                    V.wait(tpeA)
                    V.tensor_tensor(AVT3[:, s * 64:s * 64 + 32, :],
                                    pb_p1[0:64, 0:512].rearrange("p (q h) -> p q h", h=16),
                                    TQKV3[:, s * 64:s * 64 + 32, :], OP.add)
                    V.wait(tpeB)
                    V.tensor_tensor(AVT3[:, s * 64 + 32:s * 64 + 64, :],
                                    pb_av[0:64, 0:512].rearrange("p (q h) -> p q h", h=16),
                                    TQKV3[:, s * 64 + 32:s * 64 + 64, :], OP.add)
                    tavs = V.tok()
                    em.wait(PE, tavs)
                pbstate["q5war"] = em.now("pe")
                if B_STAGE < 6:
                    continue
                S.square(XWT, AVT)
                tsq = S.tok()
                V.wait(tsq)
                V.reduce_sum(P1, XWT3, axis=AX)
                V.tensor_tensor(XWT3, AVT3,
                                NWAT.rearrange("p (o h) -> p o h", o=1).broadcast_to([64, 128, 16]),
                                OP.mult)
                V.reduce_max(M1, XWT3, axis=AX, apply_absolute_value=True)
                tv2 = V.tok()
                em.wait(PE, tv2)
                PE.transpose(pb_sp[:, 0:64], P1, IDENTF[0:64, 0:64])
                PE.transpose(pb_sp[:, 64:128], M1, IDENTF[0:64, 0:64])
                tpe = PE.mark()
                V.wait(tpe)
                V.reduce_sum(stc(c_sum + c), pb_sp[:, 0:64], axis=AX)
                V.reduce_max(stc(c_m1 + c), pb_sp[:, 64:128], axis=AX)
                V.tensor_scalar(stc(c_t0 + c), stc(c_sum + c), 1.0 / D, EPS_RMS,
                                OP.mult, OP.add)
                V.reciprocal(stc(c_t1 + c), stc(c_t0 + c))
                tv = V.tok()
                S.wait(tv)
                S.sqrt(stc(c_rstd + c), stc(c_t1 + c))
                ts = S.tok()
                V.wait(ts)
                V.tensor_tensor(stc(c_xs + c), stc(c_rstd + c), stc(c_m1 + c), OP.mult)
                V.tensor_scalar(stc(c_xs + c), stc(c_xs + c), 1e-5, None, OP.max)
                V.reciprocal(stc(c_t0 + c), stc(c_xs + c))
                V.tensor_tensor(stc(c_t1 + c), stc(c_t0 + c), stc(c_rstd + c), OP.mult)
                V.tensor_scalar(stc(c_qsc + c), stc(c_t1 + c), 127.0, None, OP.mult)
                V.tensor_scalar(stc(c_xsa + c), stc(c_xs + c), 1.0 / 127.0, None, OP.mult)
                tv = V.tok()
                em.wait(PE, tv)
                PE.transpose(pb_sp[0:1, 128:256], stc(c_qsc + c), IDENTF[:])
                tpe = PE.mark()
                S.wait(tpe)
                for nm in ("dob", "drw"):
                    if em.count.get(nm, 0):
                        em.wait(nc.scalar, em.now(nm))
                S.copy(QROW, pb_sp[0:1, 128:256])
                tqr = S.tok()
                em.wait(PE, tqr)
                PE.matmul(pb_sp[0:64, 256:384], ONE1F[0:1, 0:1].broadcast_to([1, 64]),
                          QROW, start=True, stop=True)
                tpe = PE.mark()
                S.wait(tpe)
                S.copy(QSCS, pb_sp[0:64, 256:384])
                tqs = S.tok()
                V.wait(tqs)
                V.tensor_tensor(XWT3, XWT3,
                                QSCS.rearrange("p (q o) -> p q o", o=1).broadcast_to([64, 128, 16]),
                                OP.mult)
                tv = V.tok()
                V.wait(em.now("dxq"))
                V.wait(em.now("pe"))
                for hf in range(2):
                    S.wait(tv)
                    S.activation(XI16[0:64, 0:D], XWT[:, hf * D:(hf + 1) * D], AF.Copy)
                    ts = S.tok()
                    V.wait(ts)
                    V.tensor_copy(XQED[0:64, hf * D:(hf + 1) * D], XI16[0:64, 0:D])
                tqe = V.tok()
                if B_STAGE < 7:
                    continue
                em.wait(SP, tqe)
                tdup = em.inc(SP.dma_start(out=XQED[64:128, 0:2048],
                                           in_=XQED[0:64, 0:2048]), "dxq", 16)
                tv0 = em.now("v")
                em.wait(SP, tv0)
                tox = em.inc(SP.dma_start(out=XOLD[:, 0:D], in_=x_st[t0:t0 + 128, :]),
                             "dxo", 16)
                if c == 0:
                    em.wait(PE, tld0)
                    em.wait(PE, tld1)
                em.wait(PE, tqe)
                em.wait(PE, tdup)
                for fh in range(2):
                    tev = em.now("v")
                    em.wait(PE, tev)
                    for j in range(8):
                        PE.matmul(pb_p1[:, 0:512],
                                  hsl(0, 64, 2 * j),
                                  WAR[0:64, fh * 4096 + j * 512:fh * 4096 + (j + 1) * 512],
                                  start=(j == 0), stop=(j == 7))
                    for j in range(8):
                        PE.matmul(pb_p2[:, 0:512],
                                  hsl(64, 128, 2 * j + 1),
                                  WAR[64:128, fh * 4096 + j * 512:fh * 4096 + (j + 1) * 512],
                                  start=(j == 0), stop=(j == 7))
                    tpe = PE.mark()
                    S.wait(tpe)
                    S.copy(SM2[:, 0:512], pb_p2[:, 0:512])
                    ts2 = S.tok()
                    V.wait(ts2)
                    V.wait(tox)
                    V.tensor_tensor(SM2[:, 0:512], SM2[:, 0:512], pb_p1[:, 0:512], OP.add)
                    V.tensor_scalar(SM2[:, 512:1024], SM2[:, 0:512],
                                    stc(c_xsa + c), WSC[:, 2:3], OP.mult, OP.mult)
                    V.tensor_tensor(SM2[:, 512:1024], SM2[:, 512:1024],
                                    BIASF[:, fh * 512:(fh + 1) * 512], OP.add)
                    V.tensor_tensor(XNEW[:, fh * 512:(fh + 1) * 512], SM2[:, 512:1024],
                                    XOLD[:, fh * 512:(fh + 1) * 512], OP.add)
                tnew = V.tok()
                em.wait(SP, tnew)
                tst = em.inc(SP.dma_start(out=x_st[t0:t0 + 128, :], in_=XNEW[:, 0:D]),
                             "dxs", 16)
                rmsnorm_quant_chunk(XNEW[:, 0:D], NWB[:, D:2 * D], c, XQN[:, 0:D])
                V.wait(tst)
                transpose_chunk(XQN[:, 0:D], pb_kn,
                                lambda f, c=c: XQ2T[:, f * AB + c * 128:f * AB + (c + 1) * 128])

    def phase_C(l, blk):
        base = blk * AB
        with pst("pc_mm", [128, 512], F32) as pc_mm, \
             pst("pc_mm2", [128, 512], F32) as pc_mm2, \
             pst("pc_bc", [128, 512], F32) as pc_bc:
            fence()
            em.wait(SP, em.now("dxs"))
            trow = cols_to_row(stc(c_x127, NPT), NPT, ROWS[0:1, 0:AB])
            em.wait(PE, trow)
            bcast_row(ROWS, AB, XSB, pc_bc)
            txsb = em.now("s")
            GLUv = BIGB
            for pan in range(16):
                woff = (pan % 2) * 4096
                tpe_guard = em.now("pe")
                em.wait(SP, tpe_guard)
                tld = stream_panel(WAR[:, woff:woff + 4096], sw1_st, DT, pan * 512, 512,
                                   sem="dwA" if pan % 2 == 0 else "dwB")
                em.wait(PE, tld)
                for pr in range(2):
                    ja = pan * 4 + pr * 2
                    jg = ja + 1
                    hidx = pan * 2 + pr
                    for t2 in range(2):
                        tev = em.now("s")
                        em.wait(PE, tev)
                        for kt in range(DT):
                            PE.matmul(pc_mm[:, 0:512],
                                      WAR[:, woff + kt * 512 + (pr * 2) * 128:woff + kt * 512 + (pr * 2 + 1) * 128],
                                      XQ2T[:, kt * AB + t2 * 512:kt * AB + (t2 + 1) * 512],
                                      start=(kt == 0), stop=(kt == DT - 1))
                        for kt in range(DT):
                            PE.matmul(pc_mm2[:, 0:512],
                                      WAR[:, woff + kt * 512 + (pr * 2 + 1) * 128:woff + kt * 512 + (pr * 2 + 2) * 128],
                                      XQ2T[:, kt * AB + t2 * 512:kt * AB + (t2 + 1) * 512],
                                      start=(kt == 0), stop=(kt == DT - 1))
                        tpe = PE.mark()
                        V.wait(tpe)
                        V.wait(txsb)
                        V.scalar_tensor_tensor(SM[:, 0:512], pc_mm[:, 0:512], WSC[:, 3:4],
                                               XSB[:, t2 * 512:(t2 + 1) * 512],
                                               OP.mult, OP.mult)
                        V.scalar_tensor_tensor(SM[:, 512:1024], pc_mm2[:, 0:512], WSC[:, 3:4],
                                               XSB[:, t2 * 512:(t2 + 1) * 512],
                                               OP.mult, OP.mult)
                        tv = V.tok()
                        S.wait(tv)
                        S.activation(SM2[:, 0:512], SM[:, 0:512], AF.Identity,
                                     bias=BCOL[:, 32 + ja:33 + ja])
                        S.activation(SM2[:, 512:1024], SM[:, 512:1024], AF.Silu,
                                     bias=BCOL[:, 32 + jg:33 + jg])
                        ts = S.tok()
                        V.wait(ts)
                        V.tensor_tensor(GLUv[:, hidx * AB + t2 * 512:hidx * AB + (t2 + 1) * 512],
                                        SM2[:, 0:512], SM2[:, 512:1024], OP.mult)
            with pst("pc_q32", [128, 128], F32) as pc_q32:
                V.reduce_max(SM[:, 0:AB],
                             GLUv[:, 0:32 * AB].rearrange("p (j t) -> p t j", j=32),
                             axis=AX, apply_absolute_value=True)
                tvt = V.tok()
                em.wait(PE, tvt)
                for ci in range(NPT):
                    PE.transpose(pc_q32[:, 0:128],
                                 SM[:, ci * 128:(ci + 1) * 128], IDENTF[:])
                    tpq = PE.mark()
                    V.wait(tpq)
                    V.reduce_max(stc(c_xs + ci), pc_q32[:, 0:128], axis=AX)
                    tvq = V.tok()
                    em.wait(PE, tvq)
                V.tensor_scalar(stc(c_xs, NPT), stc(c_xs, NPT), 1e-5, None, OP.max)
                V.reciprocal(stc(c_t0, NPT), stc(c_xs, NPT))
                V.tensor_scalar(stc(c_qsc, NPT), stc(c_t0, NPT), 127.0, None, OP.mult)
                V.tensor_scalar(stc(c_x127, NPT), stc(c_xs, NPT), 1.0 / 127.0, None, OP.mult)
            trow2 = cols_to_row(stc(c_qsc, NPT), NPT, ROWS[0:1, 0:AB])
            em.wait(PE, trow2)
            bcast_row(ROWS, AB, XSB, pc_bc)
            tq = em.now("s")
            for j in range(32):
                if j == 0:
                    V.wait(tq)
                V.tensor_tensor(SM[:, 0:D], GLUv[:, j * AB:(j + 1) * AB], XSB[:], OP.mult)
                tvq = V.tok()
                S.wait(tvq)
                S.activation(XI16[:, 0:D], SM[:, 0:D], AF.Copy)
                ts = S.tok()
                V.wait(ts)
                V.tensor_copy(GLUv[:, j * AB:(j + 1) * AB], XI16[:, 0:D])
            for fhv in range(2):
                tpe_guard = em.now("pe")
                em.wait(SP, tpe_guard)
                em.wait(SP, em.now("dxs"))
                tlda = stream_panel(WAR[:, 0:8192], sw2_st, HID // 128, fhv * 512, 256,
                                    kw=256, sem="dwA")
                tldb = stream_panel(WAR[:, 8192:16384], sw2_st, HID // 128,
                                    fhv * 512 + 256, 256, kw=256, sem="dwB")
                em.wait(PE, tlda)
                em.wait(PE, tldb)
                for c in range(NPT):
                    t0 = base + c * 128
                    tv0 = em.now("v")
                    em.wait(SP, tv0)
                    tox = em.inc(SP.dma_start(out=XOLD[:, 0:D],
                                              in_=x_st[t0:t0 + 128, :]), "dxo", 16)
                    for q2 in range(2):
                        woff = q2 * 8192
                        fcol = fhv * 512 + q2 * 256
                        dst = pc_mm if q2 == 0 else pc_mm2
                        tev = em.now("v")
                        em.wait(PE, tev)
                        for kt in range(HID // 128):
                            PE.matmul(dst[:, 0:256],
                                      GLUv[:, kt * AB + c * 128:kt * AB + (c + 1) * 128],
                                      WAR[:, woff + kt * 256:woff + (kt + 1) * 256],
                                      start=(kt == 0), stop=(kt == HID // 128 - 1))
                        tpe = PE.mark()
                        V.wait(tpe)
                        V.wait(tox)
                        V.tensor_scalar(SM[:, 0:256], dst[:, 0:256],
                                        stc(c_x127 + c), WSC[:, 4:5], OP.mult, OP.mult)
                        V.tensor_tensor(SM[:, 0:256], SM[:, 0:256],
                                        BIASF[:, D + fcol:D + fcol + 256], OP.add)
                        V.tensor_tensor(XNEW[:, fcol:fcol + 256], SM[:, 0:256],
                                        XOLD[:, fcol:fcol + 256], OP.add)
                    tnew = V.tok()
                    em.wait(SP, tnew)
                    tst = em.inc(SP.dma_start(
                        out=x_st[t0:t0 + 128, fhv * 512:(fhv + 1) * 512],
                        in_=XNEW[:, fhv * 512:(fhv + 1) * 512]), "dxs", 16)
                    V.wait(tst)

    def ffn_consts(e, w1b, w2b, w3b):
        with pst("fb_c", [128, 32], F32) as fb_c, \
             pst("fb_bc", [128, 512], F32) as fb_bc:
            r1 = lambda ap: ap.rearrange("(a b) -> a b", a=1)

            def ld_row2(src, n):
                tp = em.now("pe")
                em.wait(SP, tp)
                for nm in ("dob", "drw"):
                    if em.count.get(nm, 0):
                        em.wait(SP, em.now(nm))
                tl = em.inc(SP.dma_start(out=ROWS[0:1, 0:n], in_=src), "drw", 16)
                em.wait(PE, tl)

            ld_row2(r1(w1b), FH)
            col_from_rows(ROWS, 0, FHT, BCF[:, 0:FHT], fb_c)
            ld_row2(r1(w2b), FH)
            col_from_rows(ROWS, 0, FHT, BCF[:, 16:16 + FHT], fb_c)
            ld_row2(r1(w3b), D)
            bcast_row(ROWS, D, BIASF[:, 0:D], fb_bc)

    def phase_FFN():
        XNT = BIGB[:, 0:DT * OWN]
        GLUH = BIGB[:, DT * OWN:DT * OWN + FHT * AB]
        with pst("ff_tr", [128, 1024], BF16) as ff_tr, \
             pst("ff_mm", [128, 512], F32) as ff_mm, \
             pst("ff_mm2", [128, 512], F32) as ff_mm2, \
             pst("ff_bc", [128, 512], F32) as ff_bc:
            fence()
            gfence()
            em.wait(SP, em.now("dxn"))
            V.memset(XNEW[:, 0:D], 0.0)
            tz = V.tok()
            em.wait(SP, tz)
            for gq in range(NOC):
                em.inc(SP.dma_start(out=acc_st[gq * 128:(gq + 1) * 128, :],
                                    in_=XNEW[:, 0:D]), "dacc", 16)
            em.wait(SP, em.now("dacc"))
            for gq in range(NOC):
                tprev_s = em.now("s")
                em.wait(nc.gpsimd, tprev_s)
                tg = em.inc(nc.gpsimd.indirect_dma_start(
                    XQN[:, 0:D], None, xn_st,
                    bass.IndirectOffsetOnAxis(IXC[:, gq:gq + 1], 0)), "dgx", 16)
                em.wait(PE, tg)
                transpose_chunk(XQN[:, 0:D], ff_tr,
                                lambda f, gq=gq: XNT[:, f * OWN + gq * 128:f * OWN + (gq + 1) * 128])
            gfence()
            tgw = None
            for kt in range(DT):
                tgw = em.inc(nc.gpsimd.dma_start(out=WAR[:, kt * FE:(kt + 1) * FE],
                                                 in_=fgw[kt * 128:(kt + 1) * 128, :]),
                             "dgw", 16)
            tp0 = em.now("pe")
            em.wait(SP, tp0)
            for nm in ("dob", "drw"):
                if em.count.get(nm, 0):
                    em.wait(SP, em.now(nm))
            tgb = em.inc(SP.dma_start(out=ROWS[0:1, 0:FE], in_=fgb), "drw", 16)
            em.wait(PE, tgw)
            em.wait(PE, tgb)
            for gq in range(NOC):
                tpv = em.now("v")
                em.wait(PE, tpv)
                for kt in range(DT):
                    PE.matmul(ff_mm[:, 0:FE],
                              XNT[:, kt * OWN + gq * 128:kt * OWN + (gq + 1) * 128],
                              WAR[:, kt * FE:(kt + 1) * FE],
                              start=(kt == 0), stop=False)
                PE.matmul(ff_mm[:, 0:FE], ONECF[0:1, 0:1].broadcast_to([1, 128]),
                          ROWS[0:1, 0:FE], start=False, stop=True)
                tpe = PE.mark()
                V.wait(tpe)
                V.reduce_max(stc(c_t0), ff_mm[:, 0:FE], axis=AX)
                V.tensor_scalar(stc(c_t1), stc(c_t0), -1.0, None, OP.mult)
                tv = V.tok()
                S.wait(tv)
                S.activation(SM[:, 0:FE], ff_mm[:, 0:FE], AF.Exp,
                             bias=stc(c_t1), accum_out=stc(c_den))
                ts = S.tok()
                V.wait(ts)
                V.reciprocal(stc(c_rden), stc(c_den))
                V.tensor_scalar(SM[:, 8:16], SM[:, 0:FE], stc(c_rden), None, OP.mult)
                V.reduce_max(stc(c_t0), SM[:, 8:16], axis=AX)
                V.tensor_scalar(SM[:, 16:24], SM[:, 8:16], stc(c_t0), None, OP.is_ge)
                V.tensor_scalar(SM[:, 24:32], SM[:, 16:24], -3.0, None, OP.mult)
                V.tensor_tensor(SM[:, 24:32], SM[:, 8:16], SM[:, 24:32], OP.add)
                V.reduce_max(stc(c_t1), SM[:, 24:32], axis=AX)
                V.tensor_scalar(SM[:, 16:24], SM[:, 8:16], stc(c_t1), None, OP.is_ge)
                V.tensor_tensor(WEB[:, gq * 8:(gq + 1) * 8], SM[:, 8:16],
                                SM[:, 16:24], OP.mult)
            for e in range(FE + FS):
                routed = e < FE
                w1 = frw1[e] if routed else fsw1[e - FE]
                w2 = frw2[e] if routed else fsw2[e - FE]
                w3 = frw3[e] if routed else fsw3[e - FE]
                b1 = frb1[e] if routed else fsb1[e - FE]
                b2 = frb2[e] if routed else fsb2[e - FE]
                b3 = frb3[e] if routed else fsb3[e - FE]
                em.wait(SP, em.now("dacc"))
                ffn_consts(e, b1, b2, b3)
                for th in range(max(OWN // AB, 1)):
                    tw0 = th * AB
                    twn = min(AB, OWN)
                    for pan in range(FH // 512):
                        woff = (pan % 2) * 4096
                        tpe_guard = em.now("pe")
                        em.wait(nc.gpsimd, tpe_guard)
                        tld = None
                        for kt in range(DT):
                            tld = em.inc(nc.gpsimd.dma_start(
                                out=WAR[:, woff + kt * 512:woff + (kt + 1) * 512],
                                in_=w1[kt * 128:(kt + 1) * 128, pan * 512:(pan + 1) * 512]),
                                "dgw", 16)
                        for kt in range(DT):
                            tld = em.inc(nc.gpsimd.dma_start(
                                out=WAR[:, 8192 + woff + kt * 512:8192 + woff + (kt + 1) * 512],
                                in_=w2[kt * 128:(kt + 1) * 128, pan * 512:(pan + 1) * 512]),
                                "dgw", 16)
                        em.wait(PE, tld)
                        for fb in range(4):
                            hglob = pan * 4 + fb
                            for t2 in range(twn // 512):
                                tcol = tw0 + t2 * 512
                                tev = em.now("v")
                                em.wait(PE, tev)
                                for kt in range(DT):
                                    PE.matmul(ff_mm[:, 0:512],
                                              WAR[:, woff + kt * 512 + fb * 128:woff + kt * 512 + (fb + 1) * 128],
                                              XNT[:, kt * OWN + tcol:kt * OWN + tcol + 512],
                                              start=(kt == 0), stop=(kt == DT - 1))
                                for kt in range(DT):
                                    PE.matmul(ff_mm2[:, 0:512],
                                              WAR[:, 8192 + woff + kt * 512 + fb * 128:8192 + woff + kt * 512 + (fb + 1) * 128],
                                              XNT[:, kt * OWN + tcol:kt * OWN + tcol + 512],
                                              start=(kt == 0), stop=(kt == DT - 1))
                                tpe = PE.mark()
                                S.wait(tpe)
                                S.activation(XQN[:, 0:512], ff_mm[:, 0:512],
                                             AF.Silu, bias=BCF[:, hglob:hglob + 1])
                                ts = S.tok()
                                V.wait(ts)
                                V.scalar_tensor_tensor(
                                    GLUH[:, hglob * AB + t2 * 512:hglob * AB + (t2 + 1) * 512],
                                    ff_mm2[:, 0:512], BCF[:, 16 + hglob:17 + hglob],
                                    XQN[:, 0:512], OP.add, OP.mult)
                    tpe_guard = em.now("pe")
                    em.wait(nc.gpsimd, tpe_guard)
                    tld = None
                    for fh2 in range(2):
                        for kt in range(FHT):
                            tld = em.inc(nc.gpsimd.dma_start(
                                out=WAR[:, fh2 * 8192 + kt * 512:fh2 * 8192 + (kt + 1) * 512],
                                in_=w3[kt * 128:(kt + 1) * 128, fh2 * 512:(fh2 + 1) * 512]),
                                "dgw", 16)
                    em.wait(PE, tld)
                    for cc in range(twn // 128):
                        gq = (tw0 + cc * 128) // 128
                        tv0 = em.now("v")
                        em.wait(SP, tv0)
                        tac = em.inc(SP.dma_start(out=XOLD[:, 0:D],
                                                  in_=acc_st[gq * 128:(gq + 1) * 128, :]),
                                     "dxo", 16)
                        for fh2 in range(2):
                            dst = ff_mm if fh2 == 0 else ff_mm2
                            tev = em.now("v")
                            em.wait(PE, tev)
                            for kt in range(FHT):
                                PE.matmul(dst[:, 0:512],
                                          GLUH[:, kt * AB + cc * 128:kt * AB + (cc + 1) * 128],
                                          WAR[:, fh2 * 8192 + kt * 512:fh2 * 8192 + (kt + 1) * 512],
                                          start=(kt == 0), stop=(kt == FHT - 1))
                            tpe = PE.mark()
                            V.wait(tpe)
                            V.wait(tac)
                            V.tensor_tensor(SM[:, 0:512], dst[:, 0:512],
                                            BIASF[:, fh2 * 512:(fh2 + 1) * 512], OP.add)
                            if routed:
                                V.tensor_scalar(SM[:, 512:1024], SM[:, 0:512],
                                                WEB[:, gq * 8 + e:gq * 8 + e + 1],
                                                None, OP.mult)
                            else:
                                V.tensor_scalar(SM[:, 512:1024], SM[:, 0:512],
                                                0.5, None, OP.mult)
                            V.tensor_tensor(XNEW[:, fh2 * 512:(fh2 + 1) * 512],
                                            SM[:, 512:1024],
                                            XOLD[:, fh2 * 512:(fh2 + 1) * 512], OP.add)
                        tnew = V.tok()
                        em.wait(SP, tnew)
                        tst = em.inc(SP.dma_start(out=acc_st[gq * 128:(gq + 1) * 128, :],
                                                  in_=XNEW[:, 0:D]), "dacc", 16)
                        V.wait(tst)

    pbstate = {}
    for l in range(LAYERS):
        if not (set("ABC") & set(phases)):
            break
        prep_weights(l)
        layer_consts(l)
        V.memset(KV[:], 0.0)
        for blk in range(NBLK):
            if "A" in phases:
                phase_A(l, blk)
            if "B" in phases:
                phase_B(l, blk, pbstate)
            if "C" in phases:
                phase_C(l, blk)

    if "F" in phases:
        phase_FFN()

    fence()
    gfence()
    em.wait(SP, em.now("dacc"))
    em.wait(SP, em.now("dxs"))
    for gq in range(NOC):
        tvp = em.now("v")
        em.wait(nc.gpsimd, tvp)
        em.wait(SP, tvp)
        tg1 = em.inc(nc.gpsimd.indirect_dma_start(
            SM[:, 0:D], None, x_st,
            bass.IndirectOffsetOnAxis(IXC[:, gq:gq + 1], 0)), "dgx", 16)
        tg2 = em.inc(nc.gpsimd.indirect_dma_start(
            SM2[:, 0:D], None, x0,
            bass.IndirectOffsetOnAxis(IXC[:, gq:gq + 1], 0)), "dgx", 16)
        ta = em.inc(SP.dma_start(out=XOLD[:, 0:D],
                                 in_=acc_st[gq * 128:(gq + 1) * 128, :]), "dxo", 16)
        V.wait(tg2)
        V.wait(ta)
        V.tensor_tensor(XNEW[:, 0:D], SM[:, 0:D], SM2[:, 0:D], OP.add)
        V.tensor_tensor(XNEW[:, 0:D], XNEW[:, 0:D], XOLD[:, 0:D], OP.add)
        tv = V.tok()
        em.wait(SP, tv)
        tw = em.inc(SP.dma_start(out=out[gq * 128:(gq + 1) * 128, :], in_=XNEW[:, 0:D]),
                    "dout", 16)
        V.wait(tw)
    em.wait(nc.gpsimd, em.now("dout"))

    ctx.close()
    return nc



def _rope_tables(n):
    inv = (1.0 / np.power(np.float32(10000.0),
                          (np.arange(0, HD, 2, dtype=np.float32) / np.float32(HD)).astype(np.float32))
           ).astype(np.float32)
    f = (np.arange(n, dtype=np.float32)[:, None] * inv[None, :]).astype(np.float32)
    cos = np.cos(f).astype(np.float32)
    sin = np.sin(f).astype(np.float32)
    rows = (np.arange(128) % HD) // 2
    crep = cos.T[rows, :]
    srep = sin.T[rows, :]
    return np.ascontiguousarray(crep), np.ascontiguousarray(srep)


def _pmat():
    P = np.zeros((128, 128), np.float32)
    for i in range(64):
        P[2 * i, 2 * i + 1] = -1.0
        P[2 * i + 1, 2 * i] = 1.0
    return np.ascontiguousarray(P.T).astype(ml_dtypes.bfloat16)


def _interleave_fc1_cols(wT):
    a = wT[:, 0:HID].reshape(wT.shape[0], HID // 128, 128)
    g = wT[:, HID:2 * HID].reshape(wT.shape[0], HID // 128, 128)
    inter = np.stack([a, g], axis=2)
    return np.ascontiguousarray(inter.reshape(wT.shape[0], 2 * HID))


def _interleave_fc1_bias(b):
    a = b[0:HID].reshape(HID // 128, 128)
    g = b[HID:2 * HID].reshape(HID // 128, 128)
    return np.ascontiguousarray(np.stack([a, g], axis=1).reshape(2 * HID))


def stage_inputs(inputs, NTOK, LAYERS, OWN, n_cores=8):
    f32 = np.float32
    L = LAYERS
    qkv_w = np.asarray(inputs["L0_attn_qkv_w"], f32)[:L]
    gate_w = np.asarray(inputs["L0_attn_gate_w"], f32)[:L]
    proj_w = np.asarray(inputs["L0_attn_proj_w"], f32)[:L]
    fc1_w = np.asarray(inputs["L0_attn_fc1_w"], f32)[:L]
    fc2_w = np.asarray(inputs["L0_attn_fc2_w"], f32)[:L]
    com = {
        "wqkvT": np.ascontiguousarray(qkv_w.transpose(0, 2, 1)),
        "wgateT": np.ascontiguousarray(gate_w.transpose(0, 2, 1)),
        "wprojT": np.ascontiguousarray(proj_w.transpose(0, 2, 1)),
        "wfc1T": np.ascontiguousarray(
            np.stack([_interleave_fc1_cols(np.ascontiguousarray(w.T)) for w in fc1_w])),
        "wfc2T": np.ascontiguousarray(fc2_w.transpose(0, 2, 1)),
        "bqkv": np.asarray(inputs["L0_attn_qkv_b"], f32)[:L],
        "bgate": np.asarray(inputs["L0_attn_gate_b"], f32)[:L],
        "bproj": np.asarray(inputs["L0_attn_proj_b"], f32)[:L],
        "bfc1": np.stack([_interleave_fc1_bias(np.asarray(b, f32))
                          for b in np.asarray(inputs["L0_attn_fc1_b"], f32)[:L]]),
        "bfc2": np.asarray(inputs["L0_attn_fc2_b"], f32)[:L],
        "norm1w": np.asarray(inputs["L0_attn_norm1_w"], f32)[:L],
        "norm2w": np.asarray(inputs["L0_attn_norm2_w"], f32)[:L],
        "anormwT": np.ascontiguousarray(
            np.asarray(inputs["L0_attn_attn_norm_w"], f32)[:L]
            .reshape(L, HEADS, HD).transpose(0, 2, 1)),
        "identb": np.eye(128, dtype=ml_dtypes.bfloat16),
        "identf": np.eye(128, dtype=np.float32),
        "pmatb": _pmat(),
        "fgw": np.asarray(inputs["L0_ffn_gate_w"], f32),
        "fgb": np.asarray(inputs["L0_ffn_gate_b"], f32).reshape(1, FE),
        "frw1": np.asarray(inputs["L0_ffn_rw1"], f32),
        "frb1": np.asarray(inputs["L0_ffn_rb1"], f32),
        "frw2": np.asarray(inputs["L0_ffn_rw2"], f32),
        "frb2": np.asarray(inputs["L0_ffn_rb2"], f32),
        "frw3": np.asarray(inputs["L0_ffn_rw3"], f32),
        "frb3": np.asarray(inputs["L0_ffn_rb3"], f32),
        "fsw1": np.asarray(inputs["L0_ffn_sw1"], f32),
        "fsb1": np.asarray(inputs["L0_ffn_sb1"], f32),
        "fsw2": np.asarray(inputs["L0_ffn_sw2"], f32),
        "fsb2": np.asarray(inputs["L0_ffn_sb2"], f32),
        "fsw3": np.asarray(inputs["L0_ffn_sw3"], f32),
        "fsb3": np.asarray(inputs["L0_ffn_sb3"], f32),
    }
    crep_, srep_ = _rope_tables(NTOK)
    com["crep"] = crep_
    com["srep"] = srep_
    x = np.asarray(inputs["L0_attn_x"], f32)
    in_maps = []
    for c in range(n_cores):
        b = c // 4
        q = c % 4
        idx = (np.arange(128, dtype=np.int32)[:, None]
               + (q * OWN + np.arange(OWN // 128, dtype=np.int32)[None, :] * 128))
        m = dict(com)
        m["x0"] = np.ascontiguousarray(x[b, 0:NTOK, :])
        m["ownidx"] = np.ascontiguousarray(idx)
        in_maps.append(m)
    return in_maps


def kernel(**inputs):
    x = np.asarray(inputs["L0_attn_x"])
    B, NTOK, _ = x.shape
    LAYERS = np.asarray(inputs["L0_attn_norm1_w"]).shape[0]
    OWN = NTOK // 4
    in_maps = stage_inputs(inputs, NTOK, LAYERS, OWN, n_cores=8)
    nc = build_program(NTOK, LAYERS, OWN)
    res = run_bass_kernel_spmd(nc, in_maps, core_ids=list(range(8)))
    outp = np.empty((B, NTOK, D), np.float32)
    for c in range(8):
        b = c // 4
        q = c % 4
        outp[b, q * OWN:(q + 1) * OWN, :] = res.results[c]["out"]
    return outp


# version 04
# speedup vs baseline: 1.3040x; 1.2174x over previous
import numpy as np
import ml_dtypes
from contextlib import ExitStack

import concourse.bass as bass
import concourse.mybir as mybir
from concourse.bass_utils import run_bass_kernel_spmd

F32 = mybir.dt.float32
BF16 = mybir.dt.bfloat16
I16 = mybir.dt.int16
I32 = mybir.dt.int32
I8 = mybir.dt.int8
AF = mybir.ActivationFunctionType
OP = mybir.AluOpType
AX = mybir.AxisListType.X

D = 1024
DT = D // 128
HEADS = 16
HD = 64
CHUNK = 128
HID = 4096
FE, FS, FH = 8, 2, 2048
FHT = FH // 128
EPS_RMS = 1e-6
EPS_LN = 1e-5
SCALE_ATT = HD ** -0.5
AB = 1024
NPT = AB // 128


class Emit:
    def __init__(self, nc):
        self.nc = nc
        self.sems = {}
        self.count = {}
        self.last_wait = {}

    def sem(self, name):
        if name not in self.sems:
            self.sems[name] = self.nc.alloc_semaphore(name)
            self.count[name] = 0
        return self.sems[name]

    def inc(self, instr, name, n=1):
        instr.then_inc(self.sem(name), n)
        self.count[name] += n
        return (name, self.count[name])

    def wait(self, eng, tok):
        if tok is None:
            return
        name, val = tok
        if val <= 0:
            return
        key = (id(eng), name)
        if self.last_wait.get(key, 0) >= val:
            return
        eng.wait_ge(self.sem(name), val)
        self.last_wait[key] = val

    def now(self, name):
        return (name, self.count.get(name, 0))


class Chained:

    def __init__(self, em, eng, name, buddies=()):
        object.__setattr__(self, "em", em)
        object.__setattr__(self, "eng", eng)
        object.__setattr__(self, "name", name)
        object.__setattr__(self, "buddies", buddies)

    def __getattr__(self, attr):
        f = getattr(self.eng, attr)

        def call(*a, **k):
            em, eng, name = self.em, self.eng, self.name
            em.wait(eng, em.now(name))
            if not getattr(em, "relax", False):
                for b in self.buddies:
                    em.wait(eng, em.now(b))
            return em.inc(f(*a, **k), name, 1)

        return call

    def wait(self, tok):
        self.em.wait(self.eng, tok)

    def tok(self):
        return self.em.now(self.name)


B_STAGE = 99


def build_program(NTOK, LAYERS, OWN, phases="ABCF"):
    NBLK = NTOK // AB
    NOC = OWN // CHUNK
    nc = bass.Bass("TRN2", target_bir_lowering=False, debug=False)
    nc.num_devices = 8

    dr = lambda n, s, dt, k="ExternalInput": nc.dram_tensor(n, s, dt, kind=k).ap()
    x0 = dr("x0", [NTOK, D], F32)
    wqkvT = dr("wqkvT", [LAYERS, D, 3 * D], F32)
    wgateT = dr("wgateT", [LAYERS, D, D], F32)
    wprojT = dr("wprojT", [LAYERS, D, D], F32)
    wfc1T = dr("wfc1T", [LAYERS, D, 2 * HID], F32)
    wfc2T = dr("wfc2T", [LAYERS, HID, D], F32)
    bqkv = dr("bqkv", [LAYERS, 3 * D], F32)
    bgate = dr("bgate", [LAYERS, D], F32)
    bproj = dr("bproj", [LAYERS, D], F32)
    bfc1 = dr("bfc1", [LAYERS, 2 * HID], F32)
    bfc2 = dr("bfc2", [LAYERS, D], F32)
    norm1w = dr("norm1w", [LAYERS, D], F32)
    norm2w = dr("norm2w", [LAYERS, D], F32)
    anormwT = dr("anormwT", [LAYERS, 64, 16], F32)
    crep = dr("crep", [128, NTOK], F32)
    srep = dr("srep", [128, NTOK], F32)
    identb = dr("identb", [128, 128], BF16)
    identf = dr("identf", [128, 128], F32)
    pmatb = dr("pmatb", [128, 128], BF16)
    fgw = dr("fgw", [D, FE], F32)
    fgb = dr("fgb", [1, FE], F32)
    frw1 = dr("frw1", [FE, D, FH], F32)
    frb1 = dr("frb1", [FE, FH], F32)
    frw2 = dr("frw2", [FE, D, FH], F32)
    frb2 = dr("frb2", [FE, FH], F32)
    frw3 = dr("frw3", [FE, FH, D], F32)
    frb3 = dr("frb3", [FE, D], F32)
    fsw1 = dr("fsw1", [FS, D, FH], F32)
    fsb1 = dr("fsb1", [FS, FH], F32)
    fsw2 = dr("fsw2", [FS, D, FH], F32)
    fsb2 = dr("fsb2", [FS, FH], F32)
    fsw3 = dr("fsw3", [FS, FH, D], F32)
    fsb3 = dr("fsb3", [FS, D], F32)
    ownidx = dr("ownidx", [128, NOC], I32)
    out = dr("out", [OWN, D], F32, "ExternalOutput")

    x_st = dr("x_st", [NTOK, D], F32, "Internal")
    xn_st = dr("xn_st", [NTOK, D], BF16, "Internal")
    acc_st = dr("acc_st", [OWN, D], F32, "Internal")
    swq_st = dr("swq_st", [D, 3 * D], BF16, "Internal")
    swg_st = dr("swg_st", [D, D], BF16, "Internal")
    swp_st = dr("swp_st", [D, D], BF16, "Internal")
    sw1_st = dr("sw1_st", [D, 2 * HID], BF16, "Internal")
    sw2_st = dr("sw2_st", [HID, D], BF16, "Internal")
    row_st = dr("row_st", [1, 2048], F32, "Internal")
    qtd = dr("qtd", [DT, 128, 128], BF16, "Internal")
    ktd = dr("ktd", [DT, 128, 128], BF16, "Internal")
    vtd = dr("vtd", [DT, 128, 128], BF16, "Internal")

    em = Emit(nc)
    ctx = ExitStack()
    sb = lambda name, shape, dt: ctx.enter_context(nc.sbuf_tensor(name, shape, dt))
    _uid = [0]

    def pst(nm, shape, dt):
        _uid[0] += 1
        return nc.psum_tensor(f"{nm}_{_uid[0]}", shape, dt)

    def fence():
        for nm in ("pe", "v", "s", "g"):
            em.wait(nc.sync, em.now(nm))

    def gfence():
        for nm in ("pe", "v", "s"):
            em.wait(nc.gpsimd, em.now(nm))

    IDENT = sb("IDENT", [128, 128], BF16)
    IDENTF = sb("IDENTF", [128, 128], F32)
    PMAT = sb("PMAT", [128, 128], BF16)
    ONECF = sb("ONECF", [128, 1], F32)
    ONE1F = sb("ONE1F", [1, 1], F32)
    ONESB = sb("ONESB", [128, 1], BF16)
    NWB = sb("NWB", [128, 2 * D], F32)
    NWAT = sb("NWAT", [64, 16], F32)
    BIASF = sb("BIASF", [128, 2 * D], F32)
    BCOL = sb("BCOL", [128, 96], F32)
    BCF = sb("BCF", [128, 48], F32)
    WSC = sb("WSC", [128, 8], F32)
    KV = sb("KV", [128, 8 * 128], F32)
    KVB = sb("KVB", [128, 8 * 128], BF16)
    DECAY = sb("DECAY", [128, 8 * NPT], F32)
    ST = sb("ST", [128, 176], F32)
    ROWS = sb("ROWS", [1, 2048], F32)
    XSB = sb("XSB", [128, AB], F32)
    SM = sb("SM", [128, 2 * D], F32)
    SM2 = sb("SM2", [128, 2 * D], F32)
    XI16 = sb("XI16", [128, D], I16)
    XQN = sb("XQN", [128, D], BF16)
    KNAT = sb("KNAT", [128, D], BF16)
    VNC = sb("VNC", [128, D], BF16)
    XOLD = sb("XOLD", [128, D], F32)
    XNEW = sb("XNEW", [128, D], F32)
    IXC = sb("IXC", [128, NOC], I32)
    WEB = sb("WEB", [128, 128], F32)
    BIGB = sb("BIGB", [128, 32768], BF16)
    WAR = sb("WAR", [128, 16384], BF16)
    WF32 = sb("WF32", [128, D], F32)
    VL4B = sb("VL4B", [128, 8192], BF16)
    UPC = sb("UPC", [128, 1024], BF16)
    XQED = sb("XQED", [128, 2048], BF16)

    XQT = BIGB[:, 0:8192]
    QR = BIGB[:, 8192:16384]
    KR = BIGB[:, 16384:24576]
    VTR = BIGB[:, 24576:32768]
    XQ2T = WAR[:, 8192:16384]
    BIGF = BIGB.bitcast(F32)
    WARF = WAR.bitcast(F32)
    CREP = WARF[:, 4096:5120]
    SREP = WARF[:, 5120:6144]

    SP = nc.sync

    class _PE:
        def __init__(self):
            self.last = None

        def matmul(self, *a, **k):
            self.last = nc.tensor.matmul(*a, **k)
            return self.last

        def transpose(self, *a, **k):
            self.last = nc.tensor.transpose(*a, **k)
            return self.last

        def wait_ge(self, sem, val):
            return nc.tensor.wait_ge(sem, val)

        def mark(self):
            return em.inc(self.last, "pe")

    PE = _PE()
    V = Chained(em, nc.vector, "v", buddies=("s",))
    S = Chained(em, nc.scalar, "s", buddies=("v",))
    GC = Chained(em, nc.gpsimd, "g")

    from contextlib import contextmanager

    @contextmanager
    def relaxed():
        em.relax = True
        try:
            yield
        finally:
            em.relax = False

    class Cols:
        def __init__(self):
            self.n = 0

        def take(self, k):
            a = self.n
            self.n += k
            assert self.n <= 176, self.n
            return a

    CL = Cols()
    c_sum = CL.take(NPT)
    c_msq = CL.take(NPT)
    c_rstd = CL.take(NPT)
    c_m1 = CL.take(NPT)
    c_xs = CL.take(NPT)
    c_qsc = CL.take(NPT)
    c_x127 = CL.take(NPT)
    c_t0 = CL.take(NPT)
    c_t1 = CL.take(NPT)
    c_xsa = CL.take(NPT)
    c_amax = CL.take(4)
    c_abias = CL.take(4)
    c_den = CL.take(4)
    c_rden = CL.take(4)
    c_wacc = CL.take(1)
    c_wtmp = CL.take(1)
    c_ln = CL.take(5 * NPT)

    def stc(c0, k=1):
        return ST[:, c0:c0 + k]

    def bcast_row(row_ap, n, dst_ap, ps_bc):
        for j in range(0, n, 512):
            w = min(512, n - j)
            PE.matmul(ps_bc[:, j % 512:j % 512 + w] if False else ps_bc[:, 0:w],
                      ONECF[0:1, 0:1].broadcast_to([1, 128]),
                      row_ap[0:1, j:j + w], start=True, stop=True)
            tpe = PE.mark()
            S.wait(tpe)
            S.copy(dst_ap[:, j:j + w], ps_bc[:, 0:w])
            ts = S.tok()
            em.wait(PE, ts)
        return S.tok()

    def col_from_rows(row_ap, start, ncols, dst_cols, ps_c):
        for j in range(ncols):
            PE.matmul(ps_c[:, j:j + 1],
                      row_ap[0:1, start + j * 128:start + (j + 1) * 128],
                      ONE1F[0:1, 0:1], start=True, stop=True)
        tpe = PE.mark()
        V.wait(tpe)
        V.tensor_copy(dst_cols, ps_c[:, 0:ncols])
        tv = V.tok()
        em.wait(PE, tv)
        return tv

    def cols_to_row(col_block, ncols, dst_row):
        em.wait(SP, em.now("v"))
        for nm in ("dob", "drw"):
            if em.count.get(nm, 0):
                em.wait(SP, em.now(nm))
        t1 = em.inc(SP.dma_start(
            out=row_st.rearrange("a b -> (a b)")[0:128 * ncols].rearrange(
                "(c p) -> p c", p=128),
            in_=col_block, allow_slow_non_contiguous=True), "dob", 16)
        em.wait(SP, t1)
        t2 = em.inc(SP.dma_start(out=dst_row, in_=row_st[0:1, 0:128 * ncols]), "dob", 16)
        return t2

    def row_to_cols(src_row, ncols, dst_cols):
        em.wait(SP, em.now("v"))
        for nm in ("dob", "drw"):
            if em.count.get(nm, 0):
                em.wait(SP, em.now(nm))
        t1 = em.inc(SP.dma_start(out=row_st[0:1, 0:128 * ncols], in_=src_row), "dob", 16)
        em.wait(SP, t1)
        t2 = em.inc(SP.dma_start(
            out=dst_cols,
            in_=row_st.rearrange("a b -> (a b)")[0:128 * ncols].rearrange(
                "(c p) -> p c", p=128), allow_slow_non_contiguous=True), "dob", 16)
        return t2

    def rmsnorm_quant_chunk(xin_f32, nwb_ap, ci, out_bf):
        xw = SM2[:, 0:D]
        V.tensor_tensor(xw, xin_f32, nwb_ap, OP.mult)
        tv = V.tok()
        S.wait(tv)
        S.square(SM2[:, D:2 * D], xin_f32)
        tsq = S.tok()
        V.wait(tsq)
        V.reduce_sum(stc(c_sum + ci), SM2[:, D:2 * D], axis=AX)
        V.reduce_max(stc(c_m1 + ci), xw, axis=AX, apply_absolute_value=True)
        V.tensor_scalar(stc(c_t0 + ci), stc(c_sum + ci), 1.0 / D, EPS_RMS, OP.mult, OP.add)
        V.reciprocal(stc(c_t1 + ci), stc(c_t0 + ci))
        tv = V.tok()
        S.wait(tv)
        S.sqrt(stc(c_rstd + ci), stc(c_t1 + ci))
        ts = S.tok()
        V.wait(ts)
        V.tensor_tensor(stc(c_xs + ci), stc(c_rstd + ci), stc(c_m1 + ci), OP.mult)
        V.tensor_scalar(stc(c_xs + ci), stc(c_xs + ci), 1e-5, None, OP.max)
        V.reciprocal(stc(c_t0 + ci), stc(c_xs + ci))
        V.tensor_tensor(stc(c_t1 + ci), stc(c_t0 + ci), stc(c_rstd + ci), OP.mult)
        V.tensor_scalar(stc(c_qsc + ci), stc(c_t1 + ci), 127.0, None, OP.mult)
        V.tensor_scalar(stc(c_x127 + ci), stc(c_xs + ci), 1.0 / 127.0, None, OP.mult)
        tv = V.tok()
        S.wait(tv)
        S.activation(XI16[:, 0:D], xw, AF.Copy, scale=stc(c_qsc + ci))
        ts = S.tok()
        V.wait(ts)
        V.wait(em.now("pe"))
        V.tensor_copy(out_bf, XI16[:, 0:D])
        return V.tok()

    def transpose_chunk(src_bf, ps_tr, col_map):
        tprev = em.now("s")
        em.wait(PE, tprev)
        em.wait(PE, em.now("v"))
        for f in range(DT):
            PE.transpose(ps_tr[:, f * 128:(f + 1) * 128],
                         src_bf[:, f * 128:(f + 1) * 128], IDENT[:])
        tpe = PE.mark()
        S.wait(tpe)
        for f in range(DT):
            S.copy(col_map(f), ps_tr[:, f * 128:(f + 1) * 128])
        return S.tok()

    tc = None
    for dst, src in ((IDENT[:], identb), (IDENTF[:], identf), (PMAT[:], pmatb), (IXC[:], ownidx)):
        tc = em.inc(SP.dma_start(out=dst, in_=src), "dcst", 16)
    V.wait(tc)
    V.memset(ONECF[:], 1.0)
    V.memset(ONE1F[:], 1.0)
    V.memset(ONESB[:], 1.0)

    for blk in range(NBLK):
        fence()
        ld = []
        for c in range(NPT):
            t0 = blk * AB + c * 128
            ld.append(em.inc(SP.dma_start(out=BIGF[:, c * D:(c + 1) * D],
                                          in_=x0[t0:t0 + 128, :]), "dpre", 16))
        for c in range(NPT):
            xc = BIGF[:, c * D:(c + 1) * D]
            V.wait(ld[-1])
            V.reduce_sum(stc(c_ln + c), xc, axis=AX)
            tv = V.tok()
            S.wait(tv)
            S.square(SM2[:, 0:D], xc)
            ts = S.tok()
            V.wait(ts)
            V.reduce_sum(stc(c_ln + NPT + c), SM2[:, 0:D], axis=AX)
        cM = stc(c_ln + 2 * NPT, NPT)
        cV_ = stc(c_ln + 3 * NPT, NPT)
        cR = stc(c_ln + 4 * NPT, NPT)
        V.tensor_scalar(cM, stc(c_ln, NPT), 1.0 / D, None, OP.mult)
        V.tensor_tensor(cV_, cM, cM, OP.mult)
        V.tensor_scalar(stc(c_ln + NPT, NPT), stc(c_ln + NPT, NPT), 1.0 / D, None, OP.mult)
        V.tensor_tensor(cV_, stc(c_ln + NPT, NPT), cV_, OP.subtract)
        V.tensor_scalar(cV_, cV_, EPS_LN, None, OP.add)
        V.reciprocal(cV_, cV_)
        tv = V.tok()
        S.wait(tv)
        S.sqrt(cR, cV_)
        ts = S.tok()
        V.wait(ts)
        V.tensor_tensor(cM, cM, cR, OP.mult)
        V.tensor_scalar(cM, cM, -1.0, None, OP.mult)
        for c in range(NPT):
            t0 = blk * AB + c * 128
            xc = BIGF[:, c * D:(c + 1) * D]
            V.tensor_scalar(SM2[:, 0:D], xc, cR[:, c:c + 1], cM[:, c:c + 1],
                            OP.mult, OP.add)
            ta = V.tok()
            em.wait(SP, ta)
            ts1 = em.inc(SP.dma_start(out=x_st[t0:t0 + 128, :], in_=SM2[:, 0:D]), "dxs", 16)
            V.tensor_copy(XQN[:, 0:D], SM2[:, 0:D])
            tb = V.tok()
            em.wait(SP, tb)
            ts2 = em.inc(SP.dma_start(out=xn_st[t0:t0 + 128, :], in_=XQN[:, 0:D]), "dxn", 16)
            V.wait(ts1)
            V.wait(ts2)

    def prep_weights(l):
        specs = [(wqkvT[l], swq_st, D, 3 * D, 0), (wgateT[l], swg_st, D, D, 1),
                 (wprojT[l], swp_st, D, D, 2), (wfc1T[l], sw1_st, D, 2 * HID, 3),
                 (wfc2T[l], sw2_st, HID, D, 4)]
        with pst("pw_ps", [128, 8], F32) as pw_ps:
            prev_store = None
            for wap, st_ap, krows, ncols, k in specs:
                V.memset(stc(c_wacc), 0.0)
                for kt in range(krows // 128):
                    for j0 in range(0, ncols, D):
                        w = min(D, ncols - j0)
                        tg = em.now("v")
                        ts0 = em.now("s")
                        em.wait(SP, tg)
                        em.wait(SP, ts0)
                        tl = em.inc(SP.dma_start(
                            out=WF32[:, 0:w],
                            in_=wap[kt * 128:(kt + 1) * 128, j0:j0 + w]), "dwf", 16)
                        S.wait(tl)
                        S.wait(prev_store)
                        S.sign(XQN[:, 0:w], WF32[:, 0:w])
                        tsg = S.tok()
                        V.wait(tl)
                        V.reduce_sum(stc(c_wtmp), WF32[:, 0:w], axis=AX,
                                      apply_absolute_value=True)
                        V.tensor_tensor(stc(c_wacc), stc(c_wacc), stc(c_wtmp), OP.add)
                        em.wait(SP, tsg)
                        prev_store = em.inc(SP.dma_start(
                            out=st_ap[kt * 128:(kt + 1) * 128, j0:j0 + w],
                            in_=XQN[:, 0:w]), "dsw", 16)
                tg = V.tok()
                em.wait(PE, tg)
                PE.matmul(pw_ps[0:1, 0:1], ONECF[:], stc(c_wacc), start=True, stop=True)
                tp = PE.mark()
                V.wait(tp)
                V.tensor_scalar(ROWS[0:1, 0:1], pw_ps[0:1, 0:1],
                                1.0 / (krows * ncols), 1e-5, OP.mult, OP.max)
                tv = V.tok()
                em.wait(PE, tv)
                PE.matmul(pw_ps[:, k:k + 1], ONE1F[0:1, 0:1].broadcast_to([1, 128]),
                          ROWS[0:1, 0:1], start=True, stop=True)
                tp = PE.mark()
                V.wait(tp)
                V.tensor_copy(WSC[:, k:k + 1], pw_ps[:, k:k + 1])
                tv = V.tok()
                em.wait(PE, tv)
        em.wait(SP, em.now("dsw"))

    def layer_consts(l):
        with pst("lc_bc", [128, 512], F32) as lc_bc, \
             pst("lc_c", [128, 96], F32) as lc_c:
            r1 = lambda ap: ap.rearrange("(a b) -> a b", a=1)

            def ld_row(src, n):
                tp = em.now("pe")
                em.wait(SP, tp)
                for nm in ("dob", "drw"):
                    if em.count.get(nm, 0):
                        em.wait(SP, em.now(nm))
                tl = em.inc(SP.dma_start(out=ROWS[0:1, 0:n], in_=src), "drw", 16)
                em.wait(PE, tl)

            for idx, row_ap in enumerate((norm1w[l], norm2w[l])):
                ld_row(r1(row_ap), D)
                bcast_row(ROWS, D, NWB[:, idx * D:(idx + 1) * D], lc_bc)
            for idx, row_ap in enumerate((r1(bproj[l]), r1(bfc2[l]))):
                ld_row(row_ap, D)
                bcast_row(ROWS, D, BIASF[:, idx * D:(idx + 1) * D], lc_bc)
            tg0 = em.now("v")
            em.wait(SP, tg0)
            tnw = em.inc(SP.dma_start(out=NWAT[:], in_=anormwT[l]), "dnw", 16)
            V.wait(tnw)
            ld_row(r1(bqkv[l])[0:1, 0:2 * D], 2 * D)
            col_from_rows(ROWS, 0, 16, BCOL[:, 0:16], lc_c)
            ld_row(r1(bqkv[l])[0:1, 2 * D:3 * D], D)
            col_from_rows(ROWS, 0, 8, BCOL[:, 16:24], lc_c)
            ld_row(r1(bgate[l]), D)
            col_from_rows(ROWS, 0, 8, BCOL[:, 24:32], lc_c)
            for piece in range(4):
                ld_row(r1(bfc1[l])[0:1, piece * 2048:(piece + 1) * 2048], 2048)
                col_from_rows(ROWS, 0, 16, BCOL[:, 32 + piece * 16:48 + piece * 16], lc_c)

    def stream_panel(dst_view, src_st, kt_rows, col0, ncols, kw=512, sem="dwA"):
        eng = SP if sem == "dwA" else nc.gpsimd
        em.wait(eng, em.now("pe"))
        tld = None
        for kt in range(kt_rows):
            tld = em.inc(eng.dma_start(
                out=dst_view[:, kt * kw:kt * kw + ncols],
                in_=src_st[kt * 128:(kt + 1) * 128, col0:col0 + ncols]), sem, 16)
        return tld

    def phase_A(l, blk):
        base = blk * AB
        with pst("pa_mm", [128, 512], F32) as pa_mm, \
             pst("pa_mm2", [128, 512], F32) as pa_mm2, \
             pst("pa_tr", [128, 1024], BF16) as pa_tr, \
             pst("pa_bc", [128, 512], F32) as pa_bc:
            fence()
            em.wait(SP, em.now("dxs"))
            trc = em.inc(SP.dma_start(out=CREP[:], in_=crep[:, base:base + AB]), "dcr", 16)
            trs = em.inc(SP.dma_start(out=SREP[:], in_=srep[:, base:base + AB]), "dcr", 16)
            for c in range(NPT):
                t0 = base + c * 128
                tv0 = em.now("v")
                em.wait(SP, tv0)
                tl = em.inc(SP.dma_start(out=XOLD[:, 0:D], in_=x_st[t0:t0 + 128, :]),
                            "dxo", 16)
                V.wait(tl)
                rmsnorm_quant_chunk(XOLD[:, 0:D], NWB[:, 0:D], c, XQN[:, 0:D])
                transpose_chunk(XQN[:, 0:D], pa_tr,
                                lambda f, c=c: XQT[:, f * AB + c * 128:f * AB + (c + 1) * 128])
            trow = cols_to_row(stc(c_x127, NPT), NPT, ROWS[0:1, 0:AB])
            em.wait(PE, trow)
            bcast_row(ROWS, AB, XSB, pa_bc)
            txsb = em.now("s")
            so = {0: em.now("s"), 1: em.now("s")}
            pv = {0: em.now("v"), 1: em.now("v")}
            with relaxed():
                for si in range(6):
                    woff = (si % 2) * 4096
                    em.wait(SP, em.now("pe"))
                    tld = stream_panel(WAR[:, woff:woff + 4096], swq_st, DT, si * 512, 512,
                                       sem="dwA" if si % 2 == 0 else "dwB")
                    em.wait(PE, tld)
                    for fb in range(4):
                        fglob = si * 4 + fb
                        dstb = QR if fglob < 8 else (KR if fglob < 16 else VTR)
                        floc = fglob % 8
                        for t2 in range(2):
                            dst = pa_mm if t2 == 0 else pa_mm2
                            em.wait(PE, pv.get(t2))
                            for kt in range(DT):
                                PE.matmul(dst[:, 0:512],
                                          WAR[:, woff + kt * 512 + fb * 128:woff + kt * 512 + (fb + 1) * 128],
                                          XQT[:, kt * AB + t2 * 512:kt * AB + (t2 + 1) * 512],
                                          start=(kt == 0), stop=(kt == DT - 1))
                            tpe = PE.mark()
                            V.wait(tpe)
                            V.wait(txsb)
                            V.wait(so.get(t2))
                            V.scalar_tensor_tensor(SM[:, t2 * 512:(t2 + 1) * 512],
                                                   dst[:, 0:512], WSC[:, 0:1],
                                                   XSB[:, t2 * 512:(t2 + 1) * 512],
                                                   OP.mult, OP.mult)
                            tv = V.tok()
                            pv[t2] = tv
                            S.wait(tv)
                            S.activation(dstb[:, floc * AB + t2 * 512:floc * AB + (t2 + 1) * 512],
                                         SM[:, t2 * 512:(t2 + 1) * 512], AF.Identity,
                                         bias=BCOL[:, fglob:fglob + 1])
                            so[t2] = S.tok()
            sg = {0: em.now("s"), 1: em.now("s")}
            vg = {0: em.now("v"), 1: em.now("v")}
            with relaxed():
                for gh in range(2):
                    woff = (gh % 2) * 4096
                    em.wait(SP, em.now("pe"))
                    tld = stream_panel(WAR[:, woff:woff + 4096], swg_st, DT, gh * 512, 512,
                                       sem="dwA" if gh % 2 == 0 else "dwB")
                    em.wait(PE, tld)
                    for fb in range(4):
                        fglob = gh * 4 + fb
                        for t2 in range(2):
                            dst = pa_mm if t2 == 0 else pa_mm2
                            em.wait(PE, pv.get(t2))
                            for kt in range(DT):
                                PE.matmul(dst[:, 0:512],
                                          WAR[:, woff + kt * 512 + fb * 128:woff + kt * 512 + (fb + 1) * 128],
                                          XQT[:, kt * AB + t2 * 512:kt * AB + (t2 + 1) * 512],
                                          start=(kt == 0), stop=(kt == DT - 1))
                            tpe = PE.mark()
                            V.wait(tpe)
                            V.wait(sg.get(t2))
                            V.scalar_tensor_tensor(SM[:, t2 * 512:(t2 + 1) * 512],
                                                   dst[:, 0:512], WSC[:, 1:2],
                                                   XSB[:, t2 * 512:(t2 + 1) * 512],
                                                   OP.mult, OP.mult)
                            tv = V.tok()
                            pv[t2] = tv
                            S.wait(tv)
                            S.wait(vg.get(t2))
                            S.activation(SM2[:, t2 * 512:(t2 + 1) * 512],
                                         SM[:, t2 * 512:(t2 + 1) * 512], AF.Sigmoid,
                                         bias=BCOL[:, 24 + fglob:25 + fglob])
                            ts = S.tok()
                            sg[t2] = ts
                            V.wait(ts)
                            for cc in range(4):
                                c = t2 * 4 + cc
                                V.reduce_sum(DECAY[:, fglob * NPT + c:fglob * NPT + c + 1],
                                             SM2[:, t2 * 512 + cc * 128:t2 * 512 + (cc + 1) * 128],
                                             axis=AX)
                            vg[t2] = V.tok()
            V.tensor_scalar(DECAY[:, 0:8 * NPT], DECAY[:, 0:8 * NPT],
                            1.0 / CHUNK, None, OP.mult)
            V.wait(trs)
            for buf in (QR, KR):
                for f in range(DT):
                    tev = em.now("s")
                    em.wait(PE, tev)
                    for t2 in range(2):
                        dst = pa_mm if t2 == 0 else pa_mm2
                        PE.matmul(dst[:, 0:512], PMAT[:],
                                  buf[:, f * AB + t2 * 512:f * AB + (t2 + 1) * 512],
                                  start=True, stop=True)
                    tpe = PE.mark()
                    S.wait(tpe)
                    S.copy(XQN[:, 0:512], pa_mm[:, 0:512])
                    S.copy(XQN[:, 512:1024], pa_mm2[:, 0:512])
                    tsh = S.tok()
                    V.tensor_tensor(SM2[:, 0:AB], buf[:, f * AB:(f + 1) * AB], CREP[:], OP.mult)
                    V.wait(tsh)
                    V.tensor_tensor(SM2[:, AB:2 * AB], XQN[:, 0:AB], SREP[:], OP.mult)
                    V.tensor_tensor(buf[:, f * AB:(f + 1) * AB], SM2[:, 0:AB],
                                    SM2[:, AB:2 * AB], OP.add)
            tv = V.tok()
            em.wait(PE, tv)

    def phase_B(l, blk, pbstate):
        base = blk * AB
        Q5 = BIGB[0:64, 0:2048].rearrange("p (h q) -> p h q", h=16)
        K5 = BIGB[0:64, 2048:4096].rearrange("p (h q) -> p h q", h=16)
        TQKV = BIGF[0:64, 2048:4096]
        TQKV3 = TQKV.rearrange("p (h q) -> p q h", h=16)
        AVT = SM[0:64, 0:2048]
        AVT3 = AVT.rearrange("p (q h) -> p q h", h=16)
        XWT = SM2[0:64, 0:2048]
        XWT3 = XWT.rearrange("p (q h) -> p q h", h=16)
        V43 = VL4B[0:16, 0:8192].rearrange("p (e q) -> p e q", q=128)
        P1 = WF32[0:64, 0:128]
        M1 = WF32[0:64, 128:256]
        QSCS = WF32[0:64, 256:384]
        RDB = XQN[0:1, 0:1024]
        QROW = ROWS[0:1, 0:128]
        KV3 = KV[:].rearrange("r (p q) -> r p q", q=128)
        DC3 = DECAY[:].rearrange("r (p n) -> r p n", n=NPT)
        XQE3 = XQED.rearrange("p (q h) -> p h q", h=16)

        def hsl(r0, r1, h):
            return XQE3[r0:r1, h:h + 1, :].rearrange("p o q -> p (o q)")

        def lsl(v3, lq, n):
            return v3[:, :, lq:lq + 1].rearrange(f"p {n} o -> p ({n} o)")

        with pst("pb_kn", [128, 1024], BF16) as pb_kn, \
             pst("pb_sp", [128, 1024], F32) as pb_sp, \
             pst("pb_dn", [128, 1024], F32) as pb_dn, \
             pst("pb_av", [128, 512], F32) as pb_av, \
             pst("pb_p1", [128, 512], F32) as pb_p1, \
             pst("pb_p2", [128, 512], F32) as pb_p2:
            fence()
            tld0 = stream_panel(WAR[:, 0:4096], swp_st, DT, 0, 512, sem="dwA")
            tld1 = stream_panel(WAR[:, 4096:8192], swp_st, DT, 512, 512, sem="dwB")
            gfence()
            pbstate["q5war"] = em.now("pe")
            for c in range(NPT):
                t0 = base + c * 128
                if em.count.get("dbn", 0):
                    em.wait(nc.gpsimd, em.now("dbn"))
                for dstd, srcb in ((qtd, QR), (ktd, KR), (vtd, VTR)):
                    em.inc(nc.gpsimd.dma_start(
                        out=dstd.rearrange("t r q -> r t q"),
                        in_=srcb.rearrange("r (t q) -> r t q", t=DT)[:, :, c * 128:(c + 1) * 128]),
                        "dbg", 16)
                if B_STAGE >= 4:
                    em.wait(SP, pbstate.get("q5war"))
                    em.wait(SP, em.now("dbg"))
                    tQ5 = em.inc(SP.dma_start(
                        out=BIGB[0:64, 0:2048].rearrange("p (h q) -> p h q", h=16),
                        in_=qtd.rearrange("t (hp d) q -> d (t hp) q", hp=2)), "dbn", 16)
                    tK5 = em.inc(SP.dma_start(
                        out=BIGB[0:64, 2048:4096].rearrange("p (h q) -> p h q", h=16),
                        in_=ktd.rearrange("t (hp d) q -> d (t hp) q", hp=2)), "dbn", 16)
                    tV4 = em.inc(SP.dma_start(
                        out=VL4B[0:16, 0:8192],
                        in_=vtd.rearrange("t (gp e) q -> (t gp) (e q)", gp=2)), "dbn", 16)
                V.wait(em.now("pe"))
                V.tensor_copy(KVB[:], KV[:])
                tkvb = V.tok()
                tprev = em.now("s")
                em.wait(PE, tprev)
                em.wait(PE, em.now("v"))
                for f in range(DT):
                    PE.transpose(pb_kn[:, f * 128:(f + 1) * 128],
                                 KR[:, f * AB + c * 128:f * AB + (c + 1) * 128], IDENT[:])
                tpe = PE.mark()
                S.wait(tpe)
                S.copy(KNAT[:, 0:D], pb_kn[:, 0:D])
                tkn = S.tok()
                em.wait(PE, tkn)
                for f in range(DT):
                    PE.transpose(pb_kn[:, f * 128:(f + 1) * 128],
                                 VTR[:, f * AB + c * 128:f * AB + (c + 1) * 128], IDENT[:])
                tpe = PE.mark()
                S.wait(tpe)
                S.copy(VNC[:, 0:D], pb_kn[:, 0:D])
                tvn = S.tok()
                if B_STAGE < 2:
                    continue
                em.wait(PE, tkvb)
                TQ4 = TQKV.rearrange("p (hh par q) -> p par hh q", hh=8, par=2)
                for par in range(2):
                    ro = par * 64
                    dst = pb_p1 if par == 0 else pb_p2
                    for hfq in range(2):
                        tev = em.now("s")
                        em.wait(PE, tev)
                        for j in range(4):
                            ft = hfq * 4 + j
                            PE.matmul(dst[0:64, j * 128:(j + 1) * 128],
                                      KVB[ro:ro + 64, ft * 128 + ro:ft * 128 + ro + 64],
                                      QR[ro:ro + 64, ft * AB + c * 128:ft * AB + (c + 1) * 128],
                                      start=True, stop=True)
                        tpe = PE.mark()
                        S.wait(tpe)
                        S.copy(TQ4[:, par:par + 1, hfq * 4:hfq * 4 + 4, :]
                               .rearrange("p o h q -> p (o h) q"),
                               dst[0:64, 0:512].rearrange("p (h q) -> p h q", h=4))
                if B_STAGE < 3:
                    continue
                em.wait(PE, tvn)
                em.wait(PE, em.now("v"))
                for p in range(8):
                    PE.matmul(pb_sp[:, p * 128:(p + 1) * 128],
                              KNAT[:, p * 128:(p + 1) * 128],
                              VNC[:, p * 128:(p + 1) * 128],
                              start=True, stop=True)
                tpe = PE.mark()
                V.wait(tpe)
                V.wait(tkvb)
                V.tensor_tensor(KV3, KV3,
                                DC3[:, :, c:c + 1].broadcast_to([128, 8, 128]),
                                OP.mult)
                V.tensor_tensor(KV[:], KV[:], pb_sp[0:128, 0:1024], OP.add)
                tkvu = V.tok()
                if B_STAGE < 5:
                    continue
                em.wait(PE, tV4)
                for s in range(2):
                    u0 = 0
                    tev = em.now("s")
                    em.wait(PE, tev)
                    em.wait(PE, em.now("v"))
                    for lo in range(64):
                        lq = s * 64 + lo
                        PE.matmul(pb_sp[0:16, lo * 16:(lo + 1) * 16],
                                  lsl(K5, lq, "h"), lsl(Q5, lq, "h"),
                                  start=True, stop=True)
                    tpe = PE.mark()
                    S.wait(tpe)
                    S.activation(UPC[u0:u0 + 16, 0:1024], pb_sp[0:16, 0:1024],
                                 AF.Exp, scale=SCALE_ATT)
                    tex = S.tok()
                    em.wait(PE, tex)
                    PE.matmul(pb_dn[0:1, 0:512], ONESB[0:16, 0:1],
                              UPC[u0:u0 + 16, 0:512], start=True, stop=True)
                    PE.matmul(pb_dn[0:1, 512:1024], ONESB[0:16, 0:1],
                              UPC[u0:u0 + 16, 512:1024], start=True, stop=True)
                    tpe = PE.mark()
                    V.wait(tpe)
                    with nc.allow_low_precision(
                            reason="softmax weights are bf16 matmul operands; "
                                   "bf16 denominator recip matches that precision"):
                        V.reciprocal(RDB, pb_dn[0:1, 0:1024])
                    trc = V.tok()
                    em.wait(PE, trc)
                    PE.matmul(pb_dn[0:16, 0:512], ONESB[0:1, 0:1].broadcast_to([1, 16]),
                              RDB[0:1, 0:512], start=True, stop=True)
                    PE.matmul(pb_dn[0:16, 512:1024], ONESB[0:1, 0:1].broadcast_to([1, 16]),
                              RDB[0:1, 512:1024], start=True, stop=True)
                    tpe = PE.mark()
                    V.wait(tpe)
                    V.tensor_tensor(UPC[u0:u0 + 16, 0:1024], UPC[u0:u0 + 16, 0:1024],
                                    pb_dn[0:16, 0:1024], OP.mult)
                    tA = V.tok()
                    em.wait(PE, tA)
                    for lo in range(32):
                        lq = s * 64 + lo
                        PE.matmul(pb_p1[0:64, lo * 16:(lo + 1) * 16],
                                  lsl(V43, lq, "e"),
                                  UPC[u0:u0 + 16, lo * 16:(lo + 1) * 16],
                                  start=True, stop=True)
                    tpeA = PE.mark()
                    for lo in range(32, 64):
                        lq = s * 64 + lo
                        PE.matmul(pb_av[0:64, (lo - 32) * 16:(lo - 31) * 16],
                                  lsl(V43, lq, "e"),
                                  UPC[u0:u0 + 16, lo * 16:(lo + 1) * 16],
                                  start=True, stop=True)
                    tpeB = PE.mark()
                    V.wait(tpeA)
                    V.tensor_tensor(AVT3[:, s * 64:s * 64 + 32, :],
                                    pb_p1[0:64, 0:512].rearrange("p (q h) -> p q h", h=16),
                                    TQKV3[:, s * 64:s * 64 + 32, :], OP.add)
                    V.wait(tpeB)
                    V.tensor_tensor(AVT3[:, s * 64 + 32:s * 64 + 64, :],
                                    pb_av[0:64, 0:512].rearrange("p (q h) -> p q h", h=16),
                                    TQKV3[:, s * 64 + 32:s * 64 + 64, :], OP.add)
                    tavs = V.tok()
                    em.wait(PE, tavs)
                pbstate["q5war"] = em.now("pe")
                if B_STAGE < 6:
                    continue
                S.square(XWT, AVT)
                tsq = S.tok()
                V.wait(tsq)
                V.reduce_sum(P1, XWT3, axis=AX)
                V.tensor_tensor(XWT3, AVT3,
                                NWAT.rearrange("p (o h) -> p o h", o=1).broadcast_to([64, 128, 16]),
                                OP.mult)
                V.reduce_max(M1, XWT3, axis=AX, apply_absolute_value=True)
                tv2 = V.tok()
                em.wait(PE, tv2)
                PE.transpose(pb_sp[:, 0:64], P1, IDENTF[0:64, 0:64])
                PE.transpose(pb_sp[:, 64:128], M1, IDENTF[0:64, 0:64])
                tpe = PE.mark()
                V.wait(tpe)
                V.reduce_sum(stc(c_sum + c), pb_sp[:, 0:64], axis=AX)
                V.reduce_max(stc(c_m1 + c), pb_sp[:, 64:128], axis=AX)
                V.tensor_scalar(stc(c_t0 + c), stc(c_sum + c), 1.0 / D, EPS_RMS,
                                OP.mult, OP.add)
                V.reciprocal(stc(c_t1 + c), stc(c_t0 + c))
                tv = V.tok()
                S.wait(tv)
                S.sqrt(stc(c_rstd + c), stc(c_t1 + c))
                ts = S.tok()
                V.wait(ts)
                V.tensor_tensor(stc(c_xs + c), stc(c_rstd + c), stc(c_m1 + c), OP.mult)
                V.tensor_scalar(stc(c_xs + c), stc(c_xs + c), 1e-5, None, OP.max)
                V.reciprocal(stc(c_t0 + c), stc(c_xs + c))
                V.tensor_tensor(stc(c_t1 + c), stc(c_t0 + c), stc(c_rstd + c), OP.mult)
                V.tensor_scalar(stc(c_qsc + c), stc(c_t1 + c), 127.0, None, OP.mult)
                V.tensor_scalar(stc(c_xsa + c), stc(c_xs + c), 1.0 / 127.0, None, OP.mult)
                tv = V.tok()
                em.wait(PE, tv)
                PE.transpose(pb_sp[0:1, 128:256], stc(c_qsc + c), IDENTF[:])
                tpe = PE.mark()
                S.wait(tpe)
                for nm in ("dob", "drw"):
                    if em.count.get(nm, 0):
                        em.wait(nc.scalar, em.now(nm))
                S.copy(QROW, pb_sp[0:1, 128:256])
                tqr = S.tok()
                em.wait(PE, tqr)
                PE.matmul(pb_sp[0:64, 256:384], ONE1F[0:1, 0:1].broadcast_to([1, 64]),
                          QROW, start=True, stop=True)
                tpe = PE.mark()
                S.wait(tpe)
                S.copy(QSCS, pb_sp[0:64, 256:384])
                tqs = S.tok()
                V.wait(tqs)
                V.tensor_tensor(XWT3, XWT3,
                                QSCS.rearrange("p (q o) -> p q o", o=1).broadcast_to([64, 128, 16]),
                                OP.mult)
                tv = V.tok()
                V.wait(em.now("dxq"))
                V.wait(em.now("pe"))
                for hf in range(2):
                    S.wait(tv)
                    S.activation(XI16[0:64, 0:D], XWT[:, hf * D:(hf + 1) * D], AF.Copy)
                    ts = S.tok()
                    V.wait(ts)
                    V.tensor_copy(XQED[0:64, hf * D:(hf + 1) * D], XI16[0:64, 0:D])
                tqe = V.tok()
                if B_STAGE < 7:
                    continue
                em.wait(SP, tqe)
                tdup = em.inc(SP.dma_start(out=XQED[64:128, 0:2048],
                                           in_=XQED[0:64, 0:2048]), "dxq", 16)
                tv0 = em.now("v")
                em.wait(SP, tv0)
                tox = em.inc(SP.dma_start(out=XOLD[:, 0:D], in_=x_st[t0:t0 + 128, :]),
                             "dxo", 16)
                if c == 0:
                    em.wait(PE, tld0)
                    em.wait(PE, tld1)
                em.wait(PE, tqe)
                em.wait(PE, tdup)
                for fh in range(2):
                    tev = em.now("v")
                    em.wait(PE, tev)
                    for j in range(8):
                        PE.matmul(pb_p1[:, 0:512],
                                  hsl(0, 64, 2 * j),
                                  WAR[0:64, fh * 4096 + j * 512:fh * 4096 + (j + 1) * 512],
                                  start=(j == 0), stop=(j == 7))
                    for j in range(8):
                        PE.matmul(pb_p2[:, 0:512],
                                  hsl(64, 128, 2 * j + 1),
                                  WAR[64:128, fh * 4096 + j * 512:fh * 4096 + (j + 1) * 512],
                                  start=(j == 0), stop=(j == 7))
                    tpe = PE.mark()
                    S.wait(tpe)
                    S.copy(SM2[:, 0:512], pb_p2[:, 0:512])
                    ts2 = S.tok()
                    V.wait(ts2)
                    V.wait(tox)
                    V.tensor_tensor(SM2[:, 0:512], SM2[:, 0:512], pb_p1[:, 0:512], OP.add)
                    V.tensor_scalar(SM2[:, 512:1024], SM2[:, 0:512],
                                    stc(c_xsa + c), WSC[:, 2:3], OP.mult, OP.mult)
                    V.tensor_tensor(SM2[:, 512:1024], SM2[:, 512:1024],
                                    BIASF[:, fh * 512:(fh + 1) * 512], OP.add)
                    V.tensor_tensor(XNEW[:, fh * 512:(fh + 1) * 512], SM2[:, 512:1024],
                                    XOLD[:, fh * 512:(fh + 1) * 512], OP.add)
                tnew = V.tok()
                em.wait(SP, tnew)
                tst = em.inc(SP.dma_start(out=x_st[t0:t0 + 128, :], in_=XNEW[:, 0:D]),
                             "dxs", 16)
                rmsnorm_quant_chunk(XNEW[:, 0:D], NWB[:, D:2 * D], c, XQN[:, 0:D])
                V.wait(tst)
                transpose_chunk(XQN[:, 0:D], pb_kn,
                                lambda f, c=c: XQ2T[:, f * AB + c * 128:f * AB + (c + 1) * 128])

    def phase_C(l, blk):
        base = blk * AB
        with pst("pc_mm", [128, 512], F32) as pc_mm, \
             pst("pc_mm2", [128, 512], F32) as pc_mm2, \
             pst("pc_mm3", [128, 512], F32) as pc_mm3, \
             pst("pc_mm4", [128, 512], F32) as pc_mm4, \
             pst("pc_bc", [128, 512], F32) as pc_bc:
            fence()
            em.wait(SP, em.now("dxs"))
            trow = cols_to_row(stc(c_x127, NPT), NPT, ROWS[0:1, 0:AB])
            em.wait(PE, trow)
            bcast_row(ROWS, AB, XSB, pc_bc)
            txsb = em.now("s")
            GLUv = BIGB
            sc = {0: em.now("s"), 1: em.now("s")}
            vc = {0: em.now("v"), 1: em.now("v")}
            pvc = {0: em.now("v")}
            with relaxed():
                for pan in range(16):
                    woff = (pan % 2) * 4096
                    em.wait(SP, em.now("pe"))
                    tld = stream_panel(WAR[:, woff:woff + 4096], sw1_st, DT, pan * 512, 512,
                                       sem="dwA" if pan % 2 == 0 else "dwB")
                    em.wait(PE, tld)
                    for pr in range(2):
                        ja = pan * 4 + pr * 2
                        jg = ja + 1
                        hidx = pan * 2 + pr
                        for t2 in range(2):
                            bslot = (pr * 2 + t2) % 2
                            dstA = pc_mm if bslot == 0 else pc_mm3
                            dstB = pc_mm2 if bslot == 0 else pc_mm4
                            em.wait(PE, pvc.get(bslot))
                            for kt in range(DT):
                                PE.matmul(dstA[:, 0:512],
                                          WAR[:, woff + kt * 512 + (pr * 2) * 128:woff + kt * 512 + (pr * 2 + 1) * 128],
                                          XQ2T[:, kt * AB + t2 * 512:kt * AB + (t2 + 1) * 512],
                                          start=(kt == 0), stop=(kt == DT - 1))
                            for kt in range(DT):
                                PE.matmul(dstB[:, 0:512],
                                          WAR[:, woff + kt * 512 + (pr * 2 + 1) * 128:woff + kt * 512 + (pr * 2 + 2) * 128],
                                          XQ2T[:, kt * AB + t2 * 512:kt * AB + (t2 + 1) * 512],
                                          start=(kt == 0), stop=(kt == DT - 1))
                            tpe = PE.mark()
                            V.wait(tpe)
                            V.wait(txsb)
                            V.wait(sc.get(bslot))
                            V.scalar_tensor_tensor(SM[:, bslot * 1024:bslot * 1024 + 512],
                                                   dstA[:, 0:512], WSC[:, 3:4],
                                                   XSB[:, t2 * 512:(t2 + 1) * 512],
                                                   OP.mult, OP.mult)
                            V.scalar_tensor_tensor(SM[:, bslot * 1024 + 512:bslot * 1024 + 1024],
                                                   dstB[:, 0:512], WSC[:, 3:4],
                                                   XSB[:, t2 * 512:(t2 + 1) * 512],
                                                   OP.mult, OP.mult)
                            tv = V.tok()
                            pvc[bslot] = tv
                            S.wait(tv)
                            S.wait(vc.get(bslot))
                            S.activation(SM2[:, bslot * 1024:bslot * 1024 + 512],
                                         SM[:, bslot * 1024:bslot * 1024 + 512], AF.Identity,
                                         bias=BCOL[:, 32 + ja:33 + ja])
                            S.activation(SM2[:, bslot * 1024 + 512:bslot * 1024 + 1024],
                                         SM[:, bslot * 1024 + 512:bslot * 1024 + 1024], AF.Silu,
                                         bias=BCOL[:, 32 + jg:33 + jg])
                            ts = S.tok()
                            sc[bslot] = ts
                            V.wait(ts)
                            V.tensor_tensor(GLUv[:, hidx * AB + t2 * 512:hidx * AB + (t2 + 1) * 512],
                                            SM2[:, bslot * 1024:bslot * 1024 + 512],
                                            SM2[:, bslot * 1024 + 512:bslot * 1024 + 1024], OP.mult)
                            vc[bslot] = V.tok()
            with pst("pc_q32", [128, 128], F32) as pc_q32:
                V.reduce_max(SM[:, 0:AB],
                             GLUv[:, 0:32 * AB].rearrange("p (j t) -> p t j", j=32),
                             axis=AX, apply_absolute_value=True)
                tvt = V.tok()
                em.wait(PE, tvt)
                for ci in range(NPT):
                    PE.transpose(pc_q32[:, 0:128],
                                 SM[:, ci * 128:(ci + 1) * 128], IDENTF[:])
                    tpq = PE.mark()
                    V.wait(tpq)
                    V.reduce_max(stc(c_xs + ci), pc_q32[:, 0:128], axis=AX)
                    tvq = V.tok()
                    em.wait(PE, tvq)
                V.tensor_scalar(stc(c_xs, NPT), stc(c_xs, NPT), 1e-5, None, OP.max)
                V.reciprocal(stc(c_t0, NPT), stc(c_xs, NPT))
                V.tensor_scalar(stc(c_qsc, NPT), stc(c_t0, NPT), 127.0, None, OP.mult)
                V.tensor_scalar(stc(c_x127, NPT), stc(c_xs, NPT), 1.0 / 127.0, None, OP.mult)
            trow2 = cols_to_row(stc(c_qsc, NPT), NPT, ROWS[0:1, 0:AB])
            em.wait(PE, trow2)
            bcast_row(ROWS, AB, XSB, pc_bc)
            tq = em.now("s")
            for j in range(32):
                if j == 0:
                    V.wait(tq)
                V.tensor_tensor(SM[:, 0:D], GLUv[:, j * AB:(j + 1) * AB], XSB[:], OP.mult)
                tvq = V.tok()
                S.wait(tvq)
                S.activation(XI16[:, 0:D], SM[:, 0:D], AF.Copy)
                ts = S.tok()
                V.wait(ts)
                V.tensor_copy(GLUv[:, j * AB:(j + 1) * AB], XI16[:, 0:D])
            for fhv in range(2):
                tpe_guard = em.now("pe")
                em.wait(SP, tpe_guard)
                em.wait(SP, em.now("dxs"))
                tlda = stream_panel(WAR[:, 0:8192], sw2_st, HID // 128, fhv * 512, 256,
                                    kw=256, sem="dwA")
                tldb = stream_panel(WAR[:, 8192:16384], sw2_st, HID // 128,
                                    fhv * 512 + 256, 256, kw=256, sem="dwB")
                em.wait(PE, tlda)
                em.wait(PE, tldb)
                for c in range(NPT):
                    t0 = base + c * 128
                    tv0 = em.now("v")
                    em.wait(SP, tv0)
                    tox = em.inc(SP.dma_start(out=XOLD[:, 0:D],
                                              in_=x_st[t0:t0 + 128, :]), "dxo", 16)
                    for q2 in range(2):
                        woff = q2 * 8192
                        fcol = fhv * 512 + q2 * 256
                        dst = pc_mm if q2 == 0 else pc_mm2
                        tev = em.now("v")
                        em.wait(PE, tev)
                        for kt in range(HID // 128):
                            PE.matmul(dst[:, 0:256],
                                      GLUv[:, kt * AB + c * 128:kt * AB + (c + 1) * 128],
                                      WAR[:, woff + kt * 256:woff + (kt + 1) * 256],
                                      start=(kt == 0), stop=(kt == HID // 128 - 1))
                        tpe = PE.mark()
                        V.wait(tpe)
                        V.wait(tox)
                        V.tensor_scalar(SM[:, 0:256], dst[:, 0:256],
                                        stc(c_x127 + c), WSC[:, 4:5], OP.mult, OP.mult)
                        V.tensor_tensor(SM[:, 0:256], SM[:, 0:256],
                                        BIASF[:, D + fcol:D + fcol + 256], OP.add)
                        V.tensor_tensor(XNEW[:, fcol:fcol + 256], SM[:, 0:256],
                                        XOLD[:, fcol:fcol + 256], OP.add)
                    tnew = V.tok()
                    em.wait(SP, tnew)
                    tst = em.inc(SP.dma_start(
                        out=x_st[t0:t0 + 128, fhv * 512:(fhv + 1) * 512],
                        in_=XNEW[:, fhv * 512:(fhv + 1) * 512]), "dxs", 16)
                    V.wait(tst)

    def ffn_consts(e, w1b, w2b, w3b):
        with pst("fb_c", [128, 32], F32) as fb_c, \
             pst("fb_bc", [128, 512], F32) as fb_bc:
            r1 = lambda ap: ap.rearrange("(a b) -> a b", a=1)

            def ld_row2(src, n):
                tp = em.now("pe")
                em.wait(SP, tp)
                for nm in ("dob", "drw"):
                    if em.count.get(nm, 0):
                        em.wait(SP, em.now(nm))
                tl = em.inc(SP.dma_start(out=ROWS[0:1, 0:n], in_=src), "drw", 16)
                em.wait(PE, tl)

            ld_row2(r1(w1b), FH)
            col_from_rows(ROWS, 0, FHT, BCF[:, 0:FHT], fb_c)
            ld_row2(r1(w2b), FH)
            col_from_rows(ROWS, 0, FHT, BCF[:, 16:16 + FHT], fb_c)
            ld_row2(r1(w3b), D)
            bcast_row(ROWS, D, BIASF[:, 0:D], fb_bc)

    def phase_FFN():
        XNT = BIGB[:, 0:DT * OWN]
        GLUH = BIGB[:, DT * OWN:DT * OWN + FHT * AB]
        with pst("ff_tr", [128, 1024], BF16) as ff_tr, \
             pst("ff_mm", [128, 512], F32) as ff_mm, \
             pst("ff_mm2", [128, 512], F32) as ff_mm2, \
             pst("ff_mm3", [128, 512], F32) as ff_mm3, \
             pst("ff_mm4", [128, 512], F32) as ff_mm4, \
             pst("ff_bc", [128, 512], F32) as ff_bc:
            fence()
            gfence()
            em.wait(SP, em.now("dxn"))
            V.memset(XNEW[:, 0:D], 0.0)
            tz = V.tok()
            em.wait(SP, tz)
            for gq in range(NOC):
                em.inc(SP.dma_start(out=acc_st[gq * 128:(gq + 1) * 128, :],
                                    in_=XNEW[:, 0:D]), "dacc", 16)
            em.wait(SP, em.now("dacc"))
            for gq in range(NOC):
                tprev_s = em.now("s")
                em.wait(nc.gpsimd, tprev_s)
                tg = em.inc(nc.gpsimd.indirect_dma_start(
                    XQN[:, 0:D], None, xn_st,
                    bass.IndirectOffsetOnAxis(IXC[:, gq:gq + 1], 0)), "dgx", 16)
                em.wait(PE, tg)
                transpose_chunk(XQN[:, 0:D], ff_tr,
                                lambda f, gq=gq: XNT[:, f * OWN + gq * 128:f * OWN + (gq + 1) * 128])
            gfence()
            tgw = None
            for kt in range(DT):
                tgw = em.inc(nc.gpsimd.dma_start(out=WAR[:, kt * FE:(kt + 1) * FE],
                                                 in_=fgw[kt * 128:(kt + 1) * 128, :]),
                             "dgw", 16)
            tp0 = em.now("pe")
            em.wait(SP, tp0)
            for nm in ("dob", "drw"):
                if em.count.get(nm, 0):
                    em.wait(SP, em.now(nm))
            tgb = em.inc(SP.dma_start(out=ROWS[0:1, 0:FE], in_=fgb), "drw", 16)
            em.wait(PE, tgw)
            em.wait(PE, tgb)
            for gq in range(NOC):
                tpv = em.now("v")
                em.wait(PE, tpv)
                for kt in range(DT):
                    PE.matmul(ff_mm[:, 0:FE],
                              XNT[:, kt * OWN + gq * 128:kt * OWN + (gq + 1) * 128],
                              WAR[:, kt * FE:(kt + 1) * FE],
                              start=(kt == 0), stop=False)
                PE.matmul(ff_mm[:, 0:FE], ONECF[0:1, 0:1].broadcast_to([1, 128]),
                          ROWS[0:1, 0:FE], start=False, stop=True)
                tpe = PE.mark()
                V.wait(tpe)
                V.reduce_max(stc(c_t0), ff_mm[:, 0:FE], axis=AX)
                V.tensor_scalar(stc(c_t1), stc(c_t0), -1.0, None, OP.mult)
                tv = V.tok()
                S.wait(tv)
                S.activation(SM[:, 0:FE], ff_mm[:, 0:FE], AF.Exp,
                             bias=stc(c_t1), accum_out=stc(c_den))
                ts = S.tok()
                V.wait(ts)
                V.reciprocal(stc(c_rden), stc(c_den))
                V.tensor_scalar(SM[:, 8:16], SM[:, 0:FE], stc(c_rden), None, OP.mult)
                V.reduce_max(stc(c_t0), SM[:, 8:16], axis=AX)
                V.tensor_scalar(SM[:, 16:24], SM[:, 8:16], stc(c_t0), None, OP.is_ge)
                V.tensor_scalar(SM[:, 24:32], SM[:, 16:24], -3.0, None, OP.mult)
                V.tensor_tensor(SM[:, 24:32], SM[:, 8:16], SM[:, 24:32], OP.add)
                V.reduce_max(stc(c_t1), SM[:, 24:32], axis=AX)
                V.tensor_scalar(SM[:, 16:24], SM[:, 8:16], stc(c_t1), None, OP.is_ge)
                V.tensor_tensor(WEB[:, gq * 8:(gq + 1) * 8], SM[:, 8:16],
                                SM[:, 16:24], OP.mult)
            for e in range(FE + FS):
                routed = e < FE
                w1 = frw1[e] if routed else fsw1[e - FE]
                w2 = frw2[e] if routed else fsw2[e - FE]
                w3 = frw3[e] if routed else fsw3[e - FE]
                b1 = frb1[e] if routed else fsb1[e - FE]
                b2 = frb2[e] if routed else fsb2[e - FE]
                b3 = frb3[e] if routed else fsb3[e - FE]
                em.wait(SP, em.now("dacc"))
                ffn_consts(e, b1, b2, b3)
                for th in range(max(OWN // AB, 1)):
                    tw0 = th * AB
                    twn = min(AB, OWN)
                    sf = {0: em.now("v"), 1: em.now("v")}
                    pvf = {0: em.now("v"), 1: em.now("v")}
                    bidx = [0]
                    with relaxed():
                        for pan in range(FH // 512):
                            woff = (pan % 2) * 4096
                            em.wait(nc.gpsimd, em.now("pe"))
                            tld = None
                            for kt in range(DT):
                                tld = em.inc(nc.gpsimd.dma_start(
                                    out=WAR[:, woff + kt * 512:woff + (kt + 1) * 512],
                                    in_=w1[kt * 128:(kt + 1) * 128, pan * 512:(pan + 1) * 512]),
                                    "dgw", 16)
                            for kt in range(DT):
                                tld = em.inc(nc.gpsimd.dma_start(
                                    out=WAR[:, 8192 + woff + kt * 512:8192 + woff + (kt + 1) * 512],
                                    in_=w2[kt * 128:(kt + 1) * 128, pan * 512:(pan + 1) * 512]),
                                    "dgw", 16)
                            em.wait(PE, tld)
                            for fb in range(4):
                                hglob = pan * 4 + fb
                                for t2 in range(twn // 512):
                                    tcol = tw0 + t2 * 512
                                    slot = bidx[0] % 2
                                    bidx[0] += 1
                                    dstA = ff_mm if slot == 0 else ff_mm3
                                    dstB = ff_mm2 if slot == 0 else ff_mm4
                                    em.wait(PE, pvf.get(slot))
                                    for kt in range(DT):
                                        PE.matmul(dstA[:, 0:512],
                                                  WAR[:, woff + kt * 512 + fb * 128:woff + kt * 512 + (fb + 1) * 128],
                                                  XNT[:, kt * OWN + tcol:kt * OWN + tcol + 512],
                                                  start=(kt == 0), stop=(kt == DT - 1))
                                    for kt in range(DT):
                                        PE.matmul(dstB[:, 0:512],
                                                  WAR[:, 8192 + woff + kt * 512 + fb * 128:8192 + woff + kt * 512 + (fb + 1) * 128],
                                                  XNT[:, kt * OWN + tcol:kt * OWN + tcol + 512],
                                                  start=(kt == 0), stop=(kt == DT - 1))
                                    tpe = PE.mark()
                                    S.wait(tpe)
                                    S.wait(sf.get(slot))
                                    S.activation(XQN[:, slot * 512:(slot + 1) * 512],
                                                 dstA[:, 0:512], AF.Silu,
                                                 bias=BCF[:, hglob:hglob + 1])
                                    ts = S.tok()
                                    V.wait(ts)
                                    V.scalar_tensor_tensor(
                                        GLUH[:, hglob * AB + t2 * 512:hglob * AB + (t2 + 1) * 512],
                                        dstB[:, 0:512], BCF[:, 16 + hglob:17 + hglob],
                                        XQN[:, slot * 512:(slot + 1) * 512], OP.add, OP.mult)
                                    tvf = V.tok()
                                    sf[slot] = tvf
                                    pvf[slot] = tvf
                    em.wait(nc.gpsimd, em.now("pe"))
                    tld = None
                    for fh2 in range(2):
                        for kt in range(FHT):
                            tld = em.inc(nc.gpsimd.dma_start(
                                out=WAR[:, fh2 * 8192 + kt * 512:fh2 * 8192 + (kt + 1) * 512],
                                in_=w3[kt * 128:(kt + 1) * 128, fh2 * 512:(fh2 + 1) * 512]),
                                "dgw", 16)
                    em.wait(PE, tld)
                    for cc in range(twn // 128):
                        gq = (tw0 + cc * 128) // 128
                        tv0 = em.now("v")
                        em.wait(SP, tv0)
                        tac = em.inc(SP.dma_start(out=XOLD[:, 0:D],
                                                  in_=acc_st[gq * 128:(gq + 1) * 128, :]),
                                     "dxo", 16)
                        for fh2 in range(2):
                            dst = ff_mm if fh2 == 0 else ff_mm2
                            tev = em.now("v")
                            em.wait(PE, tev)
                            for kt in range(FHT):
                                PE.matmul(dst[:, 0:512],
                                          GLUH[:, kt * AB + cc * 128:kt * AB + (cc + 1) * 128],
                                          WAR[:, fh2 * 8192 + kt * 512:fh2 * 8192 + (kt + 1) * 512],
                                          start=(kt == 0), stop=(kt == FHT - 1))
                            tpe = PE.mark()
                            V.wait(tpe)
                            V.wait(tac)
                            V.tensor_tensor(SM[:, 0:512], dst[:, 0:512],
                                            BIASF[:, fh2 * 512:(fh2 + 1) * 512], OP.add)
                            if routed:
                                V.tensor_scalar(SM[:, 512:1024], SM[:, 0:512],
                                                WEB[:, gq * 8 + e:gq * 8 + e + 1],
                                                None, OP.mult)
                            else:
                                V.tensor_scalar(SM[:, 512:1024], SM[:, 0:512],
                                                0.5, None, OP.mult)
                            V.tensor_tensor(XNEW[:, fh2 * 512:(fh2 + 1) * 512],
                                            SM[:, 512:1024],
                                            XOLD[:, fh2 * 512:(fh2 + 1) * 512], OP.add)
                        tnew = V.tok()
                        em.wait(SP, tnew)
                        tst = em.inc(SP.dma_start(out=acc_st[gq * 128:(gq + 1) * 128, :],
                                                  in_=XNEW[:, 0:D]), "dacc", 16)
                        V.wait(tst)

    pbstate = {}
    for l in range(LAYERS):
        if not (set("ABC") & set(phases)):
            break
        prep_weights(l)
        layer_consts(l)
        V.memset(KV[:], 0.0)
        for blk in range(NBLK):
            if "A" in phases:
                phase_A(l, blk)
            if "B" in phases:
                phase_B(l, blk, pbstate)
            if "C" in phases:
                phase_C(l, blk)

    if "F" in phases:
        phase_FFN()

    fence()
    gfence()
    em.wait(SP, em.now("dacc"))
    em.wait(SP, em.now("dxs"))
    for gq in range(NOC):
        tvp = em.now("v")
        em.wait(nc.gpsimd, tvp)
        em.wait(SP, tvp)
        tg1 = em.inc(nc.gpsimd.indirect_dma_start(
            SM[:, 0:D], None, x_st,
            bass.IndirectOffsetOnAxis(IXC[:, gq:gq + 1], 0)), "dgx", 16)
        tg2 = em.inc(nc.gpsimd.indirect_dma_start(
            SM2[:, 0:D], None, x0,
            bass.IndirectOffsetOnAxis(IXC[:, gq:gq + 1], 0)), "dgx", 16)
        ta = em.inc(SP.dma_start(out=XOLD[:, 0:D],
                                 in_=acc_st[gq * 128:(gq + 1) * 128, :]), "dxo", 16)
        V.wait(tg2)
        V.wait(ta)
        V.tensor_tensor(XNEW[:, 0:D], SM[:, 0:D], SM2[:, 0:D], OP.add)
        V.tensor_tensor(XNEW[:, 0:D], XNEW[:, 0:D], XOLD[:, 0:D], OP.add)
        tv = V.tok()
        em.wait(SP, tv)
        tw = em.inc(SP.dma_start(out=out[gq * 128:(gq + 1) * 128, :], in_=XNEW[:, 0:D]),
                    "dout", 16)
        V.wait(tw)
    em.wait(nc.gpsimd, em.now("dout"))

    ctx.close()
    return nc



def _rope_tables(n):
    inv = (1.0 / np.power(np.float32(10000.0),
                          (np.arange(0, HD, 2, dtype=np.float32) / np.float32(HD)).astype(np.float32))
           ).astype(np.float32)
    f = (np.arange(n, dtype=np.float32)[:, None] * inv[None, :]).astype(np.float32)
    cos = np.cos(f).astype(np.float32)
    sin = np.sin(f).astype(np.float32)
    rows = (np.arange(128) % HD) // 2
    crep = cos.T[rows, :]
    srep = sin.T[rows, :]
    return np.ascontiguousarray(crep), np.ascontiguousarray(srep)


def _pmat():
    P = np.zeros((128, 128), np.float32)
    for i in range(64):
        P[2 * i, 2 * i + 1] = -1.0
        P[2 * i + 1, 2 * i] = 1.0
    return np.ascontiguousarray(P.T).astype(ml_dtypes.bfloat16)


def _interleave_fc1_cols(wT):
    a = wT[:, 0:HID].reshape(wT.shape[0], HID // 128, 128)
    g = wT[:, HID:2 * HID].reshape(wT.shape[0], HID // 128, 128)
    inter = np.stack([a, g], axis=2)
    return np.ascontiguousarray(inter.reshape(wT.shape[0], 2 * HID))


def _interleave_fc1_bias(b):
    a = b[0:HID].reshape(HID // 128, 128)
    g = b[HID:2 * HID].reshape(HID // 128, 128)
    return np.ascontiguousarray(np.stack([a, g], axis=1).reshape(2 * HID))


def stage_inputs(inputs, NTOK, LAYERS, OWN, n_cores=8):
    f32 = np.float32
    L = LAYERS
    qkv_w = np.asarray(inputs["L0_attn_qkv_w"], f32)[:L]
    gate_w = np.asarray(inputs["L0_attn_gate_w"], f32)[:L]
    proj_w = np.asarray(inputs["L0_attn_proj_w"], f32)[:L]
    fc1_w = np.asarray(inputs["L0_attn_fc1_w"], f32)[:L]
    fc2_w = np.asarray(inputs["L0_attn_fc2_w"], f32)[:L]
    com = {
        "wqkvT": np.ascontiguousarray(qkv_w.transpose(0, 2, 1)),
        "wgateT": np.ascontiguousarray(gate_w.transpose(0, 2, 1)),
        "wprojT": np.ascontiguousarray(proj_w.transpose(0, 2, 1)),
        "wfc1T": np.ascontiguousarray(
            np.stack([_interleave_fc1_cols(np.ascontiguousarray(w.T)) for w in fc1_w])),
        "wfc2T": np.ascontiguousarray(fc2_w.transpose(0, 2, 1)),
        "bqkv": np.asarray(inputs["L0_attn_qkv_b"], f32)[:L],
        "bgate": np.asarray(inputs["L0_attn_gate_b"], f32)[:L],
        "bproj": np.asarray(inputs["L0_attn_proj_b"], f32)[:L],
        "bfc1": np.stack([_interleave_fc1_bias(np.asarray(b, f32))
                          for b in np.asarray(inputs["L0_attn_fc1_b"], f32)[:L]]),
        "bfc2": np.asarray(inputs["L0_attn_fc2_b"], f32)[:L],
        "norm1w": np.asarray(inputs["L0_attn_norm1_w"], f32)[:L],
        "norm2w": np.asarray(inputs["L0_attn_norm2_w"], f32)[:L],
        "anormwT": np.ascontiguousarray(
            np.asarray(inputs["L0_attn_attn_norm_w"], f32)[:L]
            .reshape(L, HEADS, HD).transpose(0, 2, 1)),
        "identb": np.eye(128, dtype=ml_dtypes.bfloat16),
        "identf": np.eye(128, dtype=np.float32),
        "pmatb": _pmat(),
        "fgw": np.asarray(inputs["L0_ffn_gate_w"], f32),
        "fgb": np.asarray(inputs["L0_ffn_gate_b"], f32).reshape(1, FE),
        "frw1": np.asarray(inputs["L0_ffn_rw1"], f32),
        "frb1": np.asarray(inputs["L0_ffn_rb1"], f32),
        "frw2": np.asarray(inputs["L0_ffn_rw2"], f32),
        "frb2": np.asarray(inputs["L0_ffn_rb2"], f32),
        "frw3": np.asarray(inputs["L0_ffn_rw3"], f32),
        "frb3": np.asarray(inputs["L0_ffn_rb3"], f32),
        "fsw1": np.asarray(inputs["L0_ffn_sw1"], f32),
        "fsb1": np.asarray(inputs["L0_ffn_sb1"], f32),
        "fsw2": np.asarray(inputs["L0_ffn_sw2"], f32),
        "fsb2": np.asarray(inputs["L0_ffn_sb2"], f32),
        "fsw3": np.asarray(inputs["L0_ffn_sw3"], f32),
        "fsb3": np.asarray(inputs["L0_ffn_sb3"], f32),
    }
    crep_, srep_ = _rope_tables(NTOK)
    com["crep"] = crep_
    com["srep"] = srep_
    x = np.asarray(inputs["L0_attn_x"], f32)
    in_maps = []
    for c in range(n_cores):
        b = c // 4
        q = c % 4
        idx = (np.arange(128, dtype=np.int32)[:, None]
               + (q * OWN + np.arange(OWN // 128, dtype=np.int32)[None, :] * 128))
        m = dict(com)
        m["x0"] = np.ascontiguousarray(x[b, 0:NTOK, :])
        m["ownidx"] = np.ascontiguousarray(idx)
        in_maps.append(m)
    return in_maps


def kernel(**inputs):
    x = np.asarray(inputs["L0_attn_x"])
    B, NTOK, _ = x.shape
    LAYERS = np.asarray(inputs["L0_attn_norm1_w"]).shape[0]
    OWN = NTOK // 4
    in_maps = stage_inputs(inputs, NTOK, LAYERS, OWN, n_cores=8)
    nc = build_program(NTOK, LAYERS, OWN)
    res = run_bass_kernel_spmd(nc, in_maps, core_ids=list(range(8)))
    outp = np.empty((B, NTOK, D), np.float32)
    for c in range(8):
        b = c // 4
        q = c % 4
        outp[b, q * OWN:(q + 1) * OWN, :] = res.results[c]["out"]
    return outp


# version 05 (final)
# speedup vs baseline: 1.3198x; 1.0121x over previous
import numpy as np
import ml_dtypes
from contextlib import ExitStack

import concourse.bass as bass
import concourse.mybir as mybir
from concourse.bass_utils import run_bass_kernel_spmd

F32 = mybir.dt.float32
BF16 = mybir.dt.bfloat16
I16 = mybir.dt.int16
I32 = mybir.dt.int32
I8 = mybir.dt.int8
AF = mybir.ActivationFunctionType
OP = mybir.AluOpType
AX = mybir.AxisListType.X

D = 1024
DT = D // 128
HEADS = 16
HD = 64
CHUNK = 128
HID = 4096
FE, FS, FH = 8, 2, 2048
FHT = FH // 128
EPS_RMS = 1e-6
EPS_LN = 1e-5
SCALE_ATT = HD ** -0.5
AB = 1024
NPT = AB // 128


class Emit:
    def __init__(self, nc):
        self.nc = nc
        self.sems = {}
        self.count = {}
        self.last_wait = {}

    def sem(self, name):
        if name not in self.sems:
            self.sems[name] = self.nc.alloc_semaphore(name)
            self.count[name] = 0
        return self.sems[name]

    def inc(self, instr, name, n=1):
        instr.then_inc(self.sem(name), n)
        self.count[name] += n
        return (name, self.count[name])

    def wait(self, eng, tok):
        if tok is None:
            return
        name, val = tok
        if val <= 0:
            return
        key = (id(eng), name)
        if self.last_wait.get(key, 0) >= val:
            return
        eng.wait_ge(self.sem(name), val)
        self.last_wait[key] = val

    def now(self, name):
        return (name, self.count.get(name, 0))


class Chained:

    def __init__(self, em, eng, name, buddies=()):
        object.__setattr__(self, "em", em)
        object.__setattr__(self, "eng", eng)
        object.__setattr__(self, "name", name)
        object.__setattr__(self, "buddies", buddies)

    def __getattr__(self, attr):
        f = getattr(self.eng, attr)

        def call(*a, **k):
            em, eng, name = self.em, self.eng, self.name
            em.wait(eng, em.now(name))
            if not getattr(em, "relax", False):
                for b in self.buddies:
                    em.wait(eng, em.now(b))
            return em.inc(f(*a, **k), name, 1)

        return call

    def wait(self, tok):
        self.em.wait(self.eng, tok)

    def tok(self):
        return self.em.now(self.name)


B_STAGE = 99


def build_program(NTOK, LAYERS, OWN, phases="ABCF"):
    NBLK = NTOK // AB
    NOC = OWN // CHUNK
    nc = bass.Bass("TRN2", target_bir_lowering=False, debug=False)
    nc.num_devices = 8

    dr = lambda n, s, dt, k="ExternalInput": nc.dram_tensor(n, s, dt, kind=k).ap()
    x0 = dr("x0", [NTOK, D], F32)
    wqkvT = dr("wqkvT", [LAYERS, D, 3 * D], F32)
    wgateT = dr("wgateT", [LAYERS, D, D], F32)
    wprojT = dr("wprojT", [LAYERS, D, D], F32)
    wfc1T = dr("wfc1T", [LAYERS, D, 2 * HID], F32)
    wfc2T = dr("wfc2T", [LAYERS, HID, D], F32)
    bqkv = dr("bqkv", [LAYERS, 3 * D], F32)
    bgate = dr("bgate", [LAYERS, D], F32)
    bproj = dr("bproj", [LAYERS, D], F32)
    bfc1 = dr("bfc1", [LAYERS, 2 * HID], F32)
    bfc2 = dr("bfc2", [LAYERS, D], F32)
    norm1w = dr("norm1w", [LAYERS, D], F32)
    norm2w = dr("norm2w", [LAYERS, D], F32)
    anormwT = dr("anormwT", [LAYERS, 64, 16], F32)
    crep = dr("crep", [128, NTOK], F32)
    srep = dr("srep", [128, NTOK], F32)
    identb = dr("identb", [128, 128], BF16)
    identf = dr("identf", [128, 128], F32)
    pmatb = dr("pmatb", [128, 128], BF16)
    fgw = dr("fgw", [D, FE], F32)
    fgb = dr("fgb", [1, FE], F32)
    frw1 = dr("frw1", [FE, D, FH], F32)
    frb1 = dr("frb1", [FE, FH], F32)
    frw2 = dr("frw2", [FE, D, FH], F32)
    frb2 = dr("frb2", [FE, FH], F32)
    frw3 = dr("frw3", [FE, FH, D], F32)
    frb3 = dr("frb3", [FE, D], F32)
    fsw1 = dr("fsw1", [FS, D, FH], F32)
    fsb1 = dr("fsb1", [FS, FH], F32)
    fsw2 = dr("fsw2", [FS, D, FH], F32)
    fsb2 = dr("fsb2", [FS, FH], F32)
    fsw3 = dr("fsw3", [FS, FH, D], F32)
    fsb3 = dr("fsb3", [FS, D], F32)
    ownidx = dr("ownidx", [128, NOC], I32)
    out = dr("out", [OWN, D], F32, "ExternalOutput")

    x_st = dr("x_st", [NTOK, D], F32, "Internal")
    xn_st = dr("xn_st", [NTOK, D], BF16, "Internal")
    acc_st = dr("acc_st", [OWN, D], F32, "Internal")
    swq_st = dr("swq_st", [D, 3 * D], BF16, "Internal")
    swg_st = dr("swg_st", [D, D], BF16, "Internal")
    swp_st = dr("swp_st", [D, D], BF16, "Internal")
    sw1_st = dr("sw1_st", [D, 2 * HID], BF16, "Internal")
    sw2_st = dr("sw2_st", [HID, D], BF16, "Internal")
    row_st = dr("row_st", [1, 2048], F32, "Internal")
    qtd = dr("qtd", [DT, 128, 128], BF16, "Internal")
    ktd = dr("ktd", [DT, 128, 128], BF16, "Internal")
    vtd = dr("vtd", [DT, 128, 128], BF16, "Internal")

    em = Emit(nc)
    ctx = ExitStack()
    sb = lambda name, shape, dt: ctx.enter_context(nc.sbuf_tensor(name, shape, dt))
    _uid = [0]

    def pst(nm, shape, dt):
        _uid[0] += 1
        return nc.psum_tensor(f"{nm}_{_uid[0]}", shape, dt)

    def fence():
        for nm in ("pe", "v", "s", "g"):
            em.wait(nc.sync, em.now(nm))

    def gfence():
        for nm in ("pe", "v", "s"):
            em.wait(nc.gpsimd, em.now(nm))

    IDENT = sb("IDENT", [128, 128], BF16)
    IDENTF = sb("IDENTF", [128, 128], F32)
    PMAT = sb("PMAT", [128, 128], BF16)
    ONECF = sb("ONECF", [128, 1], F32)
    ONE1F = sb("ONE1F", [1, 1], F32)
    ONESB = sb("ONESB", [128, 1], BF16)
    NWB = sb("NWB", [128, 2 * D], F32)
    NWAT = sb("NWAT", [64, 16], F32)
    BIASF = sb("BIASF", [128, 2 * D], F32)
    BCOL = sb("BCOL", [128, 96], F32)
    BCF = sb("BCF", [128, 48], F32)
    WSC = sb("WSC", [128, 8], F32)
    KV = sb("KV", [128, 8 * 128], F32)
    KVB = sb("KVB", [128, 8 * 128], BF16)
    DECAY = sb("DECAY", [128, 8 * NPT], F32)
    ST = sb("ST", [128, 176], F32)
    ROWS = sb("ROWS", [1, 2048], F32)
    XSB = sb("XSB", [128, AB], F32)
    SM = sb("SM", [128, 2 * D], F32)
    SM2 = sb("SM2", [128, 2 * D], F32)
    XI16 = sb("XI16", [128, D], I16)
    XQN = sb("XQN", [128, D], BF16)
    KNAT = sb("KNAT", [128, D], BF16)
    VNC = sb("VNC", [128, D], BF16)
    XOLD = sb("XOLD", [128, D], F32)
    XNEW = sb("XNEW", [128, D], F32)
    IXC = sb("IXC", [128, NOC], I32)
    WEB = sb("WEB", [128, 128], F32)
    BIGB = sb("BIGB", [128, 32768], BF16)
    WAR = sb("WAR", [128, 16384], BF16)
    WF32 = sb("WF32", [128, D], F32)
    VL4B = sb("VL4B", [128, 8192], BF16)
    UPC = sb("UPC", [128, 1024], BF16)
    XQED = sb("XQED", [128, 2048], BF16)

    XQT = BIGB[:, 0:8192]
    QR = BIGB[:, 8192:16384]
    KR = BIGB[:, 16384:24576]
    VTR = BIGB[:, 24576:32768]
    XQ2T = WAR[:, 8192:16384]
    BIGF = BIGB.bitcast(F32)
    WARF = WAR.bitcast(F32)
    CREP = WARF[:, 4096:5120]
    SREP = WARF[:, 5120:6144]

    SP = nc.sync

    class _PE:
        def __init__(self):
            self.last = None

        def matmul(self, *a, **k):
            self.last = nc.tensor.matmul(*a, **k)
            return self.last

        def transpose(self, *a, **k):
            self.last = nc.tensor.transpose(*a, **k)
            return self.last

        def wait_ge(self, sem, val):
            return nc.tensor.wait_ge(sem, val)

        def mark(self):
            return em.inc(self.last, "pe")

    PE = _PE()
    V = Chained(em, nc.vector, "v", buddies=("s",))
    S = Chained(em, nc.scalar, "s", buddies=("v",))
    GC = Chained(em, nc.gpsimd, "g")

    from contextlib import contextmanager

    @contextmanager
    def relaxed():
        em.relax = True
        try:
            yield
        finally:
            em.relax = False

    class Cols:
        def __init__(self):
            self.n = 0

        def take(self, k):
            a = self.n
            self.n += k
            assert self.n <= 176, self.n
            return a

    CL = Cols()
    c_sum = CL.take(NPT)
    c_msq = CL.take(NPT)
    c_rstd = CL.take(NPT)
    c_m1 = CL.take(NPT)
    c_xs = CL.take(NPT)
    c_qsc = CL.take(NPT)
    c_x127 = CL.take(NPT)
    c_t0 = CL.take(NPT)
    c_t1 = CL.take(NPT)
    c_xsa = CL.take(NPT)
    c_amax = CL.take(4)
    c_abias = CL.take(4)
    c_den = CL.take(4)
    c_rden = CL.take(4)
    c_wacc = CL.take(1)
    c_wtmp = CL.take(1)
    c_ln = CL.take(5 * NPT)

    def stc(c0, k=1):
        return ST[:, c0:c0 + k]

    def bcast_row(row_ap, n, dst_ap, ps_bc):
        for j in range(0, n, 512):
            w = min(512, n - j)
            PE.matmul(ps_bc[:, j % 512:j % 512 + w] if False else ps_bc[:, 0:w],
                      ONECF[0:1, 0:1].broadcast_to([1, 128]),
                      row_ap[0:1, j:j + w], start=True, stop=True)
            tpe = PE.mark()
            S.wait(tpe)
            S.copy(dst_ap[:, j:j + w], ps_bc[:, 0:w])
            ts = S.tok()
            em.wait(PE, ts)
        return S.tok()

    def col_from_rows(row_ap, start, ncols, dst_cols, ps_c):
        for j in range(ncols):
            PE.matmul(ps_c[:, j:j + 1],
                      row_ap[0:1, start + j * 128:start + (j + 1) * 128],
                      ONE1F[0:1, 0:1], start=True, stop=True)
        tpe = PE.mark()
        V.wait(tpe)
        V.tensor_copy(dst_cols, ps_c[:, 0:ncols])
        tv = V.tok()
        em.wait(PE, tv)
        return tv

    def cols_to_row(col_block, ncols, dst_row):
        em.wait(SP, em.now("v"))
        for nm in ("dob", "drw"):
            if em.count.get(nm, 0):
                em.wait(SP, em.now(nm))
        t1 = em.inc(SP.dma_start(
            out=row_st.rearrange("a b -> (a b)")[0:128 * ncols].rearrange(
                "(c p) -> p c", p=128),
            in_=col_block, allow_slow_non_contiguous=True), "dob", 16)
        em.wait(SP, t1)
        t2 = em.inc(SP.dma_start(out=dst_row, in_=row_st[0:1, 0:128 * ncols]), "dob", 16)
        return t2

    def row_to_cols(src_row, ncols, dst_cols):
        em.wait(SP, em.now("v"))
        for nm in ("dob", "drw"):
            if em.count.get(nm, 0):
                em.wait(SP, em.now(nm))
        t1 = em.inc(SP.dma_start(out=row_st[0:1, 0:128 * ncols], in_=src_row), "dob", 16)
        em.wait(SP, t1)
        t2 = em.inc(SP.dma_start(
            out=dst_cols,
            in_=row_st.rearrange("a b -> (a b)")[0:128 * ncols].rearrange(
                "(c p) -> p c", p=128), allow_slow_non_contiguous=True), "dob", 16)
        return t2

    def rmsnorm_quant_chunk(xin_f32, nwb_ap, ci, out_bf):
        xw = SM2[:, 0:D]
        V.tensor_tensor(xw, xin_f32, nwb_ap, OP.mult)
        tv = V.tok()
        S.wait(tv)
        S.square(SM2[:, D:2 * D], xin_f32)
        tsq = S.tok()
        V.wait(tsq)
        V.reduce_sum(stc(c_sum + ci), SM2[:, D:2 * D], axis=AX)
        V.reduce_max(stc(c_m1 + ci), xw, axis=AX, apply_absolute_value=True)
        V.tensor_scalar(stc(c_t0 + ci), stc(c_sum + ci), 1.0 / D, EPS_RMS, OP.mult, OP.add)
        V.reciprocal(stc(c_t1 + ci), stc(c_t0 + ci))
        tv = V.tok()
        S.wait(tv)
        S.sqrt(stc(c_rstd + ci), stc(c_t1 + ci))
        ts = S.tok()
        V.wait(ts)
        V.tensor_tensor(stc(c_xs + ci), stc(c_rstd + ci), stc(c_m1 + ci), OP.mult)
        V.tensor_scalar(stc(c_xs + ci), stc(c_xs + ci), 1e-5, None, OP.max)
        V.reciprocal(stc(c_t0 + ci), stc(c_xs + ci))
        V.tensor_tensor(stc(c_t1 + ci), stc(c_t0 + ci), stc(c_rstd + ci), OP.mult)
        V.tensor_scalar(stc(c_qsc + ci), stc(c_t1 + ci), 127.0, None, OP.mult)
        V.tensor_scalar(stc(c_x127 + ci), stc(c_xs + ci), 1.0 / 127.0, None, OP.mult)
        tv = V.tok()
        S.wait(tv)
        S.activation(XI16[:, 0:D], xw, AF.Copy, scale=stc(c_qsc + ci))
        ts = S.tok()
        V.wait(ts)
        V.wait(em.now("pe"))
        V.tensor_copy(out_bf, XI16[:, 0:D])
        return V.tok()

    def transpose_chunk(src_bf, ps_tr, col_map):
        tprev = em.now("s")
        em.wait(PE, tprev)
        em.wait(PE, em.now("v"))
        for f in range(DT):
            PE.transpose(ps_tr[:, f * 128:(f + 1) * 128],
                         src_bf[:, f * 128:(f + 1) * 128], IDENT[:])
        tpe = PE.mark()
        S.wait(tpe)
        for f in range(DT):
            S.copy(col_map(f), ps_tr[:, f * 128:(f + 1) * 128])
        return S.tok()

    tc = None
    for dst, src in ((IDENT[:], identb), (IDENTF[:], identf), (PMAT[:], pmatb), (IXC[:], ownidx)):
        tc = em.inc(SP.dma_start(out=dst, in_=src), "dcst", 16)
    V.wait(tc)
    V.memset(ONECF[:], 1.0)
    V.memset(ONE1F[:], 1.0)
    V.memset(ONESB[:], 1.0)

    for blk in range(NBLK):
        fence()
        ld = []
        for c in range(NPT):
            t0 = blk * AB + c * 128
            ld.append(em.inc(SP.dma_start(out=BIGF[:, c * D:(c + 1) * D],
                                          in_=x0[t0:t0 + 128, :]), "dpre", 16))
        for c in range(NPT):
            xc = BIGF[:, c * D:(c + 1) * D]
            V.wait(ld[-1])
            V.reduce_sum(stc(c_ln + c), xc, axis=AX)
            tv = V.tok()
            S.wait(tv)
            S.square(SM2[:, 0:D], xc)
            ts = S.tok()
            V.wait(ts)
            V.reduce_sum(stc(c_ln + NPT + c), SM2[:, 0:D], axis=AX)
        cM = stc(c_ln + 2 * NPT, NPT)
        cV_ = stc(c_ln + 3 * NPT, NPT)
        cR = stc(c_ln + 4 * NPT, NPT)
        V.tensor_scalar(cM, stc(c_ln, NPT), 1.0 / D, None, OP.mult)
        V.tensor_tensor(cV_, cM, cM, OP.mult)
        V.tensor_scalar(stc(c_ln + NPT, NPT), stc(c_ln + NPT, NPT), 1.0 / D, None, OP.mult)
        V.tensor_tensor(cV_, stc(c_ln + NPT, NPT), cV_, OP.subtract)
        V.tensor_scalar(cV_, cV_, EPS_LN, None, OP.add)
        V.reciprocal(cV_, cV_)
        tv = V.tok()
        S.wait(tv)
        S.sqrt(cR, cV_)
        ts = S.tok()
        V.wait(ts)
        V.tensor_tensor(cM, cM, cR, OP.mult)
        V.tensor_scalar(cM, cM, -1.0, None, OP.mult)
        for c in range(NPT):
            t0 = blk * AB + c * 128
            xc = BIGF[:, c * D:(c + 1) * D]
            V.tensor_scalar(SM2[:, 0:D], xc, cR[:, c:c + 1], cM[:, c:c + 1],
                            OP.mult, OP.add)
            ta = V.tok()
            em.wait(SP, ta)
            ts1 = em.inc(SP.dma_start(out=x_st[t0:t0 + 128, :], in_=SM2[:, 0:D]), "dxs", 16)
            V.tensor_copy(XQN[:, 0:D], SM2[:, 0:D])
            tb = V.tok()
            em.wait(SP, tb)
            ts2 = em.inc(SP.dma_start(out=xn_st[t0:t0 + 128, :], in_=XQN[:, 0:D]), "dxn", 16)
            V.wait(ts1)
            V.wait(ts2)

    def prep_weights(l):
        specs = [(wqkvT[l], swq_st, D, 3 * D, 0), (wgateT[l], swg_st, D, D, 1),
                 (wprojT[l], swp_st, D, D, 2), (wfc1T[l], sw1_st, D, 2 * HID, 3),
                 (wfc2T[l], sw2_st, HID, D, 4)]
        with pst("pw_ps", [128, 8], F32) as pw_ps:
            em.wait(nc.scalar, em.now("dxn"))
            em.wait(nc.scalar, em.now("dxs"))
            s_tok = {0: em.now("s"), 1: em.now("s")}
            v_tok = {0: em.now("v"), 1: em.now("v")}
            st_tok = {0: em.now("dsw"), 1: em.now("dsw")}
            bi = [0]
            with relaxed():
                for wap, st_ap, krows, ncols, k in specs:
                    V.memset(stc(c_wacc), 0.0)
                    for kt in range(krows // 128):
                        for j0 in range(0, ncols, 512):
                            w = min(512, ncols - j0)
                            slot = bi[0] % 2
                            bi[0] += 1
                            co = slot * 512
                            em.wait(SP, s_tok[slot])
                            em.wait(SP, v_tok[slot])
                            tl = em.inc(SP.dma_start(
                                out=WF32[:, co:co + w],
                                in_=wap[kt * 128:(kt + 1) * 128, j0:j0 + w]),
                                "dwf" if slot == 0 else "dwf2", 16)
                            S.wait(tl)
                            S.wait(st_tok[slot])
                            S.sign(XQN[:, co:co + w], WF32[:, co:co + w])
                            s_tok[slot] = S.tok()
                            V.wait(tl)
                            V.reduce_sum(stc(c_wtmp), WF32[:, co:co + w], axis=AX,
                                          apply_absolute_value=True)
                            V.tensor_tensor(stc(c_wacc), stc(c_wacc), stc(c_wtmp), OP.add)
                            v_tok[slot] = V.tok()
                            em.wait(nc.gpsimd, s_tok[slot])
                            st_tok[slot] = em.inc(nc.gpsimd.dma_start(
                                out=st_ap[kt * 128:(kt + 1) * 128, j0:j0 + w],
                                in_=XQN[:, co:co + w]),
                                "dsw" if slot == 0 else "dsw2", 16)
                    tg = V.tok()
                    em.wait(PE, tg)
                    PE.matmul(pw_ps[0:1, 0:1], ONECF[:], stc(c_wacc), start=True, stop=True)
                    tp = PE.mark()
                    V.wait(tp)
                    V.tensor_scalar(ROWS[0:1, 0:1], pw_ps[0:1, 0:1],
                                    1.0 / (krows * ncols), 1e-5, OP.mult, OP.max)
                    tv = V.tok()
                    em.wait(PE, tv)
                    PE.matmul(pw_ps[:, k:k + 1], ONE1F[0:1, 0:1].broadcast_to([1, 128]),
                              ROWS[0:1, 0:1], start=True, stop=True)
                    tp = PE.mark()
                    V.wait(tp)
                    V.tensor_copy(WSC[:, k:k + 1], pw_ps[:, k:k + 1])
                    tv = V.tok()
                    em.wait(PE, tv)
        em.wait(SP, em.now("dsw"))
        em.wait(SP, em.now("dsw2"))
        em.wait(nc.gpsimd, em.now("dsw"))
        em.wait(nc.gpsimd, em.now("dsw2"))

    def layer_consts(l):
        with pst("lc_bc", [128, 512], F32) as lc_bc, \
             pst("lc_c", [128, 96], F32) as lc_c:
            r1 = lambda ap: ap.rearrange("(a b) -> a b", a=1)

            def ld_row(src, n):
                tp = em.now("pe")
                em.wait(SP, tp)
                for nm in ("dob", "drw"):
                    if em.count.get(nm, 0):
                        em.wait(SP, em.now(nm))
                tl = em.inc(SP.dma_start(out=ROWS[0:1, 0:n], in_=src), "drw", 16)
                em.wait(PE, tl)

            for idx, row_ap in enumerate((norm1w[l], norm2w[l])):
                ld_row(r1(row_ap), D)
                bcast_row(ROWS, D, NWB[:, idx * D:(idx + 1) * D], lc_bc)
            for idx, row_ap in enumerate((r1(bproj[l]), r1(bfc2[l]))):
                ld_row(row_ap, D)
                bcast_row(ROWS, D, BIASF[:, idx * D:(idx + 1) * D], lc_bc)
            tg0 = em.now("v")
            em.wait(SP, tg0)
            tnw = em.inc(SP.dma_start(out=NWAT[:], in_=anormwT[l]), "dnw", 16)
            V.wait(tnw)
            ld_row(r1(bqkv[l])[0:1, 0:2 * D], 2 * D)
            col_from_rows(ROWS, 0, 16, BCOL[:, 0:16], lc_c)
            ld_row(r1(bqkv[l])[0:1, 2 * D:3 * D], D)
            col_from_rows(ROWS, 0, 8, BCOL[:, 16:24], lc_c)
            ld_row(r1(bgate[l]), D)
            col_from_rows(ROWS, 0, 8, BCOL[:, 24:32], lc_c)
            for piece in range(4):
                ld_row(r1(bfc1[l])[0:1, piece * 2048:(piece + 1) * 2048], 2048)
                col_from_rows(ROWS, 0, 16, BCOL[:, 32 + piece * 16:48 + piece * 16], lc_c)

    def stream_panel(dst_view, src_st, kt_rows, col0, ncols, kw=512, sem="dwA"):
        eng = SP if sem == "dwA" else nc.gpsimd
        em.wait(eng, em.now("pe"))
        tld = None
        for kt in range(kt_rows):
            tld = em.inc(eng.dma_start(
                out=dst_view[:, kt * kw:kt * kw + ncols],
                in_=src_st[kt * 128:(kt + 1) * 128, col0:col0 + ncols]), sem, 16)
        return tld

    def phase_A(l, blk):
        base = blk * AB
        with pst("pa_mm", [128, 512], F32) as pa_mm, \
             pst("pa_mm2", [128, 512], F32) as pa_mm2, \
             pst("pa_tr", [128, 1024], BF16) as pa_tr, \
             pst("pa_bc", [128, 512], F32) as pa_bc:
            fence()
            em.wait(SP, em.now("dxs"))
            trc = em.inc(SP.dma_start(out=CREP[:], in_=crep[:, base:base + AB]), "dcr", 16)
            trs = em.inc(SP.dma_start(out=SREP[:], in_=srep[:, base:base + AB]), "dcr", 16)
            for c in range(NPT):
                t0 = base + c * 128
                tv0 = em.now("v")
                em.wait(SP, tv0)
                tl = em.inc(SP.dma_start(out=XOLD[:, 0:D], in_=x_st[t0:t0 + 128, :]),
                            "dxo", 16)
                V.wait(tl)
                rmsnorm_quant_chunk(XOLD[:, 0:D], NWB[:, 0:D], c, XQN[:, 0:D])
                transpose_chunk(XQN[:, 0:D], pa_tr,
                                lambda f, c=c: XQT[:, f * AB + c * 128:f * AB + (c + 1) * 128])
            trow = cols_to_row(stc(c_x127, NPT), NPT, ROWS[0:1, 0:AB])
            em.wait(PE, trow)
            bcast_row(ROWS, AB, XSB, pa_bc)
            txsb = em.now("s")
            so = {0: em.now("s"), 1: em.now("s")}
            pv = {0: em.now("v"), 1: em.now("v")}
            with relaxed():
                for si in range(6):
                    woff = (si % 2) * 4096
                    em.wait(SP, em.now("pe"))
                    tld = stream_panel(WAR[:, woff:woff + 4096], swq_st, DT, si * 512, 512,
                                       sem="dwA" if si % 2 == 0 else "dwB")
                    em.wait(PE, tld)
                    for fb in range(4):
                        fglob = si * 4 + fb
                        dstb = QR if fglob < 8 else (KR if fglob < 16 else VTR)
                        floc = fglob % 8
                        for t2 in range(2):
                            dst = pa_mm if t2 == 0 else pa_mm2
                            em.wait(PE, pv.get(t2))
                            for kt in range(DT):
                                PE.matmul(dst[:, 0:512],
                                          WAR[:, woff + kt * 512 + fb * 128:woff + kt * 512 + (fb + 1) * 128],
                                          XQT[:, kt * AB + t2 * 512:kt * AB + (t2 + 1) * 512],
                                          start=(kt == 0), stop=(kt == DT - 1))
                            tpe = PE.mark()
                            V.wait(tpe)
                            V.wait(txsb)
                            V.wait(so.get(t2))
                            V.scalar_tensor_tensor(SM[:, t2 * 512:(t2 + 1) * 512],
                                                   dst[:, 0:512], WSC[:, 0:1],
                                                   XSB[:, t2 * 512:(t2 + 1) * 512],
                                                   OP.mult, OP.mult)
                            tv = V.tok()
                            pv[t2] = tv
                            S.wait(tv)
                            S.activation(dstb[:, floc * AB + t2 * 512:floc * AB + (t2 + 1) * 512],
                                         SM[:, t2 * 512:(t2 + 1) * 512], AF.Identity,
                                         bias=BCOL[:, fglob:fglob + 1])
                            so[t2] = S.tok()
            sg = {0: em.now("s"), 1: em.now("s")}
            vg = {0: em.now("v"), 1: em.now("v")}
            with relaxed():
                for gh in range(2):
                    woff = (gh % 2) * 4096
                    em.wait(SP, em.now("pe"))
                    tld = stream_panel(WAR[:, woff:woff + 4096], swg_st, DT, gh * 512, 512,
                                       sem="dwA" if gh % 2 == 0 else "dwB")
                    em.wait(PE, tld)
                    for fb in range(4):
                        fglob = gh * 4 + fb
                        for t2 in range(2):
                            dst = pa_mm if t2 == 0 else pa_mm2
                            em.wait(PE, pv.get(t2))
                            for kt in range(DT):
                                PE.matmul(dst[:, 0:512],
                                          WAR[:, woff + kt * 512 + fb * 128:woff + kt * 512 + (fb + 1) * 128],
                                          XQT[:, kt * AB + t2 * 512:kt * AB + (t2 + 1) * 512],
                                          start=(kt == 0), stop=(kt == DT - 1))
                            tpe = PE.mark()
                            V.wait(tpe)
                            V.wait(sg.get(t2))
                            V.scalar_tensor_tensor(SM[:, t2 * 512:(t2 + 1) * 512],
                                                   dst[:, 0:512], WSC[:, 1:2],
                                                   XSB[:, t2 * 512:(t2 + 1) * 512],
                                                   OP.mult, OP.mult)
                            tv = V.tok()
                            pv[t2] = tv
                            S.wait(tv)
                            S.wait(vg.get(t2))
                            S.activation(SM2[:, t2 * 512:(t2 + 1) * 512],
                                         SM[:, t2 * 512:(t2 + 1) * 512], AF.Sigmoid,
                                         bias=BCOL[:, 24 + fglob:25 + fglob])
                            ts = S.tok()
                            sg[t2] = ts
                            V.wait(ts)
                            for cc in range(4):
                                c = t2 * 4 + cc
                                V.reduce_sum(DECAY[:, fglob * NPT + c:fglob * NPT + c + 1],
                                             SM2[:, t2 * 512 + cc * 128:t2 * 512 + (cc + 1) * 128],
                                             axis=AX)
                            vg[t2] = V.tok()
            V.tensor_scalar(DECAY[:, 0:8 * NPT], DECAY[:, 0:8 * NPT],
                            1.0 / CHUNK, None, OP.mult)
            V.wait(trs)
            for buf in (QR, KR):
                for f in range(DT):
                    tev = em.now("s")
                    em.wait(PE, tev)
                    for t2 in range(2):
                        dst = pa_mm if t2 == 0 else pa_mm2
                        PE.matmul(dst[:, 0:512], PMAT[:],
                                  buf[:, f * AB + t2 * 512:f * AB + (t2 + 1) * 512],
                                  start=True, stop=True)
                    tpe = PE.mark()
                    S.wait(tpe)
                    S.copy(XQN[:, 0:512], pa_mm[:, 0:512])
                    S.copy(XQN[:, 512:1024], pa_mm2[:, 0:512])
                    tsh = S.tok()
                    V.tensor_tensor(SM2[:, 0:AB], buf[:, f * AB:(f + 1) * AB], CREP[:], OP.mult)
                    V.wait(tsh)
                    V.tensor_tensor(SM2[:, AB:2 * AB], XQN[:, 0:AB], SREP[:], OP.mult)
                    V.tensor_tensor(buf[:, f * AB:(f + 1) * AB], SM2[:, 0:AB],
                                    SM2[:, AB:2 * AB], OP.add)
            tv = V.tok()
            em.wait(PE, tv)

    def phase_B(l, blk, pbstate):
        base = blk * AB
        Q5 = BIGB[0:64, 0:2048].rearrange("p (h q) -> p h q", h=16)
        K5 = BIGB[0:64, 2048:4096].rearrange("p (h q) -> p h q", h=16)
        TQKV = BIGF[0:64, 2048:4096]
        TQKV3 = TQKV.rearrange("p (h q) -> p q h", h=16)
        AVT = SM[0:64, 0:2048]
        AVT3 = AVT.rearrange("p (q h) -> p q h", h=16)
        XWT = SM2[0:64, 0:2048]
        XWT3 = XWT.rearrange("p (q h) -> p q h", h=16)
        V43 = VL4B[0:16, 0:8192].rearrange("p (e q) -> p e q", q=128)
        P1 = WF32[0:64, 0:128]
        M1 = WF32[0:64, 128:256]
        QSCS = WF32[0:64, 256:384]
        RDB = XQN[0:1, 0:1024]
        QROW = ROWS[0:1, 0:128]
        KV3 = KV[:].rearrange("r (p q) -> r p q", q=128)
        DC3 = DECAY[:].rearrange("r (p n) -> r p n", n=NPT)
        XQE3 = XQED.rearrange("p (q h) -> p h q", h=16)

        def hsl(r0, r1, h):
            return XQE3[r0:r1, h:h + 1, :].rearrange("p o q -> p (o q)")

        def lsl(v3, lq, n):
            return v3[:, :, lq:lq + 1].rearrange(f"p {n} o -> p ({n} o)")

        with pst("pb_kn", [128, 1024], BF16) as pb_kn, \
             pst("pb_sp", [128, 1024], F32) as pb_sp, \
             pst("pb_dn", [128, 1024], F32) as pb_dn, \
             pst("pb_av", [128, 512], F32) as pb_av, \
             pst("pb_p1", [128, 512], F32) as pb_p1, \
             pst("pb_p2", [128, 512], F32) as pb_p2:
            fence()
            tld0 = stream_panel(WAR[:, 0:4096], swp_st, DT, 0, 512, sem="dwA")
            tld1 = stream_panel(WAR[:, 4096:8192], swp_st, DT, 512, 512, sem="dwB")
            gfence()
            pbstate["q5war"] = em.now("pe")
            for c in range(NPT):
                t0 = base + c * 128
                if em.count.get("dbn", 0):
                    em.wait(nc.gpsimd, em.now("dbn"))
                for dstd, srcb in ((qtd, QR), (ktd, KR), (vtd, VTR)):
                    em.inc(nc.gpsimd.dma_start(
                        out=dstd.rearrange("t r q -> r t q"),
                        in_=srcb.rearrange("r (t q) -> r t q", t=DT)[:, :, c * 128:(c + 1) * 128]),
                        "dbg", 16)
                if B_STAGE >= 4:
                    em.wait(SP, pbstate.get("q5war"))
                    em.wait(SP, em.now("dbg"))
                    tQ5 = em.inc(SP.dma_start(
                        out=BIGB[0:64, 0:2048].rearrange("p (h q) -> p h q", h=16),
                        in_=qtd.rearrange("t (hp d) q -> d (t hp) q", hp=2)), "dbn", 16)
                    tK5 = em.inc(SP.dma_start(
                        out=BIGB[0:64, 2048:4096].rearrange("p (h q) -> p h q", h=16),
                        in_=ktd.rearrange("t (hp d) q -> d (t hp) q", hp=2)), "dbn", 16)
                    tV4 = em.inc(SP.dma_start(
                        out=VL4B[0:16, 0:8192],
                        in_=vtd.rearrange("t (gp e) q -> (t gp) (e q)", gp=2)), "dbn", 16)
                V.wait(em.now("pe"))
                V.tensor_copy(KVB[:], KV[:])
                tkvb = V.tok()
                tprev = em.now("s")
                em.wait(PE, tprev)
                em.wait(PE, em.now("v"))
                for f in range(DT):
                    PE.transpose(pb_kn[:, f * 128:(f + 1) * 128],
                                 KR[:, f * AB + c * 128:f * AB + (c + 1) * 128], IDENT[:])
                tpe = PE.mark()
                S.wait(tpe)
                S.copy(KNAT[:, 0:D], pb_kn[:, 0:D])
                tkn = S.tok()
                em.wait(PE, tkn)
                for f in range(DT):
                    PE.transpose(pb_kn[:, f * 128:(f + 1) * 128],
                                 VTR[:, f * AB + c * 128:f * AB + (c + 1) * 128], IDENT[:])
                tpe = PE.mark()
                S.wait(tpe)
                S.copy(VNC[:, 0:D], pb_kn[:, 0:D])
                tvn = S.tok()
                if B_STAGE < 2:
                    continue
                em.wait(PE, tkvb)
                TQ4 = TQKV.rearrange("p (hh par q) -> p par hh q", hh=8, par=2)
                for par in range(2):
                    ro = par * 64
                    dst = pb_p1 if par == 0 else pb_p2
                    for hfq in range(2):
                        tev = em.now("s")
                        em.wait(PE, tev)
                        for j in range(4):
                            ft = hfq * 4 + j
                            PE.matmul(dst[0:64, j * 128:(j + 1) * 128],
                                      KVB[ro:ro + 64, ft * 128 + ro:ft * 128 + ro + 64],
                                      QR[ro:ro + 64, ft * AB + c * 128:ft * AB + (c + 1) * 128],
                                      start=True, stop=True)
                        tpe = PE.mark()
                        S.wait(tpe)
                        S.copy(TQ4[:, par:par + 1, hfq * 4:hfq * 4 + 4, :]
                               .rearrange("p o h q -> p (o h) q"),
                               dst[0:64, 0:512].rearrange("p (h q) -> p h q", h=4))
                if B_STAGE < 3:
                    continue
                em.wait(PE, tvn)
                em.wait(PE, em.now("v"))
                for p in range(8):
                    PE.matmul(pb_sp[:, p * 128:(p + 1) * 128],
                              KNAT[:, p * 128:(p + 1) * 128],
                              VNC[:, p * 128:(p + 1) * 128],
                              start=True, stop=True)
                tpe = PE.mark()
                V.wait(tpe)
                V.wait(tkvb)
                V.tensor_tensor(KV3, KV3,
                                DC3[:, :, c:c + 1].broadcast_to([128, 8, 128]),
                                OP.mult)
                V.tensor_tensor(KV[:], KV[:], pb_sp[0:128, 0:1024], OP.add)
                tkvu = V.tok()
                if B_STAGE < 5:
                    continue
                em.wait(PE, tV4)
                for s in range(2):
                    u0 = 0
                    tev = em.now("s")
                    em.wait(PE, tev)
                    em.wait(PE, em.now("v"))
                    for lo in range(64):
                        lq = s * 64 + lo
                        PE.matmul(pb_sp[0:16, lo * 16:(lo + 1) * 16],
                                  lsl(K5, lq, "h"), lsl(Q5, lq, "h"),
                                  start=True, stop=True)
                    tpe = PE.mark()
                    S.wait(tpe)
                    S.activation(UPC[u0:u0 + 16, 0:1024], pb_sp[0:16, 0:1024],
                                 AF.Exp, scale=SCALE_ATT)
                    tex = S.tok()
                    em.wait(PE, tex)
                    PE.matmul(pb_dn[0:1, 0:512], ONESB[0:16, 0:1],
                              UPC[u0:u0 + 16, 0:512], start=True, stop=True)
                    PE.matmul(pb_dn[0:1, 512:1024], ONESB[0:16, 0:1],
                              UPC[u0:u0 + 16, 512:1024], start=True, stop=True)
                    tpe = PE.mark()
                    V.wait(tpe)
                    with nc.allow_low_precision(
                            reason="softmax weights are bf16 matmul operands; "
                                   "bf16 denominator recip matches that precision"):
                        V.reciprocal(RDB, pb_dn[0:1, 0:1024])
                    trc = V.tok()
                    em.wait(PE, trc)
                    PE.matmul(pb_dn[0:16, 0:512], ONESB[0:1, 0:1].broadcast_to([1, 16]),
                              RDB[0:1, 0:512], start=True, stop=True)
                    PE.matmul(pb_dn[0:16, 512:1024], ONESB[0:1, 0:1].broadcast_to([1, 16]),
                              RDB[0:1, 512:1024], start=True, stop=True)
                    tpe = PE.mark()
                    V.wait(tpe)
                    V.tensor_tensor(UPC[u0:u0 + 16, 0:1024], UPC[u0:u0 + 16, 0:1024],
                                    pb_dn[0:16, 0:1024], OP.mult)
                    tA = V.tok()
                    em.wait(PE, tA)
                    for lo in range(32):
                        lq = s * 64 + lo
                        PE.matmul(pb_p1[0:64, lo * 16:(lo + 1) * 16],
                                  lsl(V43, lq, "e"),
                                  UPC[u0:u0 + 16, lo * 16:(lo + 1) * 16],
                                  start=True, stop=True)
                    tpeA = PE.mark()
                    for lo in range(32, 64):
                        lq = s * 64 + lo
                        PE.matmul(pb_av[0:64, (lo - 32) * 16:(lo - 31) * 16],
                                  lsl(V43, lq, "e"),
                                  UPC[u0:u0 + 16, lo * 16:(lo + 1) * 16],
                                  start=True, stop=True)
                    tpeB = PE.mark()
                    V.wait(tpeA)
                    V.tensor_tensor(AVT3[:, s * 64:s * 64 + 32, :],
                                    pb_p1[0:64, 0:512].rearrange("p (q h) -> p q h", h=16),
                                    TQKV3[:, s * 64:s * 64 + 32, :], OP.add)
                    V.wait(tpeB)
                    V.tensor_tensor(AVT3[:, s * 64 + 32:s * 64 + 64, :],
                                    pb_av[0:64, 0:512].rearrange("p (q h) -> p q h", h=16),
                                    TQKV3[:, s * 64 + 32:s * 64 + 64, :], OP.add)
                    tavs = V.tok()
                    em.wait(PE, tavs)
                pbstate["q5war"] = em.now("pe")
                if B_STAGE < 6:
                    continue
                S.square(XWT, AVT)
                tsq = S.tok()
                V.wait(tsq)
                V.reduce_sum(P1, XWT3, axis=AX)
                V.tensor_tensor(XWT3, AVT3,
                                NWAT.rearrange("p (o h) -> p o h", o=1).broadcast_to([64, 128, 16]),
                                OP.mult)
                V.reduce_max(M1, XWT3, axis=AX, apply_absolute_value=True)
                tv2 = V.tok()
                em.wait(PE, tv2)
                PE.transpose(pb_sp[:, 0:64], P1, IDENTF[0:64, 0:64])
                PE.transpose(pb_sp[:, 64:128], M1, IDENTF[0:64, 0:64])
                tpe = PE.mark()
                V.wait(tpe)
                V.reduce_sum(stc(c_sum + c), pb_sp[:, 0:64], axis=AX)
                V.reduce_max(stc(c_m1 + c), pb_sp[:, 64:128], axis=AX)
                V.tensor_scalar(stc(c_t0 + c), stc(c_sum + c), 1.0 / D, EPS_RMS,
                                OP.mult, OP.add)
                V.reciprocal(stc(c_t1 + c), stc(c_t0 + c))
                tv = V.tok()
                S.wait(tv)
                S.sqrt(stc(c_rstd + c), stc(c_t1 + c))
                ts = S.tok()
                V.wait(ts)
                V.tensor_tensor(stc(c_xs + c), stc(c_rstd + c), stc(c_m1 + c), OP.mult)
                V.tensor_scalar(stc(c_xs + c), stc(c_xs + c), 1e-5, None, OP.max)
                V.reciprocal(stc(c_t0 + c), stc(c_xs + c))
                V.tensor_tensor(stc(c_t1 + c), stc(c_t0 + c), stc(c_rstd + c), OP.mult)
                V.tensor_scalar(stc(c_qsc + c), stc(c_t1 + c), 127.0, None, OP.mult)
                V.tensor_scalar(stc(c_xsa + c), stc(c_xs + c), 1.0 / 127.0, None, OP.mult)
                tv = V.tok()
                em.wait(PE, tv)
                PE.transpose(pb_sp[0:1, 128:256], stc(c_qsc + c), IDENTF[:])
                tpe = PE.mark()
                S.wait(tpe)
                for nm in ("dob", "drw"):
                    if em.count.get(nm, 0):
                        em.wait(nc.scalar, em.now(nm))
                S.copy(QROW, pb_sp[0:1, 128:256])
                tqr = S.tok()
                em.wait(PE, tqr)
                PE.matmul(pb_sp[0:64, 256:384], ONE1F[0:1, 0:1].broadcast_to([1, 64]),
                          QROW, start=True, stop=True)
                tpe = PE.mark()
                S.wait(tpe)
                S.copy(QSCS, pb_sp[0:64, 256:384])
                tqs = S.tok()
                V.wait(tqs)
                V.tensor_tensor(XWT3, XWT3,
                                QSCS.rearrange("p (q o) -> p q o", o=1).broadcast_to([64, 128, 16]),
                                OP.mult)
                tv = V.tok()
                V.wait(em.now("dxq"))
                V.wait(em.now("pe"))
                for hf in range(2):
                    S.wait(tv)
                    S.activation(XI16[0:64, 0:D], XWT[:, hf * D:(hf + 1) * D], AF.Copy)
                    ts = S.tok()
                    V.wait(ts)
                    V.tensor_copy(XQED[0:64, hf * D:(hf + 1) * D], XI16[0:64, 0:D])
                tqe = V.tok()
                if B_STAGE < 7:
                    continue
                em.wait(SP, tqe)
                tdup = em.inc(SP.dma_start(out=XQED[64:128, 0:2048],
                                           in_=XQED[0:64, 0:2048]), "dxq", 16)
                tv0 = em.now("v")
                em.wait(SP, tv0)
                tox = em.inc(SP.dma_start(out=XOLD[:, 0:D], in_=x_st[t0:t0 + 128, :]),
                             "dxo", 16)
                if c == 0:
                    em.wait(PE, tld0)
                    em.wait(PE, tld1)
                em.wait(PE, tqe)
                em.wait(PE, tdup)
                for fh in range(2):
                    tev = em.now("v")
                    em.wait(PE, tev)
                    for j in range(8):
                        PE.matmul(pb_p1[:, 0:512],
                                  hsl(0, 64, 2 * j),
                                  WAR[0:64, fh * 4096 + j * 512:fh * 4096 + (j + 1) * 512],
                                  start=(j == 0), stop=(j == 7))
                    for j in range(8):
                        PE.matmul(pb_p2[:, 0:512],
                                  hsl(64, 128, 2 * j + 1),
                                  WAR[64:128, fh * 4096 + j * 512:fh * 4096 + (j + 1) * 512],
                                  start=(j == 0), stop=(j == 7))
                    tpe = PE.mark()
                    S.wait(tpe)
                    S.copy(SM2[:, 0:512], pb_p2[:, 0:512])
                    ts2 = S.tok()
                    V.wait(ts2)
                    V.wait(tox)
                    V.tensor_tensor(SM2[:, 0:512], SM2[:, 0:512], pb_p1[:, 0:512], OP.add)
                    V.tensor_scalar(SM2[:, 512:1024], SM2[:, 0:512],
                                    stc(c_xsa + c), WSC[:, 2:3], OP.mult, OP.mult)
                    V.tensor_tensor(SM2[:, 512:1024], SM2[:, 512:1024],
                                    BIASF[:, fh * 512:(fh + 1) * 512], OP.add)
                    V.tensor_tensor(XNEW[:, fh * 512:(fh + 1) * 512], SM2[:, 512:1024],
                                    XOLD[:, fh * 512:(fh + 1) * 512], OP.add)
                tnew = V.tok()
                em.wait(SP, tnew)
                tst = em.inc(SP.dma_start(out=x_st[t0:t0 + 128, :], in_=XNEW[:, 0:D]),
                             "dxs", 16)
                rmsnorm_quant_chunk(XNEW[:, 0:D], NWB[:, D:2 * D], c, XQN[:, 0:D])
                V.wait(tst)
                transpose_chunk(XQN[:, 0:D], pb_kn,
                                lambda f, c=c: XQ2T[:, f * AB + c * 128:f * AB + (c + 1) * 128])

    def phase_C(l, blk):
        base = blk * AB
        with pst("pc_mm", [128, 512], F32) as pc_mm, \
             pst("pc_mm2", [128, 512], F32) as pc_mm2, \
             pst("pc_mm3", [128, 512], F32) as pc_mm3, \
             pst("pc_mm4", [128, 512], F32) as pc_mm4, \
             pst("pc_bc", [128, 512], F32) as pc_bc:
            fence()
            em.wait(SP, em.now("dxs"))
            trow = cols_to_row(stc(c_x127, NPT), NPT, ROWS[0:1, 0:AB])
            em.wait(PE, trow)
            bcast_row(ROWS, AB, XSB, pc_bc)
            txsb = em.now("s")
            GLUv = BIGB
            sc = {0: em.now("s"), 1: em.now("s")}
            vc = {0: em.now("v"), 1: em.now("v")}
            pvc = {0: em.now("v")}
            with relaxed():
                for pan in range(16):
                    woff = (pan % 2) * 4096
                    em.wait(SP, em.now("pe"))
                    tld = stream_panel(WAR[:, woff:woff + 4096], sw1_st, DT, pan * 512, 512,
                                       sem="dwA" if pan % 2 == 0 else "dwB")
                    em.wait(PE, tld)
                    for pr in range(2):
                        ja = pan * 4 + pr * 2
                        jg = ja + 1
                        hidx = pan * 2 + pr
                        for t2 in range(2):
                            bslot = (pr * 2 + t2) % 2
                            dstA = pc_mm if bslot == 0 else pc_mm3
                            dstB = pc_mm2 if bslot == 0 else pc_mm4
                            em.wait(PE, pvc.get(bslot))
                            for kt in range(DT):
                                PE.matmul(dstA[:, 0:512],
                                          WAR[:, woff + kt * 512 + (pr * 2) * 128:woff + kt * 512 + (pr * 2 + 1) * 128],
                                          XQ2T[:, kt * AB + t2 * 512:kt * AB + (t2 + 1) * 512],
                                          start=(kt == 0), stop=(kt == DT - 1))
                            for kt in range(DT):
                                PE.matmul(dstB[:, 0:512],
                                          WAR[:, woff + kt * 512 + (pr * 2 + 1) * 128:woff + kt * 512 + (pr * 2 + 2) * 128],
                                          XQ2T[:, kt * AB + t2 * 512:kt * AB + (t2 + 1) * 512],
                                          start=(kt == 0), stop=(kt == DT - 1))
                            tpe = PE.mark()
                            V.wait(tpe)
                            V.wait(txsb)
                            V.wait(sc.get(bslot))
                            V.scalar_tensor_tensor(SM[:, bslot * 1024:bslot * 1024 + 512],
                                                   dstA[:, 0:512], WSC[:, 3:4],
                                                   XSB[:, t2 * 512:(t2 + 1) * 512],
                                                   OP.mult, OP.mult)
                            V.scalar_tensor_tensor(SM[:, bslot * 1024 + 512:bslot * 1024 + 1024],
                                                   dstB[:, 0:512], WSC[:, 3:4],
                                                   XSB[:, t2 * 512:(t2 + 1) * 512],
                                                   OP.mult, OP.mult)
                            tv = V.tok()
                            pvc[bslot] = tv
                            S.wait(tv)
                            S.wait(vc.get(bslot))
                            S.activation(SM2[:, bslot * 1024:bslot * 1024 + 512],
                                         SM[:, bslot * 1024:bslot * 1024 + 512], AF.Identity,
                                         bias=BCOL[:, 32 + ja:33 + ja])
                            S.activation(SM2[:, bslot * 1024 + 512:bslot * 1024 + 1024],
                                         SM[:, bslot * 1024 + 512:bslot * 1024 + 1024], AF.Silu,
                                         bias=BCOL[:, 32 + jg:33 + jg])
                            ts = S.tok()
                            sc[bslot] = ts
                            V.wait(ts)
                            V.tensor_tensor(GLUv[:, hidx * AB + t2 * 512:hidx * AB + (t2 + 1) * 512],
                                            SM2[:, bslot * 1024:bslot * 1024 + 512],
                                            SM2[:, bslot * 1024 + 512:bslot * 1024 + 1024], OP.mult)
                            vc[bslot] = V.tok()
            with pst("pc_q32", [128, 128], F32) as pc_q32:
                V.reduce_max(SM[:, 0:AB],
                             GLUv[:, 0:32 * AB].rearrange("p (j t) -> p t j", j=32),
                             axis=AX, apply_absolute_value=True)
                tvt = V.tok()
                em.wait(PE, tvt)
                for ci in range(NPT):
                    PE.transpose(pc_q32[:, 0:128],
                                 SM[:, ci * 128:(ci + 1) * 128], IDENTF[:])
                    tpq = PE.mark()
                    V.wait(tpq)
                    V.reduce_max(stc(c_xs + ci), pc_q32[:, 0:128], axis=AX)
                    tvq = V.tok()
                    em.wait(PE, tvq)
                V.tensor_scalar(stc(c_xs, NPT), stc(c_xs, NPT), 1e-5, None, OP.max)
                V.reciprocal(stc(c_t0, NPT), stc(c_xs, NPT))
                V.tensor_scalar(stc(c_qsc, NPT), stc(c_t0, NPT), 127.0, None, OP.mult)
                V.tensor_scalar(stc(c_x127, NPT), stc(c_xs, NPT), 1.0 / 127.0, None, OP.mult)
            trow2 = cols_to_row(stc(c_qsc, NPT), NPT, ROWS[0:1, 0:AB])
            em.wait(PE, trow2)
            bcast_row(ROWS, AB, XSB, pc_bc)
            tq = em.now("s")
            for j in range(32):
                if j == 0:
                    V.wait(tq)
                V.tensor_tensor(SM[:, 0:D], GLUv[:, j * AB:(j + 1) * AB], XSB[:], OP.mult)
                tvq = V.tok()
                S.wait(tvq)
                S.activation(XI16[:, 0:D], SM[:, 0:D], AF.Copy)
                ts = S.tok()
                V.wait(ts)
                V.tensor_copy(GLUv[:, j * AB:(j + 1) * AB], XI16[:, 0:D])
            for fhv in range(2):
                tpe_guard = em.now("pe")
                em.wait(SP, tpe_guard)
                em.wait(SP, em.now("dxs"))
                tlda = stream_panel(WAR[:, 0:8192], sw2_st, HID // 128, fhv * 512, 256,
                                    kw=256, sem="dwA")
                tldb = stream_panel(WAR[:, 8192:16384], sw2_st, HID // 128,
                                    fhv * 512 + 256, 256, kw=256, sem="dwB")
                em.wait(PE, tlda)
                em.wait(PE, tldb)
                for c in range(NPT):
                    t0 = base + c * 128
                    tv0 = em.now("v")
                    em.wait(SP, tv0)
                    tox = em.inc(SP.dma_start(out=XOLD[:, 0:D],
                                              in_=x_st[t0:t0 + 128, :]), "dxo", 16)
                    for q2 in range(2):
                        woff = q2 * 8192
                        fcol = fhv * 512 + q2 * 256
                        dst = pc_mm if q2 == 0 else pc_mm2
                        tev = em.now("v")
                        em.wait(PE, tev)
                        for kt in range(HID // 128):
                            PE.matmul(dst[:, 0:256],
                                      GLUv[:, kt * AB + c * 128:kt * AB + (c + 1) * 128],
                                      WAR[:, woff + kt * 256:woff + (kt + 1) * 256],
                                      start=(kt == 0), stop=(kt == HID // 128 - 1))
                        tpe = PE.mark()
                        V.wait(tpe)
                        V.wait(tox)
                        V.tensor_scalar(SM[:, 0:256], dst[:, 0:256],
                                        stc(c_x127 + c), WSC[:, 4:5], OP.mult, OP.mult)
                        V.tensor_tensor(SM[:, 0:256], SM[:, 0:256],
                                        BIASF[:, D + fcol:D + fcol + 256], OP.add)
                        V.tensor_tensor(XNEW[:, fcol:fcol + 256], SM[:, 0:256],
                                        XOLD[:, fcol:fcol + 256], OP.add)
                    tnew = V.tok()
                    em.wait(SP, tnew)
                    tst = em.inc(SP.dma_start(
                        out=x_st[t0:t0 + 128, fhv * 512:(fhv + 1) * 512],
                        in_=XNEW[:, fhv * 512:(fhv + 1) * 512]), "dxs", 16)
                    V.wait(tst)

    def ffn_consts(e, w1b, w2b, w3b):
        with pst("fb_c", [128, 32], F32) as fb_c, \
             pst("fb_bc", [128, 512], F32) as fb_bc:
            r1 = lambda ap: ap.rearrange("(a b) -> a b", a=1)

            def ld_row2(src, n):
                tp = em.now("pe")
                em.wait(SP, tp)
                for nm in ("dob", "drw"):
                    if em.count.get(nm, 0):
                        em.wait(SP, em.now(nm))
                tl = em.inc(SP.dma_start(out=ROWS[0:1, 0:n], in_=src), "drw", 16)
                em.wait(PE, tl)

            ld_row2(r1(w1b), FH)
            col_from_rows(ROWS, 0, FHT, BCF[:, 0:FHT], fb_c)
            ld_row2(r1(w2b), FH)
            col_from_rows(ROWS, 0, FHT, BCF[:, 16:16 + FHT], fb_c)
            ld_row2(r1(w3b), D)
            bcast_row(ROWS, D, BIASF[:, 0:D], fb_bc)

    def phase_FFN():
        XNT = BIGB[:, 0:DT * OWN]
        GLUH = BIGB[:, DT * OWN:DT * OWN + FHT * AB]
        with pst("ff_tr", [128, 1024], BF16) as ff_tr, \
             pst("ff_mm", [128, 512], F32) as ff_mm, \
             pst("ff_mm2", [128, 512], F32) as ff_mm2, \
             pst("ff_mm3", [128, 512], F32) as ff_mm3, \
             pst("ff_mm4", [128, 512], F32) as ff_mm4, \
             pst("ff_bc", [128, 512], F32) as ff_bc:
            fence()
            gfence()
            em.wait(SP, em.now("dxn"))
            V.memset(XNEW[:, 0:D], 0.0)
            tz = V.tok()
            em.wait(SP, tz)
            for gq in range(NOC):
                em.inc(SP.dma_start(out=acc_st[gq * 128:(gq + 1) * 128, :],
                                    in_=XNEW[:, 0:D]), "dacc", 16)
            em.wait(SP, em.now("dacc"))
            for gq in range(NOC):
                tprev_s = em.now("s")
                em.wait(nc.gpsimd, tprev_s)
                tg = em.inc(nc.gpsimd.indirect_dma_start(
                    XQN[:, 0:D], None, xn_st,
                    bass.IndirectOffsetOnAxis(IXC[:, gq:gq + 1], 0)), "dgx", 16)
                em.wait(PE, tg)
                transpose_chunk(XQN[:, 0:D], ff_tr,
                                lambda f, gq=gq: XNT[:, f * OWN + gq * 128:f * OWN + (gq + 1) * 128])
            gfence()
            tgw = None
            for kt in range(DT):
                tgw = em.inc(nc.gpsimd.dma_start(out=WAR[:, kt * FE:(kt + 1) * FE],
                                                 in_=fgw[kt * 128:(kt + 1) * 128, :]),
                             "dgw", 16)
            tp0 = em.now("pe")
            em.wait(SP, tp0)
            for nm in ("dob", "drw"):
                if em.count.get(nm, 0):
                    em.wait(SP, em.now(nm))
            tgb = em.inc(SP.dma_start(out=ROWS[0:1, 0:FE], in_=fgb), "drw", 16)
            em.wait(PE, tgw)
            em.wait(PE, tgb)
            for gq in range(NOC):
                tpv = em.now("v")
                em.wait(PE, tpv)
                for kt in range(DT):
                    PE.matmul(ff_mm[:, 0:FE],
                              XNT[:, kt * OWN + gq * 128:kt * OWN + (gq + 1) * 128],
                              WAR[:, kt * FE:(kt + 1) * FE],
                              start=(kt == 0), stop=False)
                PE.matmul(ff_mm[:, 0:FE], ONECF[0:1, 0:1].broadcast_to([1, 128]),
                          ROWS[0:1, 0:FE], start=False, stop=True)
                tpe = PE.mark()
                V.wait(tpe)
                V.reduce_max(stc(c_t0), ff_mm[:, 0:FE], axis=AX)
                V.tensor_scalar(stc(c_t1), stc(c_t0), -1.0, None, OP.mult)
                tv = V.tok()
                S.wait(tv)
                S.activation(SM[:, 0:FE], ff_mm[:, 0:FE], AF.Exp,
                             bias=stc(c_t1), accum_out=stc(c_den))
                ts = S.tok()
                V.wait(ts)
                V.reciprocal(stc(c_rden), stc(c_den))
                V.tensor_scalar(SM[:, 8:16], SM[:, 0:FE], stc(c_rden), None, OP.mult)
                V.reduce_max(stc(c_t0), SM[:, 8:16], axis=AX)
                V.tensor_scalar(SM[:, 16:24], SM[:, 8:16], stc(c_t0), None, OP.is_ge)
                V.tensor_scalar(SM[:, 24:32], SM[:, 16:24], -3.0, None, OP.mult)
                V.tensor_tensor(SM[:, 24:32], SM[:, 8:16], SM[:, 24:32], OP.add)
                V.reduce_max(stc(c_t1), SM[:, 24:32], axis=AX)
                V.tensor_scalar(SM[:, 16:24], SM[:, 8:16], stc(c_t1), None, OP.is_ge)
                V.tensor_tensor(WEB[:, gq * 8:(gq + 1) * 8], SM[:, 8:16],
                                SM[:, 16:24], OP.mult)
            for e in range(FE + FS):
                routed = e < FE
                w1 = frw1[e] if routed else fsw1[e - FE]
                w2 = frw2[e] if routed else fsw2[e - FE]
                w3 = frw3[e] if routed else fsw3[e - FE]
                b1 = frb1[e] if routed else fsb1[e - FE]
                b2 = frb2[e] if routed else fsb2[e - FE]
                b3 = frb3[e] if routed else fsb3[e - FE]
                em.wait(SP, em.now("dacc"))
                ffn_consts(e, b1, b2, b3)
                for th in range(max(OWN // AB, 1)):
                    tw0 = th * AB
                    twn = min(AB, OWN)
                    sf = {0: em.now("v"), 1: em.now("v")}
                    pvf = {0: em.now("v"), 1: em.now("v")}
                    bidx = [0]
                    with relaxed():
                        for pan in range(FH // 512):
                            woff = (pan % 2) * 4096
                            em.wait(nc.gpsimd, em.now("pe"))
                            tld = None
                            for kt in range(DT):
                                tld = em.inc(nc.gpsimd.dma_start(
                                    out=WAR[:, woff + kt * 512:woff + (kt + 1) * 512],
                                    in_=w1[kt * 128:(kt + 1) * 128, pan * 512:(pan + 1) * 512]),
                                    "dgw", 16)
                            for kt in range(DT):
                                tld = em.inc(nc.gpsimd.dma_start(
                                    out=WAR[:, 8192 + woff + kt * 512:8192 + woff + (kt + 1) * 512],
                                    in_=w2[kt * 128:(kt + 1) * 128, pan * 512:(pan + 1) * 512]),
                                    "dgw", 16)
                            em.wait(PE, tld)
                            for fb in range(4):
                                hglob = pan * 4 + fb
                                for t2 in range(twn // 512):
                                    tcol = tw0 + t2 * 512
                                    slot = bidx[0] % 2
                                    bidx[0] += 1
                                    dstA = ff_mm if slot == 0 else ff_mm3
                                    dstB = ff_mm2 if slot == 0 else ff_mm4
                                    em.wait(PE, pvf.get(slot))
                                    for kt in range(DT):
                                        PE.matmul(dstA[:, 0:512],
                                                  WAR[:, woff + kt * 512 + fb * 128:woff + kt * 512 + (fb + 1) * 128],
                                                  XNT[:, kt * OWN + tcol:kt * OWN + tcol + 512],
                                                  start=(kt == 0), stop=(kt == DT - 1))
                                    for kt in range(DT):
                                        PE.matmul(dstB[:, 0:512],
                                                  WAR[:, 8192 + woff + kt * 512 + fb * 128:8192 + woff + kt * 512 + (fb + 1) * 128],
                                                  XNT[:, kt * OWN + tcol:kt * OWN + tcol + 512],
                                                  start=(kt == 0), stop=(kt == DT - 1))
                                    tpe = PE.mark()
                                    S.wait(tpe)
                                    S.wait(sf.get(slot))
                                    S.activation(XQN[:, slot * 512:(slot + 1) * 512],
                                                 dstA[:, 0:512], AF.Silu,
                                                 bias=BCF[:, hglob:hglob + 1])
                                    ts = S.tok()
                                    V.wait(ts)
                                    V.scalar_tensor_tensor(
                                        GLUH[:, hglob * AB + t2 * 512:hglob * AB + (t2 + 1) * 512],
                                        dstB[:, 0:512], BCF[:, 16 + hglob:17 + hglob],
                                        XQN[:, slot * 512:(slot + 1) * 512], OP.add, OP.mult)
                                    tvf = V.tok()
                                    sf[slot] = tvf
                                    pvf[slot] = tvf
                    em.wait(nc.gpsimd, em.now("pe"))
                    tld = None
                    for fh2 in range(2):
                        for kt in range(FHT):
                            tld = em.inc(nc.gpsimd.dma_start(
                                out=WAR[:, fh2 * 8192 + kt * 512:fh2 * 8192 + (kt + 1) * 512],
                                in_=w3[kt * 128:(kt + 1) * 128, fh2 * 512:(fh2 + 1) * 512]),
                                "dgw", 16)
                    em.wait(PE, tld)
                    for cc in range(twn // 128):
                        gq = (tw0 + cc * 128) // 128
                        tv0 = em.now("v")
                        em.wait(SP, tv0)
                        tac = em.inc(SP.dma_start(out=XOLD[:, 0:D],
                                                  in_=acc_st[gq * 128:(gq + 1) * 128, :]),
                                     "dxo", 16)
                        for fh2 in range(2):
                            dst = ff_mm if fh2 == 0 else ff_mm2
                            tev = em.now("v")
                            em.wait(PE, tev)
                            for kt in range(FHT):
                                PE.matmul(dst[:, 0:512],
                                          GLUH[:, kt * AB + cc * 128:kt * AB + (cc + 1) * 128],
                                          WAR[:, fh2 * 8192 + kt * 512:fh2 * 8192 + (kt + 1) * 512],
                                          start=(kt == 0), stop=(kt == FHT - 1))
                            tpe = PE.mark()
                            V.wait(tpe)
                            V.wait(tac)
                            V.tensor_tensor(SM[:, 0:512], dst[:, 0:512],
                                            BIASF[:, fh2 * 512:(fh2 + 1) * 512], OP.add)
                            if routed:
                                V.tensor_scalar(SM[:, 512:1024], SM[:, 0:512],
                                                WEB[:, gq * 8 + e:gq * 8 + e + 1],
                                                None, OP.mult)
                            else:
                                V.tensor_scalar(SM[:, 512:1024], SM[:, 0:512],
                                                0.5, None, OP.mult)
                            V.tensor_tensor(XNEW[:, fh2 * 512:(fh2 + 1) * 512],
                                            SM[:, 512:1024],
                                            XOLD[:, fh2 * 512:(fh2 + 1) * 512], OP.add)
                        tnew = V.tok()
                        em.wait(SP, tnew)
                        tst = em.inc(SP.dma_start(out=acc_st[gq * 128:(gq + 1) * 128, :],
                                                  in_=XNEW[:, 0:D]), "dacc", 16)
                        V.wait(tst)

    pbstate = {}
    for l in range(LAYERS):
        if not (set("ABC") & set(phases)):
            break
        prep_weights(l)
        layer_consts(l)
        V.memset(KV[:], 0.0)
        for blk in range(NBLK):
            if "A" in phases:
                phase_A(l, blk)
            if "B" in phases:
                phase_B(l, blk, pbstate)
            if "C" in phases:
                phase_C(l, blk)

    if "F" in phases:
        phase_FFN()

    fence()
    gfence()
    em.wait(SP, em.now("dacc"))
    em.wait(SP, em.now("dxs"))
    for gq in range(NOC):
        tvp = em.now("v")
        em.wait(nc.gpsimd, tvp)
        em.wait(SP, tvp)
        tg1 = em.inc(nc.gpsimd.indirect_dma_start(
            SM[:, 0:D], None, x_st,
            bass.IndirectOffsetOnAxis(IXC[:, gq:gq + 1], 0)), "dgx", 16)
        tg2 = em.inc(nc.gpsimd.indirect_dma_start(
            SM2[:, 0:D], None, x0,
            bass.IndirectOffsetOnAxis(IXC[:, gq:gq + 1], 0)), "dgx", 16)
        ta = em.inc(SP.dma_start(out=XOLD[:, 0:D],
                                 in_=acc_st[gq * 128:(gq + 1) * 128, :]), "dxo", 16)
        V.wait(tg2)
        V.wait(ta)
        V.tensor_tensor(XNEW[:, 0:D], SM[:, 0:D], SM2[:, 0:D], OP.add)
        V.tensor_tensor(XNEW[:, 0:D], XNEW[:, 0:D], XOLD[:, 0:D], OP.add)
        tv = V.tok()
        em.wait(SP, tv)
        tw = em.inc(SP.dma_start(out=out[gq * 128:(gq + 1) * 128, :], in_=XNEW[:, 0:D]),
                    "dout", 16)
        V.wait(tw)
    em.wait(nc.gpsimd, em.now("dout"))

    ctx.close()
    return nc



def _rope_tables(n):
    inv = (1.0 / np.power(np.float32(10000.0),
                          (np.arange(0, HD, 2, dtype=np.float32) / np.float32(HD)).astype(np.float32))
           ).astype(np.float32)
    f = (np.arange(n, dtype=np.float32)[:, None] * inv[None, :]).astype(np.float32)
    cos = np.cos(f).astype(np.float32)
    sin = np.sin(f).astype(np.float32)
    rows = (np.arange(128) % HD) // 2
    crep = cos.T[rows, :]
    srep = sin.T[rows, :]
    return np.ascontiguousarray(crep), np.ascontiguousarray(srep)


def _pmat():
    P = np.zeros((128, 128), np.float32)
    for i in range(64):
        P[2 * i, 2 * i + 1] = -1.0
        P[2 * i + 1, 2 * i] = 1.0
    return np.ascontiguousarray(P.T).astype(ml_dtypes.bfloat16)


def _interleave_fc1_cols(wT):
    a = wT[:, 0:HID].reshape(wT.shape[0], HID // 128, 128)
    g = wT[:, HID:2 * HID].reshape(wT.shape[0], HID // 128, 128)
    inter = np.stack([a, g], axis=2)
    return np.ascontiguousarray(inter.reshape(wT.shape[0], 2 * HID))


def _interleave_fc1_bias(b):
    a = b[0:HID].reshape(HID // 128, 128)
    g = b[HID:2 * HID].reshape(HID // 128, 128)
    return np.ascontiguousarray(np.stack([a, g], axis=1).reshape(2 * HID))


def stage_inputs(inputs, NTOK, LAYERS, OWN, n_cores=8):
    f32 = np.float32
    L = LAYERS
    qkv_w = np.asarray(inputs["L0_attn_qkv_w"], f32)[:L]
    gate_w = np.asarray(inputs["L0_attn_gate_w"], f32)[:L]
    proj_w = np.asarray(inputs["L0_attn_proj_w"], f32)[:L]
    fc1_w = np.asarray(inputs["L0_attn_fc1_w"], f32)[:L]
    fc2_w = np.asarray(inputs["L0_attn_fc2_w"], f32)[:L]
    com = {
        "wqkvT": np.ascontiguousarray(qkv_w.transpose(0, 2, 1)),
        "wgateT": np.ascontiguousarray(gate_w.transpose(0, 2, 1)),
        "wprojT": np.ascontiguousarray(proj_w.transpose(0, 2, 1)),
        "wfc1T": np.ascontiguousarray(
            np.stack([_interleave_fc1_cols(np.ascontiguousarray(w.T)) for w in fc1_w])),
        "wfc2T": np.ascontiguousarray(fc2_w.transpose(0, 2, 1)),
        "bqkv": np.asarray(inputs["L0_attn_qkv_b"], f32)[:L],
        "bgate": np.asarray(inputs["L0_attn_gate_b"], f32)[:L],
        "bproj": np.asarray(inputs["L0_attn_proj_b"], f32)[:L],
        "bfc1": np.stack([_interleave_fc1_bias(np.asarray(b, f32))
                          for b in np.asarray(inputs["L0_attn_fc1_b"], f32)[:L]]),
        "bfc2": np.asarray(inputs["L0_attn_fc2_b"], f32)[:L],
        "norm1w": np.asarray(inputs["L0_attn_norm1_w"], f32)[:L],
        "norm2w": np.asarray(inputs["L0_attn_norm2_w"], f32)[:L],
        "anormwT": np.ascontiguousarray(
            np.asarray(inputs["L0_attn_attn_norm_w"], f32)[:L]
            .reshape(L, HEADS, HD).transpose(0, 2, 1)),
        "identb": np.eye(128, dtype=ml_dtypes.bfloat16),
        "identf": np.eye(128, dtype=np.float32),
        "pmatb": _pmat(),
        "fgw": np.asarray(inputs["L0_ffn_gate_w"], f32),
        "fgb": np.asarray(inputs["L0_ffn_gate_b"], f32).reshape(1, FE),
        "frw1": np.asarray(inputs["L0_ffn_rw1"], f32),
        "frb1": np.asarray(inputs["L0_ffn_rb1"], f32),
        "frw2": np.asarray(inputs["L0_ffn_rw2"], f32),
        "frb2": np.asarray(inputs["L0_ffn_rb2"], f32),
        "frw3": np.asarray(inputs["L0_ffn_rw3"], f32),
        "frb3": np.asarray(inputs["L0_ffn_rb3"], f32),
        "fsw1": np.asarray(inputs["L0_ffn_sw1"], f32),
        "fsb1": np.asarray(inputs["L0_ffn_sb1"], f32),
        "fsw2": np.asarray(inputs["L0_ffn_sw2"], f32),
        "fsb2": np.asarray(inputs["L0_ffn_sb2"], f32),
        "fsw3": np.asarray(inputs["L0_ffn_sw3"], f32),
        "fsb3": np.asarray(inputs["L0_ffn_sb3"], f32),
    }
    crep_, srep_ = _rope_tables(NTOK)
    com["crep"] = crep_
    com["srep"] = srep_
    x = np.asarray(inputs["L0_attn_x"], f32)
    in_maps = []
    for c in range(n_cores):
        b = c // 4
        q = c % 4
        idx = (np.arange(128, dtype=np.int32)[:, None]
               + (q * OWN + np.arange(OWN // 128, dtype=np.int32)[None, :] * 128))
        m = dict(com)
        m["x0"] = np.ascontiguousarray(x[b, 0:NTOK, :])
        m["ownidx"] = np.ascontiguousarray(idx)
        in_maps.append(m)
    return in_maps


def kernel(**inputs):
    x = np.asarray(inputs["L0_attn_x"])
    B, NTOK, _ = x.shape
    LAYERS = np.asarray(inputs["L0_attn_norm1_w"]).shape[0]
    OWN = NTOK // 4
    in_maps = stage_inputs(inputs, NTOK, LAYERS, OWN, n_cores=8)
    nc = build_program(NTOK, LAYERS, OWN)
    res = run_bass_kernel_spmd(nc, in_maps, core_ids=list(range(8)))
    outp = np.empty((B, NTOK, D), np.float32)
    for c in range(8):
        b = c // 4
        q = c % 4
        outp[b, q * OWN:(q + 1) * OWN, :] = res.results[c]["out"]
    return outp
